# version 10
# baseline (speedup 1.0000x reference)
import math
import numpy as np
import ml_dtypes
import concourse.bass as bass
import concourse.mybir as mybir
from concourse.bass_utils import run_bass_kernel_spmd

F32 = mybir.dt.float32
BF16 = mybir.dt.bfloat16
I32 = mybir.dt.int32
AF = mybir.ActivationFunctionType
ALU = mybir.AluOpType
AX = mybir.AxisListType

BF = ml_dtypes.bfloat16

B, T, D = 4, 2048, 4096
H, KVH, HD = 32, 8, 128
DT = D // 128
NKT = 16
SCALE = 1.0 / math.sqrt(HD)
NEGC = -10.0
MASKV = -30000.0
EPS = 1e-5
E, R = 8, 16
LORA_SCALING = 2.0
STOP_PHASE = 99
KV_KINDS = "kv"
KV_COUNT = 99


def _rope_tables():
    base = 500000.0
    inv_freq = 1.0 / base ** (np.arange(0, HD, 2, dtype=np.float32) / HD)
    factor, low_f, high_f, old_ctx = 32.0, 1.0, 4.0, 8192
    low_wl, high_wl = old_ctx / low_f, old_ctx / high_f
    wavelen = 2 * math.pi / inv_freq
    inv_llama = np.where(wavelen > low_wl, inv_freq / factor, inv_freq)
    smooth = (old_ctx / wavelen - low_f) / (high_f - low_f)
    smoothed = (1 - smooth) * inv_llama / factor + smooth * inv_llama
    is_med = ~(wavelen < high_wl) & ~(wavelen > low_wl)
    inv = np.where(is_med, smoothed, inv_llama).astype(np.float32)
    pos = np.arange(T, dtype=np.float32)
    freqs = pos[:, None] * inv[None, :]
    emb = np.concatenate([freqs, freqs], axis=-1)
    return np.cos(emb).astype(np.float32), np.sin(emb).astype(np.float32)


def _core_perm(s):
    p = np.arange(T)
    if s == 0:
        return np.concatenate([p[512:1024], p[0:512], p[1536:2048], p[1024:1536]])
    return np.concatenate([p[0:512], p[512:1536], p[1536:2048]])


ENGS = ("pe", "act", "dve", "sp", "pool")


class Ctx:

    DMA_FAMS = ("cst", "cs", "xb", "xq", "orhs", "sp0", "sp1", "outq",
                "w0", "w1", "w2", "c2")

    def __init__(self, nc):
        self.nc = nc
        self.q = {e: [] for e in ENGS}
        self.sem_names = ("pe", "act", "dve") + self.DMA_FAMS
        self.sems = {}
        self.cnt = {n: 0 for n in self.sem_names}
        self.hw = {e: {} for e in ENGS}
        self.buf = {}

    def make_sems(self, enter):
        for n in self.sem_names:
            self.sems[n] = enter(self.nc.semaphore(n))

    LOG = []

    def _need(self, e, sem, val):
        Ctx.LOG.append((e, "wait", sem, val, self.cnt.get(sem)))
        if sem in self.DMA_FAMS:
            val = self.cnt[sem]
        if self.hw[e].get(sem, 0) >= val:
            return
        self.hw[e][sem] = val
        s = self.sems[sem]
        self.q[e].append(lambda eng, s=s, val=val: eng.wait_ge(s, val))

    def _entry(self, k):
        if k not in self.buf:
            self.buf[k] = {"w": None, "r": []}
        return self.buf[k]

    def _waits(self, e, reads, writes, war):
        for k in reads:
            ent = self._entry(k)
            if ent["w"]:
                self._need(e, *ent["w"])
            if isinstance(k[0], str) and k[0].startswith("pb"):
                for r in ent["r"]:
                    if r[0] != e:
                        self._need(e, *r)
        for k in tuple(writes) + tuple(war):
            ent = self._entry(k)
            if ent["w"]:
                self._need(e, *ent["w"])
            for r in ent["r"]:
                self._need(e, *r)

    def fence(self):
        Ctx.LOG.append(("ALL", "FENCE", None, None, None))
        """Full barrier: every engine waits everything issued so far. Emitted at
        SBUF free points, where later allocations reuse freed addresses that the
        logical-key tracker cannot see."""
        snap = dict(self.cnt)
        for e in ENGS:
            for sem, val in snap.items():
                if val > 0:
                    self._need(e, sem, val)

    def comp(self, e, fn, reads=(), writes=(), war=(), inc=True):
        Ctx.LOG.append((e, "comp", self.cnt[e] + 1, tuple(writes), tuple(reads)))
        self._waits(e, reads, writes, war)
        sem = e
        val = self.cnt[sem] + 1
        if inc:
            self.cnt[sem] = val
            s = self.sems[sem]
            self.q[e].append(lambda eng, fn=fn, s=s: fn(eng).then_inc(s, 1))
        else:
            self.q[e].append(lambda eng, fn=fn: fn(eng))
        for k in reads:
            self._entry(k)["r"].append((sem, val))
        for k in writes:
            ent = self._entry(k)
            ent["w"] = (sem, val)
            ent["r"] = []
        return (sem, val)

    def dma(self, queue, fn, reads=(), writes=(), fam="cst"):
        Ctx.LOG.append((queue, "dma", fam, tuple(writes), tuple(reads)))
        """Emit a DMA on queue 'sp'|'pool' using semaphore family `fam`.

        DMA completions on one queue interleave across in-flight transfers, so
        cumulative counts are only sound per family where buffer reuse is
        WAR-gated (each family = one rotating buffer, or a write-once batch).
        """
        e = queue
        self._waits(e, reads, writes, ())
        sem = fam
        val = self.cnt[sem] + 16
        self.cnt[sem] = val
        s = self.sems[sem]
        self.q[e].append(lambda eng, fn=fn, s=s: fn(eng).then_inc(s, 16))
        for k in reads:
            self._entry(k)["r"].append((sem, val))
        for k in writes:
            ent = self._entry(k)
            ent["w"] = (sem, val)
            ent["r"] = []
        return (sem, val)



def build_program():
    nc = bass.Bass("TRN2", target_bir_lowering=False)

    def din(name, shape, dt=F32):
        return nc.dram_tensor(name, list(shape), dt, kind="ExternalInput").ap()

    xkv = din("xkv", (T, D))
    wq4 = din("wq4", (H, 128, DT, 128))
    wk4 = din("wk4", (KVH, 128, DT, 128))
    wv4 = din("wv4", (KVH, 128, DT, 128))
    wo4 = din("wo4", (DT, 128, DT, 128))
    wb4 = din("wb4", (DT, 128, DT, 128))
    wrT = din("wrT", (D, 16))
    atT = din("atT", (D, 128))
    btL = din("btL", (128, D))
    bbT = din("bbT", (128, DT))
    rb_row = din("rb_row", (1, 15))
    em_row = din("em_row", (1, 15), I32)
    coskT = din("coskT", (128, T))
    sinkT = din("sinkT", (128, T))
    tri_in = din("tri_in", (128, 4 * 512), BF16)
    biasv_in = din("biasv_in", (128, 32))
    ident_in = din("ident_in", (128, 128), BF16)
    ones128_in = din("ones128_in", (128, 128), BF16)
    ones1_in = din("ones1_in", (1, 128))
    ar8_in = din("ar8_in", (8, 1))
    expand_in = din("expand_in", (8, 128), BF16)
    ar15_in = din("ar15_in", (128, 64))

    out = nc.dram_tensor("out", [1024, D], F32, kind="ExternalOutput").ap()
    oall = nc.dram_tensor("oall", [2, 128, H, 512], BF16, kind="Internal").ap()

    live = {}
    done = []

    def enter(g):
        return g.__enter__()

    def sb(name, shape, dt):
        g = nc.sbuf_tensor(name, list(shape), dt)
        live[name] = g
        return g.__enter__()

    def ps(name, shape=(128, 512), dt=F32):
        g = nc.psum_tensor(name, list(shape), dt)
        live[name] = g
        return g.__enter__()

    def free(*names):
        for n in names:
            g = live.pop(n)
            g.__exit__(None, None, None)
            done.append(n)

    ctx = Ctx(nc)
    sem_guards = []
    ctx.make_sems(lambda g: (sem_guards.append(g), g.__enter__())[1])

    ident = sb("ident", (128, 128), BF16)
    stats = sb("stats", (128, 8), F32)
    scratch = sb("scratch", (128, D), BF16)
    wbuf = [sb(f"wbuf{i}", (128, DT * 128), BF16) for i in range(3)]
    m1 = sb("m1", (16, 128), BF16)
    rbb = sb("rbb", (128, 16), F32)
    bbs = sb("bbs", (128, DT), F32)
    epsb = sb("epsb", (128, 1), F32)

    ones128 = sb("ones128", (128, 128), BF16)
    tri = sb("tri", (128, 4 * 512), BF16)
    biasv = sb("biasv", (128, 32), F32)
    cosh = sb("cosh", (128, 1024), BF16)
    sinh = sb("sinh", (128, 1024), BF16)
    ktall = sb("ktall", (128, KVH * T), BF16)
    vall = sb("vall", (128, NKT * 1024), BF16)
    slotA = sb("slotA", (128, DT * 512), BF16)
    slotB = sb("slotB", (128, DT * 512), BF16)
    xbuf = sb("xbuf", (128, D), F32)
    ksb = sb("ksb", (128, 512), F32)
    t1b = sb("t1b", (128, 512), F32)
    vtb = sb("vtb", (128, 512), BF16)
    qh = [sb(f"qh{i}", (128, 512), BF16) for i in range(3)]
    ptile = [sb("ptile0", (128, 6 * 512), BF16)] * 2
    linv = [sb("linv0", (128, 512), F32)] * 2
    pacc = sb("pacc", (128, 4 * 512), BF16)
    r_misc = sb("r_misc", (16, 64), F32)
    ones1 = sb("ones1", (1, 128), F32)
    ar8 = sb("ar8", (8, 1), F32)
    expand = sb("expand", (8, 128), BF16)
    emi = sb("emi", (1, 15), I32)
    rb_sb = sb("rb_sb", (1, 15), F32)
    ohT_bf = sb("ohT_bf", (8, 15), BF16)

    pbA = ps("pbA", (128, 1024))
    pbB = ps("pbB", (128, 1024))
    pb47 = [ps(f"pbt{i}") for i in range(4, 8)]
    pb = [pbA[:, 0:512], pbA[:, 512:1024], pbB[:, 0:512], pbB[:, 512:1024],
          pb47[0][:], pb47[1][:], pb47[2][:], pb47[3][:]]
    pbS = [pbA, pbB]

    def mm(dst_ap, lhs_ap, rhs_ap, start, stop, reads, writes=(), war=(), inc=False):
        return ctx.comp(
            "pe",
            lambda t, d=dst_ap, l=lhs_ap, r=rhs_ap, s1=start, s2=stop: t.matmul(
                d, l, r, start=s1, stop=s2),
            reads=reads, writes=writes, war=war, inc=inc)

    ctx.dma("sp", lambda e: e.dma_start(out=ident[:], in_=ident_in), writes=[("ident",)])
    ctx.dma("sp", lambda e: e.dma_start(out=ones128[:], in_=ones128_in), writes=[("ones128",)])
    ctx.dma("sp", lambda e: e.dma_start(out=tri[:], in_=tri_in), writes=[("tri",)])
    ctx.dma("sp", lambda e: e.dma_start(out=biasv[:], in_=biasv_in), writes=[("biasv",)])
    ctx.dma("sp", lambda e: e.dma_start(out=bbs[:], in_=bbT), writes=[("bbs",)])
    ctx.dma("sp", lambda e: e.dma_start(out=ones1[:], in_=ones1_in), writes=[("ones1",)])
    ctx.dma("sp", lambda e: e.dma_start(out=ar8[:], in_=ar8_in), writes=[("ar8",)])
    ctx.dma("sp", lambda e: e.dma_start(out=expand[:], in_=expand_in), writes=[("expand",)])
    ctx.dma("sp", lambda e: e.dma_start(out=emi[:], in_=em_row), writes=[("emi",)])
    ctx.dma("sp", lambda e: e.dma_start(out=rb_sb[:], in_=rb_row), writes=[("rb_sb",)])

    def load_cos_half(hf):
        ctx.dma("pool", lambda e, hf=hf: e.dma_start(
            out=cosh[:], in_=coskT[:, hf * 1024:(hf + 1) * 1024]), writes=[("cosh",)], fam="cs")
        ctx.dma("pool", lambda e, hf=hf: e.dma_start(
            out=sinh[:], in_=sinkT[:, hf * 1024:(hf + 1) * 1024]), writes=[("sinh",)], fam="cs")

    load_cos_half(0)
    ctx.comp("dve", lambda v: v.memset(epsb[:], EPS), writes=[("epsb",)])

    mm(pb[2][:, 0:15], ones1[:, 0:128], rb_sb[:], True, True,
       reads=[("ones1",), ("rb_sb",)], war=[("pb2",)], writes=[("pb2",)], inc=True)
    ctx.comp("act", lambda s: s.copy(rbb[:, 0:15], pb[2][:, 0:15]),
             reads=[("pb2",)], writes=[("rbb",)])
    ctx.comp("dve", lambda v: v.tensor_copy(r_misc[0:1, 0:15], emi[:]),
             reads=[("emi",)], writes=[("emf",)])
    mm(pb[3][0:8, 0:15], ones1[:, 0:8], r_misc[0:1, 0:15], True, True,
       reads=[("emf",), ("ones1",)], war=[("pb3",)], writes=[("pb3",)], inc=True)
    ctx.comp("act", lambda s: s.copy(r_misc[0:8, 16:31], pb[3][0:8, 0:15]),
             reads=[("pb3",)], writes=[("emb8",)])
    ctx.comp("dve", lambda v: v.tensor_scalar(
        r_misc[0:8, 32:47], r_misc[0:8, 16:31], ar8[:, 0:1], None, op0=ALU.is_equal),
        reads=[("emb8",), ("ar8",)], writes=[("ohT",)])
    ctx.comp("dve", lambda v: v.tensor_copy(ohT_bf[:], r_misc[0:8, 32:47]),
             reads=[("ohT",)], writes=[("ohT_bf",)])
    mm(pb[4][0:15, 0:128], ohT_bf[:], expand[:], True, True,
       reads=[("ohT_bf",), ("expand",)], war=[("pb4",)], writes=[("pb4",)], inc=True)
    ctx.comp("dve", lambda v: v.memset(m1[:], 0.0), writes=[("m1",)])
    ctx.comp("act", lambda s: s.copy(m1[0:15, :], pb[4][0:15, 0:128]),
             reads=[("pb4",)], writes=[("m1",)])
    ctx.fence()
    free("ohT_bf", "rb_sb", "emi", "expand", "ar8", "ones1", "r_misc")

    def ln_stats(src_ap, src_keys):
        stc = lambda c: stats[:, c:c + 1]
        K = lambda c: ("st", c)
        ctx.comp("dve", lambda v: v.reduce_sum(stc(0), src_ap, axis=AX.X),
                 reads=list(src_keys), writes=[K(0)])
        ctx.comp("act", lambda s: s.activation(
            scratch[:], src_ap, AF.Square, accum_out=stc(1)),
            reads=list(src_keys), writes=[("scratch",), K(1)])
        ctx.comp("dve", lambda v: v.tensor_scalar_mul(stc(2), stc(0), 1.0 / D),
                 reads=[K(0)], writes=[K(2)])
        ctx.comp("dve", lambda v: v.tensor_tensor(stc(3), stc(2), stc(2), op=ALU.mult),
                 reads=[K(2)], writes=[K(3)])
        ctx.comp("dve", lambda v: v.tensor_scalar(
            stc(4), stc(1), 1.0 / D, None, op0=ALU.mult),
            reads=[K(1)], writes=[K(4)])
        ctx.comp("dve", lambda v: v.tensor_tensor(stc(4), stc(4), stc(3), op=ALU.subtract),
                 reads=[K(4), K(3)], writes=[K(4)])
        ctx.comp("act", lambda s: s.activation(stc(5), stc(4), AF.Ln, bias=epsb[:, 0:1]),
                 reads=[K(4), ("epsb",)], writes=[K(5)])
        ctx.comp("act", lambda s: s.activation(stc(6), stc(5), AF.Exp, scale=-0.5),
                 reads=[K(5)], writes=[K(6)])
        ctx.comp("dve", lambda v: v.tensor_tensor(stc(7), stc(2), stc(6), op=ALU.mult),
                 reads=[K(2), K(6)], writes=[K(7)])
        return stc(6), stc(7)

    def ln_normalize(dst_ap, src_ap, src_keys, dst_keys):
        rstd, nmr = stats[:, 6:7], stats[:, 7:8]
        ctx.comp("dve", lambda v: v.tensor_scalar(
            dst_ap, src_ap, rstd, nmr, op0=ALU.mult, op1=ALU.subtract),
            reads=list(src_keys) + [("st", 6), ("st", 7)], writes=list(dst_keys))

    def transpose_batch(src, src_key, n_groups, dest_fn, dest_key, evac_eng="act"):
        for g in range(n_groups):
            for j in range(4):
                idx = 4 * g + j
                last = (j == 3)
                mm(pb[6][:, j * 128:(j + 1) * 128],
                   src[:, idx * 128:(idx + 1) * 128], ident[:],
                   True, True,
                   reads=[src_key, ("ident",)],
                   war=[("pb6",)] if j == 0 else (),
                   writes=[("pb6",)] if last else (),
                   inc=last)
            if evac_eng == "act":
                ctx.comp("act", lambda e, g=g: e.copy(dest_fn(g), pb[6]),
                         reads=[("pb6",)], writes=[dest_key])
            else:
                ctx.comp("dve", lambda e, g=g: e.tensor_copy(dest_fn(g), pb[6]),
                         reads=[("pb6",)], writes=[dest_key])

    def ln_quarter(qi, slot, slot_name):
        for tl in range(4):
            t = qi * 4 + tl
            ctx.dma("sp", lambda e, t=t: e.dma_start(
                out=xbuf[:], in_=xkv[t * 128:(t + 1) * 128, :]), writes=[("xbuf",)], fam="xb")
            ln_stats(xbuf[:], [("xbuf",)])
            ln_normalize(scratch[:], xbuf[:], [("xbuf",)], [("scratch",)])
            transpose_batch(
                scratch, ("scratch",), 8,
                lambda g, tl=tl, slot=slot: slot[:].rearrange(
                    "p (a m) -> p a m", a=DT)[:, 4 * g:4 * g + 4, tl * 128:(tl + 1) * 128],
                (slot_name,))

    def kv_units(c, slot, slot_name):
        units = [("k", i) for i in range(KVH) if "k" in KV_KINDS] + \
                [("v", i) for i in range(KVH) if "v" in KV_KINDS]
        units = units[:KV_COUNT]

        def prefetch(u, par):
            kind, i = units[u]
            w = wk4 if kind == "k" else wv4
            ctx.dma("pool", lambda e, w=w, i=i, par=par: e.dma_start(
                out=wbuf[par][:].rearrange("p (a m) -> p a m", a=DT),
                in_=w[i]),
                writes=[("wbuf", par)], fam=f"w{par}")

        for uu in range(min(3, len(units))):
            prefetch(uu, uu % 3)
        co = c % 2
        for u, (kind, i) in enumerate(units):
            par = u % 3
            bank = 2 * (u % 2) + co
            bk = (f"pb{bank}",)
            for dt in range(DT):
                mm(pb[bank],
                   wbuf[par][:, dt * 128:(dt + 1) * 128],
                   slot[:, dt * 512:(dt + 1) * 512],
                   dt == 0, dt == DT - 1,
                   reads=[("wbuf", par), (slot_name,)],
                   war=[bk] if dt == 0 else (),
                   writes=[bk] if dt == DT - 1 else (),
                   inc=(dt == DT - 1))
            if kind == "k":
                ctx.comp("act", lambda s, b=bank: s.mul(ksb[0:64, :], pb[b][64:128, :], -1.0),
                         reads=[bk], writes=[("ksb",)], war=[("ksb2",)])
                ctx.comp("act", lambda s, b=bank: s.copy(ksb[64:128, :], pb[b][0:64, :]),
                         reads=[bk], writes=[("ksb2",)])
                ctx.comp("dve", lambda v, b=bank, co=co: v.tensor_tensor(
                    t1b[:], pb[b], cosh[:, co * 512:(co + 1) * 512], op=ALU.mult),
                    reads=[bk, ("cosh",)], writes=[("t1b",)])
                ctx.comp("dve", lambda v, co=co: v.tensor_tensor(
                    ksb[:], ksb[:], sinh[:, co * 512:(co + 1) * 512], op=ALU.mult),
                    reads=[("ksb",), ("ksb2",), ("sinh",)],
                    writes=[("ksb",), ("ksb2",)])
                dst = ktall[:, i * T + c * 512: i * T + (c + 1) * 512]
                ctx.comp("dve", lambda v, dst=dst: v.tensor_tensor(
                    dst, t1b[:], ksb[:], op=ALU.add),
                    reads=[("t1b",), ("ksb",), ("ksb2",)],
                    writes=[("ktall",)])
            else:
                ctx.comp("act", lambda s, b=bank: s.copy(vtb[:], pb[b]),
                         reads=[bk], writes=[("vtb",)])
                transpose_batch(
                    vtb, ("vtb",), 1,
                    lambda g, c=c, i=i: vall[:].rearrange(
                        "p (a m) -> p a m", a=NKT)[:, 4 * c:4 * c + 4, i * 128:(i + 1) * 128],
                    ("vall",))
            if u + 3 < len(units):
                prefetch(u + 3, par)

    def attention(ch, qslot, qslot_name):
        qht = qh + [vtb]
        qhk = lambda i: ("qh", i) if i < 3 else ("vtb",)
        dead = 2 if ch == 0 else 1
        nkt = 8 if ch == 0 else 16
        diag0 = 4 if ch == 0 else 8
        qcos0 = 512 if ch == 0 else 0
        pair_slot = [0] + [1] * (nkt // 2 - 1)
        folds_after = {kt: [(0, 1)] for kt in range(3, nkt, 2)}

        def wq_prefetch(h, par):
            ctx.dma("pool", lambda e, h=h, par=par: e.dma_start(
                out=wbuf[par][:].rearrange("p (a m) -> p a m", a=DT),
                in_=wq4[h]),
                writes=[("wbuf", par)], fam=f"w{par}")

        def q_proj(h, par, qbank):
            qk = (f"pb{qbank}",)
            for dt in range(DT):
                mm(pb[qbank],
                   wbuf[par][:, dt * 128:(dt + 1) * 128],
                   qslot[:, dt * 512:(dt + 1) * 512],
                   dt == 0, dt == DT - 1,
                   reads=[("wbuf", par), (qslot_name,)],
                   war=[qk] if dt == 0 else (),
                   writes=[qk] if dt == DT - 1 else (),
                   inc=(dt == DT - 1))

        def rope_q(qbank, s):
            qk = (f"pb{qbank}",)
            ctx.comp("act", lambda sc, b=qbank: sc.mul(ksb[0:64, :], pb[b][64:128, :], -1.0),
                     reads=[qk], writes=[("ksb",)], war=[("ksb2",)])
            ctx.comp("act", lambda sc, b=qbank: sc.copy(ksb[64:128, :], pb[b][0:64, :]),
                     reads=[qk], writes=[("ksb2",)])
            ctx.comp("dve", lambda v, b=qbank: v.tensor_tensor(
                t1b[:], pb[b], cosh[:, qcos0:qcos0 + 512], op=ALU.mult),
                reads=[qk, ("cosh",)], writes=[("t1b",)])
            ctx.comp("dve", lambda v: v.tensor_tensor(
                ksb[:], ksb[:], sinh[:, qcos0:qcos0 + 512], op=ALU.mult),
                reads=[("ksb",), ("ksb2",), ("sinh",)],
                writes=[("ksb",), ("ksb2",)])
            ctx.comp("dve", lambda v, s=s: v.tensor_tensor(
                qht[s][:], t1b[:], ksb[:], op=ALU.add),
                reads=[("t1b",), ("ksb",), ("ksb2",)],
                writes=[qhk(s)])

        npair = H // 2

        def emit_q_pair(p):
            if p >= npair:
                return
            q_proj(2 * p, (2 * p) % 3, 6)
            rope_q(6, (2 * p) % 4)
            q_proj(2 * p + 1, (2 * p + 1) % 3, 7)
            rope_q(7, (2 * p + 1) % 4)
            if p + 1 < npair:
                wq_prefetch(2 * p + 2, (2 * p + 2) % 3)
                wq_prefetch(2 * p + 3, (2 * p + 3) % 3)

        wq_prefetch(0, 0)
        wq_prefetch(1, 1)
        emit_q_pair(0)
        emit_q_pair(1)
        for hp in range(npair):
            h0, h1 = 2 * hp, 2 * hp + 1
            g = h0 // 4
            s0q, s1q = (2 * hp) % 4, (2 * hp + 1) % 4
            def av_step(kt):
                kslot = kt % 3
                pk = ("pt", kslot)
                for s in range(2):
                    mm(pb[4 + s],
                       vall[:, kt * 1024 + g * 128: kt * 1024 + g * 128 + 128],
                       ptile[0][:, kslot * 1024 + s * 512: kslot * 1024 + (s + 1) * 512],
                       kt == 0, kt == nkt - 1,
                       reads=[("vall",), pk],
                       war=[(f"pb{4 + s}",)] if kt == 0 else (),
                       writes=[(f"pb{4 + s}",)] if kt == nkt - 1 else (),
                       inc=(kt == nkt - 1))

            for kt in range(nkt):
                kslot = kt % 3
                sb2 = 2 * (kt % 2)
                pslot = ptile[0][:, kslot * 1024:(kslot + 1) * 1024]
                pk = ("pt", kslot)
                for s, sq in ((0, s0q), (1, s1q)):
                    mm(pb[sb2 + s],
                       ktall[:, g * T + kt * 128: g * T + kt * 128 + 128],
                       qht[sq][:],
                       True, True,
                       reads=[("ktall",), qhk(sq)],
                       war=[(f"pb{sb2 + s}",)], writes=[(f"pb{sb2 + s}",)], inc=True)
                pcol = biasv[:, ch * 16 + kt: ch * 16 + kt + 1]
                ctx.comp("act", lambda sc, pslot=pslot, pcol=pcol, kt=kt: sc.activation(
                    pslot, pbS[kt % 2][:], AF.Exp, bias=pcol, scale=SCALE),
                    reads=[(f"pb{sb2}",), (f"pb{sb2 + 1}",), ("biasv",)],
                    writes=[pk])
                if diag0 <= kt < diag0 + 4:
                    ktr = kt - diag0
                    for s in range(2):
                        ctx.comp("dve", lambda v, kslot=kslot, s=s, ktr=ktr: v.tensor_tensor(
                            ptile[0][:, kslot * 1024 + s * 512: kslot * 1024 + (s + 1) * 512],
                            ptile[0][:, kslot * 1024 + s * 512: kslot * 1024 + (s + 1) * 512],
                            tri[:, ktr * 512:(ktr + 1) * 512], op=ALU.mult),
                            reads=[pk, ("tri",)],
                            writes=[pk])
                if kt > 0:
                    av_step(kt - 1)
                if kt == nkt - 1:
                    av_step(kt)
                if kt % 2 == 1:
                    sidx = pair_slot[kt // 2]
                    k0, k1 = (kt - 1) % 3, kt % 3
                    ctx.comp("dve", lambda v, sidx=sidx, k0=k0, k1=k1: v.tensor_tensor(
                        pacc[:, sidx * 1024:(sidx + 1) * 1024],
                        ptile[0][:, k0 * 1024:(k0 + 1) * 1024],
                        ptile[0][:, k1 * 1024:(k1 + 1) * 1024], op=ALU.add),
                        reads=[("pt", k0), ("pt", k1)],
                        writes=[("pacc", sidx)])
                    for (a, b2) in folds_after.get(kt, ()):
                        ctx.comp("dve", lambda v, a=a, b2=b2: v.tensor_tensor(
                            pacc[:, a * 1024:(a + 1) * 1024],
                            pacc[:, a * 1024:(a + 1) * 1024],
                            pacc[:, b2 * 1024:(b2 + 1) * 1024], op=ALU.add),
                            reads=[("pacc", a), ("pacc", b2)],
                            writes=[("pacc", a)])
            emit_q_pair(hp + 2)
            for s, h in ((0, h0), (1, h1)):
                lbank = s
                mm(pb[lbank], ones128[:], pacc[:, s * 512:(s + 1) * 512], True, True,
                   reads=[("ones128",), ("pacc", 0)],
                   war=[(f"pb{lbank}",)], writes=[(f"pb{lbank}",)], inc=True)
                ctx.comp("dve", lambda v, lbank=lbank: v.reciprocal(linv[0][:], pb[lbank]),
                         reads=[(f"pb{lbank}",)], writes=[("linv",)])
                oslice = ptile[0][:, dead * 1024 + s * 512: dead * 1024 + (s + 1) * 512]
                ctx.comp("dve", lambda v, s=s, oslice=oslice: v.tensor_tensor(
                    oslice, pb[4 + s], linv[0][:], op=ALU.mult),
                    reads=[(f"pb{4 + s}",), ("linv",)],
                    writes=[("pt", dead)])
                ctx.dma("sp", lambda e, ch=ch, h=h, oslice=oslice: e.dma_start(
                    out=oall[ch, :, h, :], in_=oslice),
                    reads=[("pt", dead)], writes=[("oall", ch)], fam=f"sp{ch}")

    if STOP_PHASE >= 2:
        ln_quarter(0, slotA, "slotA")
    if STOP_PHASE >= 3:
        kv_units(0, slotA, "slotA")
    if STOP_PHASE >= 2:
        ln_quarter(1, slotB, "slotB")
    if STOP_PHASE >= 3:
        kv_units(1, slotB, "slotB")
    if STOP_PHASE >= 4:
        attention(0, slotB, "slotB")
    if STOP_PHASE >= 5:
        ln_quarter(2, slotA, "slotA")
        load_cos_half(1)
        kv_units(2, slotA, "slotA")
        ln_quarter(3, slotB, "slotB")
        kv_units(3, slotB, "slotB")
        attention(1, slotA, "slotA")

    ctx.fence()
    free("pacc", "linv0", "ptile0", "qh2", "qh1", "qh0",
         "vtb", "t1b", "ksb", "xbuf", "slotB", "slotA", "vall", "ktall",
         "sinh", "cosh", "biasv", "tri", "ones128")

    attnT = [sb(f"attnT{i}", (128, 512), BF16) for i in range(2)]
    x1 = sb("x1", (128, 8 * D), BF16)
    sc2 = sb("sc2", (128, D), F32)
    orhs = sb("orhs", (128, H * 512), BF16)
    atok = sb("atok", (128, 4 * D), BF16)
    xqb = sb("xqb", (128, D), F32)

    for ch in (range(2) if STOP_PHASE >= 6 else ()):
        ctx.dma("sp", lambda e, ch=ch: e.dma_start(
            out=orhs[:].rearrange("p (a m) -> p a m", a=H),
            in_=oall[ch]),
            reads=[("oall", ch)], writes=[("orhs",)], fam="orhs")

        def wo_prefetch(od, par):
            ctx.dma("pool", lambda e, od=od, par=par: e.dma_start(
                out=wbuf[par][:].rearrange("p (a m) -> p a m", a=DT),
                in_=wo4[od]),
                writes=[("wbuf", par)], fam=f"w{par}")

        for uu in range(3):
            wo_prefetch(uu, uu)
        pending = []
        for od in range(DT):
            par = od % 3
            bank = od % 2
            bk = (f"pb{bank}",)
            ap2 = od % 2
            for ct in range(H):
                mm(pb[bank],
                   wbuf[par][:, ct * 128:(ct + 1) * 128],
                   orhs[:, ct * 512:(ct + 1) * 512],
                   ct == 0, ct == H - 1,
                   reads=[("wbuf", par), ("orhs",)],
                   war=[bk] if ct == 0 else (),
                   writes=[bk] if ct == H - 1 else (),
                   inc=(ct == H - 1))
            if od + 3 < DT:
                wo_prefetch(od + 3, par)
            ctx.comp("act", lambda s, b=bank, ap2=ap2: s.copy(attnT[ap2][:], pb[b]),
                     reads=[bk], writes=[("attnT", ap2)])
            for fl in pending:
                fl()
            pending = [lambda od=od, ap2=ap2: transpose_batch(
                attnT[ap2], ("attnT", ap2), 1,
                lambda g, od=od: atok[:].rearrange(
                    "p (a m) -> p a m", a=4)[:, 0:4, od * 128:(od + 1) * 128],
                ("atok",), evac_eng="dve")]
        for fl in pending:
            fl()
        for j in range(4):
            tglob = ch * 4 + j
            ctx.dma("sp", lambda e, tglob=tglob: e.dma_start(
                out=xqb[:], in_=xkv[512 + tglob * 128: 512 + (tglob + 1) * 128, :]),
                writes=[("xqb",)], fam="xq")
            tile = atok[:, j * D:(j + 1) * D]
            ln_stats(tile, [("atok",)])
            ln_normalize(sc2[:], tile, [("atok",)], [("sc2",)])
            ctx.comp("dve", lambda v, tglob=tglob: v.tensor_tensor(
                x1[:, tglob * D:(tglob + 1) * D], sc2[:], xqb[:], op=ALU.add),
                reads=[("sc2",), ("xqb",)], writes=[("x1", tglob)])

    ctx.fence()
    free("xqb", "atok", "orhs", "sc2")

    xn3T = [sb("xn3T0", (128, DT * 512), BF16)] * 2
    ftok = sb("ftok", (128, 4 * D), BF16)
    lg4 = sb("lg4", (128, 64), F32)
    er4 = sb("er4", (128, 64), F32)
    rank4 = sb("rank4", (128, 64), F32)
    cmp4 = sb("cmp4", (128, 64), F32)
    sel4 = sb("sel4", (128, 64), F32)
    lr4 = sb("lr4", (128, 8), F32)
    pm4 = sb("pm4", (128, 64), BF16)
    ar15 = sb("ar15", (128, 64), F32)
    pmT = sb("pmT", (16, 512), BF16)
    hw_bf = sb("hw_bf", (128, 512), BF16)
    wr = sb("wr", (128, DT * 16), BF16)
    bt = sb("bt", (128, D), BF16)
    outb = sb("outb", (128, D), F32)

    ctx.dma("pool", lambda e: e.dma_start(
        out=wr[:].rearrange("p (a m) -> p a m", a=DT),
        in_=wrT.rearrange("(a p) o -> p a o", p=128)), writes=[("wr",)], fam="c2")
    ctx.dma("pool", lambda e: e.dma_start(out=bt[:], in_=btL), writes=[("bt",)], fam="c2")
    ctx.dma("sp", lambda e: e.dma_start(out=ar15[:], in_=ar15_in), writes=[("ar15",)], fam="xq")

    def seg(buf, j, w=16):
        return buf[:, j * w:(j + 1) * w]

    def seg3(buf, a, b):
        return buf[:].rearrange("p (a m) -> p a m", a=4)[:, :, a:b]

    def ln3_chunk(ch):
        for j in range(4):
            tglob = ch * 4 + j
            tile = x1[:, tglob * D:(tglob + 1) * D]
            ln_stats(tile, [("x1", tglob)])
            ln_normalize(scratch[:], tile, [("x1", tglob)], [("scratch",)])
            transpose_batch(
                scratch, ("scratch",), 8,
                lambda g, j=j, ch=ch: xn3T[ch][:].rearrange(
                    "p (a m) -> p a m", a=DT)[:, 4 * g:4 * g + 4, j * 128:(j + 1) * 128],
                ("xn3T",))

    def router_chunk(ch):
        xk = ("xn3T",)
        for j in range(4):
            rb_bank = 2 + (j % 2)
            rbk = (f"pb{rb_bank}",)
            for dt in range(DT):
                mm(pb[rb_bank][:, 0:16],
                   xn3T[ch][:, dt * 512 + j * 128: dt * 512 + (j + 1) * 128],
                   wr[:, dt * 16:(dt + 1) * 16],
                   dt == 0, dt == DT - 1,
                   reads=[xk, ("wr",)],
                   war=[rbk] if dt == 0 else (),
                   writes=[rbk] if dt == DT - 1 else (),
                   inc=(dt == DT - 1))
            ctx.comp("dve", lambda v, j=j, b=rb_bank: v.tensor_tensor(
                seg(lg4, j)[:, 0:15], pb[b][:, 0:15], rbb[:, 0:15], op=ALU.add),
                reads=[rbk, ("rbb",)], writes=[("lg4", j)])
            ctx.comp("act", lambda s, j=j: s.activation(
                seg(er4, j)[:, 0:15], seg(lg4, j)[:, 0:15], AF.Exp,
                accum_out=lr4[:, j:j + 1]),
                reads=[("lg4", j)], writes=[("er4", j), ("lr4",)])
        ctx.comp("dve", lambda v: v.tensor_copy(rank4[:], ar15[:]),
                 reads=[("ar15",)], writes=[("rank4",)])
        er_keys = [("er4", j) for j in range(4)]
        for s_ in range(1, 15):
            w = 15 - s_
            ctx.comp("dve", lambda v, s_=s_, w=w: v.tensor_tensor(
                seg3(cmp4, 0, w), seg3(er4, s_, 15), seg3(er4, 0, w), op=ALU.is_gt),
                reads=er_keys, writes=[("cmp4",)])
            ctx.comp("dve", lambda v, w=w: v.tensor_tensor(
                seg3(rank4, 0, w), seg3(rank4, 0, w), seg3(cmp4, 0, w), op=ALU.add),
                reads=[("cmp4",)], writes=[("rank4",)])
            ctx.comp("dve", lambda v, s_=s_, w=w: v.tensor_tensor(
                seg3(rank4, s_, 15), seg3(rank4, s_, 15), seg3(cmp4, 0, w), op=ALU.subtract),
                reads=[("cmp4",)], writes=[("rank4",)])
        ctx.comp("dve", lambda v: v.tensor_single_scalar(
            sel4[:], rank4[:], 7.5, op=ALU.is_lt),
            reads=[("rank4",)], writes=[("sel4",)])
        ctx.comp("dve", lambda v: v.reciprocal(lr4[:, 4:8], lr4[:, 0:4]),
                 reads=[("lr4",)], writes=[("lr4i",)])
        ctx.comp("dve", lambda v: v.memset(seg3(pm4, 15, 16), 0.0), writes=[("pm4z",)])
        for j in range(4):
            ctx.comp("dve", lambda v, j=j: v.tensor_scalar(
                seg(pm4, j)[:, 0:15], seg(er4, j)[:, 0:15], lr4[:, 4 + j:5 + j],
                None, op0=ALU.mult),
                reads=[("er4", j), ("lr4i",)], writes=[("pm4", j, "a")])
            ctx.comp("dve", lambda v, j=j: v.tensor_tensor(
                seg(pm4, j)[:, 0:15], seg(pm4, j)[:, 0:15], seg(sel4, j)[:, 0:15],
                op=ALU.mult),
                reads=[("pm4", j, "a"), ("sel4",)], writes=[("pm4", j)])
            mm(pb[5][0:16, j * 128:(j + 1) * 128], seg(pm4, j), ident[:], True, True,
               reads=[("pm4", j), ("pm4z",), ("ident",)],
               war=[("pb5",)] if j == 0 else (),
               writes=[("pb5",)] if j == 3 else (),
               inc=(j == 3))
        ctx.comp("act", lambda s: s.copy(pmT[:], pb[5][0:16, :]),
                 reads=[("pb5",)], writes=[("pmT",)])
        mm(pb[3], m1[:], pmT[:], True, True,
           reads=[("m1",), ("pmT",)], war=[("pb3",)], writes=[("pb3",)], inc=True)
        ctx.dma("pool", lambda e: e.dma_start(
            out=wbuf[2][:].rearrange("p (a m) -> p a m", a=DT),
            in_=atT.rearrange("(a p) o -> p a o", p=128)),
            writes=[("wbuf", 2)], fam="w2")
        for dt in range(DT):
            mm(pb[4],
               wbuf[2][:, dt * 128:(dt + 1) * 128],
               xn3T[ch][:, dt * 512:(dt + 1) * 512],
               dt == 0, dt == DT - 1,
               reads=[("wbuf", 2), xk],
               war=[("pb4",)] if dt == 0 else (),
               writes=[("pb4",)] if dt == DT - 1 else (),
               inc=(dt == DT - 1))
        ctx.comp("act", lambda s: s.mul(scratch[:, 0:512], pb[4], LORA_SCALING),
                 reads=[("pb4",)], writes=[("scratch",)])
        ctx.comp("dve", lambda v: v.tensor_tensor(
            hw_bf[:], scratch[:, 0:512], pb[3], op=ALU.mult),
            reads=[("scratch",), ("pb3",)], writes=[("hw",)])

    def base_chunk(ch):
        xk = ("xn3T",)

        def wb_prefetch(ot, par):
            ctx.dma("pool", lambda e, ot=ot, par=par: e.dma_start(
                out=wbuf[par][:].rearrange("p (a m) -> p a m", a=DT),
                in_=wb4[ot]),
                writes=[("wbuf", par)], fam=f"w{par}")

        for uu in range(3):
            wb_prefetch(uu, uu)
        pending = []
        for ot in range(DT):
            par = ot % 3
            bank = ot % 2
            bk = (f"pb{bank}",)
            ap2 = ot % 2
            for dt in range(DT):
                mm(pb[bank],
                   wbuf[par][:, dt * 128:(dt + 1) * 128],
                   xn3T[ch][:, dt * 512:(dt + 1) * 512],
                   dt == 0, False,
                   reads=[("wbuf", par), xk],
                   war=[bk] if dt == 0 else (),
                   inc=False)
            mm(pb[bank], bt[:, ot * 128:(ot + 1) * 128], hw_bf[:],
               False, True,
               reads=[("bt",), ("hw",)],
               writes=[bk], inc=True)
            if ot + 3 < DT:
                wb_prefetch(ot + 3, par)
            ctx.comp("act", lambda s, b=bank, ot=ot, ap2=ap2: s.activation(
                attnT[ap2][:], pb[b], AF.Identity, bias=bbs[:, ot:ot + 1]),
                reads=[bk, ("bbs",)], writes=[("attnT", ap2)])
            for fl in pending:
                fl()
            pending = [lambda ot=ot, ap2=ap2: transpose_batch(
                attnT[ap2], ("attnT", ap2), 1,
                lambda g, ot=ot: ftok[:].rearrange(
                    "p (a m) -> p a m", a=4)[:, 0:4, ot * 128:(ot + 1) * 128],
                ("ftok",), evac_eng="dve")]
        for fl in pending:
            fl()

    def ln4_chunk(ch):
        for j in range(4):
            tglob = ch * 4 + j
            tile = ftok[:, j * D:(j + 1) * D]
            ln_stats(tile, [("ftok",)])
            ctx.comp("dve", lambda v, tile=tile, tglob=tglob: v.scalar_tensor_tensor(
                outb[:], tile, stats[:, 6:7], x1[:, tglob * D:(tglob + 1) * D],
                op0=ALU.mult, op1=ALU.add),
                reads=[("ftok",), ("st", 6), ("x1", tglob)], writes=[("outb",)])
            ctx.comp("dve", lambda v: v.tensor_scalar(
                outb[:], outb[:], stats[:, 7:8], None, op0=ALU.subtract),
                reads=[("st", 7), ("outb",)], writes=[("outb",)])
            ctx.dma("sp", lambda e, tglob=tglob: e.dma_start(
                out=out[tglob * 128:(tglob + 1) * 128, :], in_=outb[:]),
                reads=[("outb",)], writes=[("out", tglob)], fam="outq")

    if STOP_PHASE >= 7:
        ln3_chunk(0)
        router_chunk(0)
        base_chunk(0)
        ln3_chunk(1)
        ln4_chunk(0)
        router_chunk(1)
        base_chunk(1)
        ln4_chunk(1)

    ctx.q["sp"].append(lambda eng, v=ctx.cnt["outq"], s=ctx.sems["outq"]: eng.wait_ge(s, v))

    with nc.Block() as block:
        @block.tensor
        def _(t):
            for f in ctx.q["pe"]:
                f(t)

        @block.scalar
        def _(s):
            for f in ctx.q["act"]:
                f(s)

        @block.vector
        def _(v):
            for f in ctx.q["dve"]:
                f(v)

        @block.sync
        def _(s):
            for f in ctx.q["sp"]:
                f(s)

        @block.gpsimd
        def _(g):
            for f in ctx.q["pool"]:
                f(g)

    for g in list(live.values())[::-1]:
        g.__exit__(None, None, None)
    for g in sem_guards[::-1]:
        g.__exit__(None, None, None)
    return nc



_NC_CACHE = None


def kernel(L0_attn_x, L0_attn_Wq, L0_attn_Wk, L0_attn_Wv, L0_attn_Wo,
           L0_ffn_base_W, L0_ffn_base_b, L0_ffn_router_W, L0_ffn_router_b,
           L0_ffn_A, L0_ffn_B, L0_ffn_expert_map):
    global _NC_CACHE
    x = np.asarray(L0_attn_x, dtype=np.float32)
    wq = np.asarray(L0_attn_Wq, dtype=np.float32)
    wk = np.asarray(L0_attn_Wk, dtype=np.float32)
    wv = np.asarray(L0_attn_Wv, dtype=np.float32)
    wo = np.asarray(L0_attn_Wo, dtype=np.float32)
    wb = np.asarray(L0_ffn_base_W, dtype=np.float32)
    bb = np.asarray(L0_ffn_base_b, dtype=np.float32)
    wr = np.asarray(L0_ffn_router_W, dtype=np.float32)
    rb = np.asarray(L0_ffn_router_b, dtype=np.float32)
    A = np.asarray(L0_ffn_A, dtype=np.float32)
    Bm = np.asarray(L0_ffn_B, dtype=np.float32)
    em = np.asarray(L0_ffn_expert_map, dtype=np.int32)

    cos_tab, sin_tab = _rope_tables()

    def tile4(wT, nu):
        return np.ascontiguousarray(
            wT.reshape(DT, 128, nu, 128).transpose(2, 1, 0, 3))

    shared = {
        "wq4": tile4(wq.T, H),
        "wk4": tile4(wk.T, KVH),
        "wv4": tile4(wv.T, KVH),
        "wo4": tile4(wo.T, DT),
        "wb4": tile4(wb.T, DT),
        "wrT": np.ascontiguousarray(
            np.concatenate([wr.T, np.zeros((D, 1), np.float32)], axis=1)),
        "atT": np.ascontiguousarray(A.reshape(E * R, D).T),
        "btL": np.ascontiguousarray(Bm.transpose(0, 2, 1).reshape(E * R, D)),
        "bbT": np.ascontiguousarray(bb.reshape(DT, 128).T),
        "rb_row": np.ascontiguousarray(rb.reshape(1, 15)),
        "em_row": np.ascontiguousarray(em.reshape(1, 15)),
        "ident_in": np.eye(128, dtype=BF),
        "ones128_in": np.ones((128, 128), dtype=BF),
        "ones1_in": np.ones((1, 128), dtype=np.float32),
        "ar8_in": np.arange(8, dtype=np.float32).reshape(8, 1),
        "expand_in": np.kron(np.eye(E), np.ones((1, R))).astype(BF),
        "ar15_in": np.tile(
            np.concatenate([np.arange(15, dtype=np.float32), [99.0]]),
            (128, 4)).astype(np.float32),
    }
    tri = (np.arange(512)[:, None] <= np.arange(512)[None, :]).astype(BF)
    shared["tri_in"] = np.ascontiguousarray(
        tri.reshape(4, 128, 512).transpose(1, 0, 2).reshape(128, 4 * 512))

    in_maps = []
    for core in range(8):
        b, s = core // 2, core % 2
        perm = _core_perm(s)
        m = dict(shared)
        m["xkv"] = np.ascontiguousarray(x[b][perm])
        m["coskT"] = np.ascontiguousarray(cos_tab[perm].T)
        m["sinkT"] = np.ascontiguousarray(sin_tab[perm].T)
        bv = np.full((128, 32), MASKV, dtype=np.float32)
        for ch in range(2):
            nkt = 8 if ch == 0 else 16
            diag0 = 4 if ch == 0 else 8
            qpos = perm[512 + ch * 512: 512 + (ch + 1) * 512]
            for kt in range(nkt):
                col = ch * 16 + kt
                if diag0 <= kt < diag0 + 4:
                    bv[:, col] = NEGC
                    continue
                kpos = perm[kt * 128:(kt + 1) * 128]
                vis = bool(np.all(kpos[:, None] <= qpos[None, :]))
                novis = bool(np.all(kpos[:, None] > qpos[None, :]))
                assert vis or novis, (core, ch, kt)
                bv[:, col] = NEGC if vis else MASKV
        m["biasv_in"] = bv
        in_maps.append(m)

    if _NC_CACHE is None:
        _NC_CACHE = build_program()
    res = run_bass_kernel_spmd(_NC_CACHE, in_maps, core_ids=list(range(8)))

    outp = np.empty((B, T, D), dtype=np.float32)
    for core in range(8):
        b, s = core // 2, core % 2
        o = res.results[core]["out"]
        if s == 0:
            outp[b, 0:512] = o[0:512]
            outp[b, 1536:2048] = o[512:1024]
        else:
            outp[b, 512:1536] = o
    return outp


# version 12
# speedup vs baseline: 1.0256x; 1.0256x over previous
import math
import numpy as np
import ml_dtypes
import concourse.bass as bass
import concourse.mybir as mybir
from concourse.bass_utils import run_bass_kernel_spmd

F32 = mybir.dt.float32
BF16 = mybir.dt.bfloat16
I32 = mybir.dt.int32
AF = mybir.ActivationFunctionType
ALU = mybir.AluOpType
AX = mybir.AxisListType

BF = ml_dtypes.bfloat16

B, T, D = 4, 2048, 4096
H, KVH, HD = 32, 8, 128
DT = D // 128
NKT = 16
SCALE = 1.0 / math.sqrt(HD)
NEGC = -10.0
MASKV = -30000.0
EPS = 1e-5
E, R = 8, 16
LORA_SCALING = 2.0
STOP_PHASE = 99
KV_KINDS = "kv"
KV_COUNT = 99


def _rope_tables():
    base = 500000.0
    inv_freq = 1.0 / base ** (np.arange(0, HD, 2, dtype=np.float32) / HD)
    factor, low_f, high_f, old_ctx = 32.0, 1.0, 4.0, 8192
    low_wl, high_wl = old_ctx / low_f, old_ctx / high_f
    wavelen = 2 * math.pi / inv_freq
    inv_llama = np.where(wavelen > low_wl, inv_freq / factor, inv_freq)
    smooth = (old_ctx / wavelen - low_f) / (high_f - low_f)
    smoothed = (1 - smooth) * inv_llama / factor + smooth * inv_llama
    is_med = ~(wavelen < high_wl) & ~(wavelen > low_wl)
    inv = np.where(is_med, smoothed, inv_llama).astype(np.float32)
    pos = np.arange(T, dtype=np.float32)
    freqs = pos[:, None] * inv[None, :]
    emb = np.concatenate([freqs, freqs], axis=-1)
    return np.cos(emb).astype(np.float32), np.sin(emb).astype(np.float32)


def _core_perm(s):
    p = np.arange(T)
    if s == 0:
        return np.concatenate([p[512:1024], p[0:512], p[1536:2048], p[1024:1536]])
    return np.concatenate([p[0:512], p[512:1536], p[1536:2048]])


ENGS = ("pe", "act", "dve", "sp", "pool")


class Ctx:

    DMA_FAMS = ("cst", "cs", "xb", "xq", "orhs", "sp0", "sp1", "outq",
                "w0", "w1", "w2", "c2")

    def __init__(self, nc):
        self.nc = nc
        self.q = {e: [] for e in ENGS}
        self.sem_names = ("pe", "act", "dve") + self.DMA_FAMS
        self.sems = {}
        self.cnt = {n: 0 for n in self.sem_names}
        self.hw = {e: {} for e in ENGS}
        self.buf = {}

    def make_sems(self, enter):
        for n in self.sem_names:
            self.sems[n] = enter(self.nc.semaphore(n))

    LOG = []

    def _need(self, e, sem, val):
        Ctx.LOG.append((e, "wait", sem, val, self.cnt.get(sem)))
        if sem in self.DMA_FAMS:
            val = self.cnt[sem]
        if self.hw[e].get(sem, 0) >= val:
            return
        self.hw[e][sem] = val
        s = self.sems[sem]
        self.q[e].append(lambda eng, s=s, val=val: eng.wait_ge(s, val))

    def _entry(self, k):
        if k not in self.buf:
            self.buf[k] = {"w": None, "r": []}
        return self.buf[k]

    def _waits(self, e, reads, writes, war):
        for k in reads:
            ent = self._entry(k)
            if ent["w"]:
                self._need(e, *ent["w"])
            if isinstance(k[0], str) and k[0].startswith("pb"):
                for r in ent["r"]:
                    if r[0] != e:
                        self._need(e, *r)
        for k in tuple(writes) + tuple(war):
            ent = self._entry(k)
            if ent["w"]:
                self._need(e, *ent["w"])
            for r in ent["r"]:
                self._need(e, *r)

    def fence(self):
        Ctx.LOG.append(("ALL", "FENCE", None, None, None))
        """Full barrier: every engine waits everything issued so far. Emitted at
        SBUF free points, where later allocations reuse freed addresses that the
        logical-key tracker cannot see."""
        snap = dict(self.cnt)
        for e in ENGS:
            for sem, val in snap.items():
                if val > 0:
                    self._need(e, sem, val)

    def comp(self, e, fn, reads=(), writes=(), war=(), inc=True):
        Ctx.LOG.append((e, "comp", self.cnt[e] + 1, tuple(writes), tuple(reads)))
        self._waits(e, reads, writes, war)
        sem = e
        val = self.cnt[sem] + 1
        if inc:
            self.cnt[sem] = val
            s = self.sems[sem]
            self.q[e].append(lambda eng, fn=fn, s=s: fn(eng).then_inc(s, 1))
        else:
            self.q[e].append(lambda eng, fn=fn: fn(eng))
        for k in reads:
            self._entry(k)["r"].append((sem, val))
        for k in writes:
            ent = self._entry(k)
            ent["w"] = (sem, val)
            ent["r"] = []
        return (sem, val)

    def dma(self, queue, fn, reads=(), writes=(), fam="cst"):
        Ctx.LOG.append((queue, "dma", fam, tuple(writes), tuple(reads)))
        """Emit a DMA on queue 'sp'|'pool' using semaphore family `fam`.

        DMA completions on one queue interleave across in-flight transfers, so
        cumulative counts are only sound per family where buffer reuse is
        WAR-gated (each family = one rotating buffer, or a write-once batch).
        """
        e = queue
        self._waits(e, reads, writes, ())
        sem = fam
        val = self.cnt[sem] + 16
        self.cnt[sem] = val
        s = self.sems[sem]
        self.q[e].append(lambda eng, fn=fn, s=s: fn(eng).then_inc(s, 16))
        for k in reads:
            self._entry(k)["r"].append((sem, val))
        for k in writes:
            ent = self._entry(k)
            ent["w"] = (sem, val)
            ent["r"] = []
        return (sem, val)



def build_program():
    nc = bass.Bass("TRN2", target_bir_lowering=False)

    def din(name, shape, dt=F32):
        return nc.dram_tensor(name, list(shape), dt, kind="ExternalInput").ap()

    xkv = din("xkv", (T, D))
    wq4 = din("wq4", (H, 128, DT, 128))
    wk4 = din("wk4", (KVH, 128, DT, 128))
    wv4 = din("wv4", (KVH, 128, DT, 128))
    wo4 = din("wo4", (DT, 128, DT, 128))
    wb4 = din("wb4", (DT, 128, DT, 128))
    wrT = din("wrT", (D, 16))
    atT = din("atT", (D, 128))
    btL = din("btL", (128, D))
    bbT = din("bbT", (128, DT))
    rb_row = din("rb_row", (1, 15))
    em_row = din("em_row", (1, 15), I32)
    coskT = din("coskT", (128, T))
    sinkT = din("sinkT", (128, T))
    tri_in = din("tri_in", (128, 4 * 512), BF16)
    biasv_in = din("biasv_in", (128, 32))
    ident_in = din("ident_in", (128, 128), BF16)
    ones128_in = din("ones128_in", (128, 128), BF16)
    ones1_in = din("ones1_in", (1, 128))
    ar8_in = din("ar8_in", (8, 1))
    expand_in = din("expand_in", (8, 128), BF16)
    ar15_in = din("ar15_in", (128, 64))

    out = nc.dram_tensor("out", [1024, D], F32, kind="ExternalOutput").ap()
    oall = nc.dram_tensor("oall", [2, 128, H, 512], BF16, kind="Internal").ap()

    live = {}
    done = []

    def enter(g):
        return g.__enter__()

    def sb(name, shape, dt):
        g = nc.sbuf_tensor(name, list(shape), dt)
        live[name] = g
        return g.__enter__()

    def ps(name, shape=(128, 512), dt=F32):
        g = nc.psum_tensor(name, list(shape), dt)
        live[name] = g
        return g.__enter__()

    def free(*names):
        for n in names:
            g = live.pop(n)
            g.__exit__(None, None, None)
            done.append(n)

    ctx = Ctx(nc)
    sem_guards = []
    ctx.make_sems(lambda g: (sem_guards.append(g), g.__enter__())[1])

    ident = sb("ident", (128, 128), BF16)
    stats = sb("stats", (128, 8), F32)
    scratch = sb("scratch", (128, D), BF16)
    wbuf = [sb(f"wbuf{i}", (128, DT * 128), BF16) for i in range(3)]
    m1 = sb("m1", (16, 128), BF16)
    rbb = sb("rbb", (128, 16), F32)
    bbs = sb("bbs", (128, DT), F32)
    epsb = sb("epsb", (128, 1), F32)

    ones128 = sb("ones128", (128, 128), BF16)
    tri = sb("tri", (128, 4 * 512), BF16)
    biasv = sb("biasv", (128, 32), F32)
    cosh = sb("cosh", (128, 1024), BF16)
    sinh = sb("sinh", (128, 1024), BF16)
    ktall = sb("ktall", (128, KVH * T), BF16)
    vall = sb("vall", (128, NKT * 1024), BF16)
    slotA = sb("slotA", (128, DT * 512), BF16)
    slotB = sb("slotB", (128, DT * 512), BF16)
    xbuf = sb("xbuf", (128, D), F32)
    ksb = sb("ksb", (128, 512), F32)
    t1b = sb("t1b", (128, 512), F32)
    vtb = sb("vtb", (128, 512), BF16)
    qh = [sb(f"qh{i}", (128, 512), BF16) for i in range(3)]
    ptile = [sb("ptile0", (128, 6 * 512), BF16)] * 2
    linv = [sb("linv0", (128, 512), F32)] * 2
    pacc = sb("pacc", (128, 4 * 512), BF16)
    r_misc = sb("r_misc", (16, 64), F32)
    ones1 = sb("ones1", (1, 128), F32)
    ar8 = sb("ar8", (8, 1), F32)
    expand = sb("expand", (8, 128), BF16)
    emi = sb("emi", (1, 15), I32)
    rb_sb = sb("rb_sb", (1, 15), F32)
    ohT_bf = sb("ohT_bf", (8, 15), BF16)

    pbA = ps("pbA", (128, 1024))
    pbB = ps("pbB", (128, 1024))
    pb47 = [ps(f"pbt{i}") for i in range(4, 8)]
    pb = [pbA[:, 0:512], pbA[:, 512:1024], pbB[:, 0:512], pbB[:, 512:1024],
          pb47[0][:], pb47[1][:], pb47[2][:], pb47[3][:]]
    pbS = [pbA, pbB]

    def mm(dst_ap, lhs_ap, rhs_ap, start, stop, reads, writes=(), war=(), inc=False):
        return ctx.comp(
            "pe",
            lambda t, d=dst_ap, l=lhs_ap, r=rhs_ap, s1=start, s2=stop: t.matmul(
                d, l, r, start=s1, stop=s2),
            reads=reads, writes=writes, war=war, inc=inc)

    ctx.dma("sp", lambda e: e.dma_start(out=ident[:], in_=ident_in), writes=[("ident",)])
    ctx.dma("sp", lambda e: e.dma_start(out=ones128[:], in_=ones128_in), writes=[("ones128",)])
    ctx.dma("sp", lambda e: e.dma_start(out=tri[:], in_=tri_in), writes=[("tri",)])
    ctx.dma("sp", lambda e: e.dma_start(out=biasv[:], in_=biasv_in), writes=[("biasv",)])
    ctx.dma("sp", lambda e: e.dma_start(out=bbs[:], in_=bbT), writes=[("bbs",)])
    ctx.dma("sp", lambda e: e.dma_start(out=ones1[:], in_=ones1_in), writes=[("ones1",)])
    ctx.dma("sp", lambda e: e.dma_start(out=ar8[:], in_=ar8_in), writes=[("ar8",)])
    ctx.dma("sp", lambda e: e.dma_start(out=expand[:], in_=expand_in), writes=[("expand",)])
    ctx.dma("sp", lambda e: e.dma_start(out=emi[:], in_=em_row), writes=[("emi",)])
    ctx.dma("sp", lambda e: e.dma_start(out=rb_sb[:], in_=rb_row), writes=[("rb_sb",)])

    def load_cos_half(hf):
        ctx.dma("pool", lambda e, hf=hf: e.dma_start(
            out=cosh[:], in_=coskT[:, hf * 1024:(hf + 1) * 1024]), writes=[("cosh",)], fam="cs")
        ctx.dma("pool", lambda e, hf=hf: e.dma_start(
            out=sinh[:], in_=sinkT[:, hf * 1024:(hf + 1) * 1024]), writes=[("sinh",)], fam="cs")

    load_cos_half(0)
    ctx.comp("dve", lambda v: v.memset(epsb[:], EPS), writes=[("epsb",)])

    mm(pb[2][:, 0:15], ones1[:, 0:128], rb_sb[:], True, True,
       reads=[("ones1",), ("rb_sb",)], war=[("pb2",)], writes=[("pb2",)], inc=True)
    ctx.comp("act", lambda s: s.copy(rbb[:, 0:15], pb[2][:, 0:15]),
             reads=[("pb2",)], writes=[("rbb",)])
    ctx.comp("dve", lambda v: v.tensor_copy(r_misc[0:1, 0:15], emi[:]),
             reads=[("emi",)], writes=[("emf",)])
    mm(pb[3][0:8, 0:15], ones1[:, 0:8], r_misc[0:1, 0:15], True, True,
       reads=[("emf",), ("ones1",)], war=[("pb3",)], writes=[("pb3",)], inc=True)
    ctx.comp("act", lambda s: s.copy(r_misc[0:8, 16:31], pb[3][0:8, 0:15]),
             reads=[("pb3",)], writes=[("emb8",)])
    ctx.comp("dve", lambda v: v.tensor_scalar(
        r_misc[0:8, 32:47], r_misc[0:8, 16:31], ar8[:, 0:1], None, op0=ALU.is_equal),
        reads=[("emb8",), ("ar8",)], writes=[("ohT",)])
    ctx.comp("dve", lambda v: v.tensor_copy(ohT_bf[:], r_misc[0:8, 32:47]),
             reads=[("ohT",)], writes=[("ohT_bf",)])
    mm(pb[4][0:15, 0:128], ohT_bf[:], expand[:], True, True,
       reads=[("ohT_bf",), ("expand",)], war=[("pb4",)], writes=[("pb4",)], inc=True)
    ctx.comp("dve", lambda v: v.memset(m1[:], 0.0), writes=[("m1",)])
    ctx.comp("act", lambda s: s.copy(m1[0:15, :], pb[4][0:15, 0:128]),
             reads=[("pb4",)], writes=[("m1",)])
    ctx.fence()
    free("ohT_bf", "rb_sb", "emi", "expand", "ar8", "ones1", "r_misc")

    def ln_stats(src_ap, src_keys):
        stc = lambda c: stats[:, c:c + 1]
        K = lambda c: ("st", c)
        ctx.comp("dve", lambda v: v.reduce_sum(stc(0), src_ap, axis=AX.X),
                 reads=list(src_keys), writes=[K(0)])
        ctx.comp("act", lambda s: s.activation(
            scratch[:], src_ap, AF.Square, accum_out=stc(1)),
            reads=list(src_keys), writes=[("scratch",), K(1)])
        ctx.comp("dve", lambda v: v.tensor_scalar_mul(stc(2), stc(0), 1.0 / D),
                 reads=[K(0)], writes=[K(2)])
        ctx.comp("dve", lambda v: v.tensor_tensor(stc(3), stc(2), stc(2), op=ALU.mult),
                 reads=[K(2)], writes=[K(3)])
        ctx.comp("dve", lambda v: v.tensor_scalar(
            stc(4), stc(1), 1.0 / D, None, op0=ALU.mult),
            reads=[K(1)], writes=[K(4)])
        ctx.comp("dve", lambda v: v.tensor_tensor(stc(4), stc(4), stc(3), op=ALU.subtract),
                 reads=[K(4), K(3)], writes=[K(4)])
        ctx.comp("act", lambda s: s.activation(stc(5), stc(4), AF.Ln, bias=epsb[:, 0:1]),
                 reads=[K(4), ("epsb",)], writes=[K(5)])
        ctx.comp("act", lambda s: s.activation(stc(6), stc(5), AF.Exp, scale=-0.5),
                 reads=[K(5)], writes=[K(6)])
        ctx.comp("dve", lambda v: v.tensor_tensor(stc(7), stc(2), stc(6), op=ALU.mult),
                 reads=[K(2), K(6)], writes=[K(7)])
        return stc(6), stc(7)

    def ln_normalize(dst_ap, src_ap, src_keys, dst_keys):
        rstd, nmr = stats[:, 6:7], stats[:, 7:8]
        ctx.comp("dve", lambda v: v.tensor_scalar(
            dst_ap, src_ap, rstd, nmr, op0=ALU.mult, op1=ALU.subtract),
            reads=list(src_keys) + [("st", 6), ("st", 7)], writes=list(dst_keys))

    def transpose_batch(src, src_key, n_groups, dest_fn, dest_key, evac_eng="act"):
        for g in range(n_groups):
            for j in range(4):
                idx = 4 * g + j
                last = (j == 3)
                mm(pb[6][:, j * 128:(j + 1) * 128],
                   src[:, idx * 128:(idx + 1) * 128], ident[:],
                   True, True,
                   reads=[src_key, ("ident",)],
                   war=[("pb6",)] if j == 0 else (),
                   writes=[("pb6",)] if last else (),
                   inc=last)
            if evac_eng == "act":
                ctx.comp("act", lambda e, g=g: e.copy(dest_fn(g), pb[6]),
                         reads=[("pb6",)], writes=[dest_key])
            else:
                ctx.comp("dve", lambda e, g=g: e.tensor_copy(dest_fn(g), pb[6]),
                         reads=[("pb6",)], writes=[dest_key])

    def ln_quarter(qi, slot, slot_name):
        for tl in range(4):
            t = qi * 4 + tl
            ctx.dma("sp", lambda e, t=t: e.dma_start(
                out=xbuf[:], in_=xkv[t * 128:(t + 1) * 128, :]), writes=[("xbuf",)], fam="xb")
            ln_stats(xbuf[:], [("xbuf",)])
            ln_normalize(scratch[:], xbuf[:], [("xbuf",)], [("scratch",)])
            transpose_batch(
                scratch, ("scratch",), 8,
                lambda g, tl=tl, slot=slot: slot[:].rearrange(
                    "p (a m) -> p a m", a=DT)[:, 4 * g:4 * g + 4, tl * 128:(tl + 1) * 128],
                (slot_name,))

    def kv_units(c, slot, slot_name):
        units = [("k", i) for i in range(KVH) if "k" in KV_KINDS] + \
                [("v", i) for i in range(KVH) if "v" in KV_KINDS]
        units = units[:KV_COUNT]

        def prefetch(u, par):
            kind, i = units[u]
            w = wk4 if kind == "k" else wv4
            ctx.dma("pool", lambda e, w=w, i=i, par=par: e.dma_start(
                out=wbuf[par][:].rearrange("p (a m) -> p a m", a=DT),
                in_=w[i]),
                writes=[("wbuf", par)], fam=f"w{par}")

        for uu in range(min(3, len(units))):
            prefetch(uu, uu % 3)
        co = c % 2
        for u, (kind, i) in enumerate(units):
            par = u % 3
            bank = 2 * (u % 2) + co
            bk = (f"pb{bank}",)
            for dt in range(DT):
                mm(pb[bank],
                   wbuf[par][:, dt * 128:(dt + 1) * 128],
                   slot[:, dt * 512:(dt + 1) * 512],
                   dt == 0, dt == DT - 1,
                   reads=[("wbuf", par), (slot_name,)],
                   war=[bk] if dt == 0 else (),
                   writes=[bk] if dt == DT - 1 else (),
                   inc=(dt == DT - 1))
            if kind == "k":
                ctx.comp("act", lambda s, b=bank: s.mul(ksb[0:64, :], pb[b][64:128, :], -1.0),
                         reads=[bk], writes=[("ksb",)], war=[("ksb2",)])
                ctx.comp("act", lambda s, b=bank: s.copy(ksb[64:128, :], pb[b][0:64, :]),
                         reads=[bk], writes=[("ksb2",)])
                ctx.comp("dve", lambda v, b=bank, co=co: v.tensor_tensor(
                    t1b[:], pb[b], cosh[:, co * 512:(co + 1) * 512], op=ALU.mult),
                    reads=[bk, ("cosh",)], writes=[("t1b",)])
                ctx.comp("dve", lambda v, co=co: v.tensor_tensor(
                    ksb[:], ksb[:], sinh[:, co * 512:(co + 1) * 512], op=ALU.mult),
                    reads=[("ksb",), ("ksb2",), ("sinh",)],
                    writes=[("ksb",), ("ksb2",)])
                dst = ktall[:, i * T + c * 512: i * T + (c + 1) * 512]
                ctx.comp("dve", lambda v, dst=dst: v.tensor_tensor(
                    dst, t1b[:], ksb[:], op=ALU.add),
                    reads=[("t1b",), ("ksb",), ("ksb2",)],
                    writes=[("ktall",)])
            else:
                ctx.comp("act", lambda s, b=bank: s.copy(vtb[:], pb[b]),
                         reads=[bk], writes=[("vtb",)])
                transpose_batch(
                    vtb, ("vtb",), 1,
                    lambda g, c=c, i=i: vall[:].rearrange(
                        "p (a m) -> p a m", a=NKT)[:, 4 * c:4 * c + 4, i * 128:(i + 1) * 128],
                    ("vall",))
            if u + 3 < len(units):
                prefetch(u + 3, par)

    def attention(ch, qslot, qslot_name):
        qht = qh + [vtb]
        qhk = lambda i: ("qh", i) if i < 3 else ("vtb",)
        dead = 2 if ch == 0 else 1
        nkt = 8 if ch == 0 else 16
        diag0 = 4 if ch == 0 else 8
        qcos0 = 512 if ch == 0 else 0
        pair_slot = [0] + [1] * (nkt // 2 - 1)
        folds_after = {kt: [(0, 1)] for kt in range(3, nkt, 2)}

        def wq_prefetch(h, par):
            ctx.dma("pool", lambda e, h=h, par=par: e.dma_start(
                out=wbuf[par][:].rearrange("p (a m) -> p a m", a=DT),
                in_=wq4[h]),
                writes=[("wbuf", par)], fam=f"w{par}")

        def q_proj_part(h, par, qbank, d0, dn):
            qk = (f"pb{qbank}",)
            for dt in range(d0, d0 + dn):
                mm(pb[qbank],
                   wbuf[par][:, dt * 128:(dt + 1) * 128],
                   qslot[:, dt * 512:(dt + 1) * 512],
                   dt == 0, dt == DT - 1,
                   reads=[("wbuf", par), (qslot_name,)],
                   war=[qk] if dt == 0 else (),
                   writes=[qk] if dt == DT - 1 else (),
                   inc=(dt == DT - 1))

        def q_proj(h, par, qbank):
            q_proj_part(h, par, qbank, 0, DT)

        def rope_q(qbank, s):
            qk = (f"pb{qbank}",)
            ctx.comp("act", lambda sc, b=qbank: sc.mul(ksb[0:64, :], pb[b][64:128, :], -1.0),
                     reads=[qk], writes=[("ksb",)], war=[("ksb2",)])
            ctx.comp("act", lambda sc, b=qbank: sc.copy(ksb[64:128, :], pb[b][0:64, :]),
                     reads=[qk], writes=[("ksb2",)])
            ctx.comp("dve", lambda v, b=qbank: v.tensor_tensor(
                t1b[:], pb[b], cosh[:, qcos0:qcos0 + 512], op=ALU.mult),
                reads=[qk, ("cosh",)], writes=[("t1b",)])
            ctx.comp("dve", lambda v: v.tensor_tensor(
                ksb[:], ksb[:], sinh[:, qcos0:qcos0 + 512], op=ALU.mult),
                reads=[("ksb",), ("ksb2",), ("sinh",)],
                writes=[("ksb",), ("ksb2",)])
            ctx.comp("dve", lambda v, s=s: v.tensor_tensor(
                qht[s][:], t1b[:], ksb[:], op=ALU.add),
                reads=[("t1b",), ("ksb",), ("ksb2",)],
                writes=[qhk(s)])

        npair = H // 2

        def q_pair_steps(p):
            if p >= npair:
                return [], []
            steps, tail = [], []
            for i, (h, qbank) in enumerate(((2 * p, 6), (2 * p + 1, 7))):
                par, sq = (2 * p + i) % 3, (2 * p + i) % 4
                for d0 in range(0, DT, 4):
                    steps.append(lambda h=h, par=par, qbank=qbank, d0=d0:
                                 q_proj_part(h, par, qbank, d0, 4))
                tail.append(lambda qbank=qbank, sq=sq: rope_q(qbank, sq))
            if p + 1 < npair:
                tail.append(lambda p=p: (wq_prefetch(2 * p + 2, (2 * p + 2) % 3),
                                         wq_prefetch(2 * p + 3, (2 * p + 3) % 3)))
            return steps, tail

        wq_prefetch(0, 0)
        wq_prefetch(1, 1)
        for p0 in (0, 1):
            a, b = q_pair_steps(p0)
            for f in a + b:
                f()
        for hp in range(npair):
            h0, h1 = 2 * hp, 2 * hp + 1
            g = h0 // 4
            s0q, s1q = (2 * hp) % 4, (2 * hp + 1) % 4
            qsteps, qtail = q_pair_steps(hp + 2)
            qdone = 0
            def av_step(kt):
                kslot = kt % 3
                pk = ("pt", kslot)
                for s in range(2):
                    mm(pb[4 + s],
                       vall[:, kt * 1024 + g * 128: kt * 1024 + g * 128 + 128],
                       ptile[0][:, kslot * 1024 + s * 512: kslot * 1024 + (s + 1) * 512],
                       kt == 0, kt == nkt - 1,
                       reads=[("vall",), pk],
                       war=[(f"pb{4 + s}",)] if kt == 0 else (),
                       writes=[(f"pb{4 + s}",)] if kt == nkt - 1 else (),
                       inc=(kt == nkt - 1))

            for kt in range(nkt):
                kslot = kt % 3
                sb2 = 2 * (kt % 2)
                pslot = ptile[0][:, kslot * 1024:(kslot + 1) * 1024]
                pk = ("pt", kslot)
                for s, sq in ((0, s0q), (1, s1q)):
                    mm(pb[sb2 + s],
                       ktall[:, g * T + kt * 128: g * T + kt * 128 + 128],
                       qht[sq][:],
                       True, True,
                       reads=[("ktall",), qhk(sq)],
                       war=[(f"pb{sb2 + s}",)], writes=[(f"pb{sb2 + s}",)], inc=True)
                pcol = biasv[:, ch * 16 + kt: ch * 16 + kt + 1]
                ctx.comp("act", lambda sc, pslot=pslot, pcol=pcol, kt=kt: sc.activation(
                    pslot, pbS[kt % 2][:], AF.Exp, bias=pcol, scale=SCALE),
                    reads=[(f"pb{sb2}",), (f"pb{sb2 + 1}",), ("biasv",)],
                    writes=[pk])
                if diag0 <= kt < diag0 + 4:
                    ktr = kt - diag0
                    for s in range(2):
                        ctx.comp("dve", lambda v, kslot=kslot, s=s, ktr=ktr: v.tensor_tensor(
                            ptile[0][:, kslot * 1024 + s * 512: kslot * 1024 + (s + 1) * 512],
                            ptile[0][:, kslot * 1024 + s * 512: kslot * 1024 + (s + 1) * 512],
                            tri[:, ktr * 512:(ktr + 1) * 512], op=ALU.mult),
                            reads=[pk, ("tri",)],
                            writes=[pk])
                if kt > 0:
                    av_step(kt - 1)
                want = (len(qsteps) * (kt + 1)) // nkt
                while qdone < want:
                    qsteps[qdone]()
                    qdone += 1
                if kt == nkt - 1:
                    av_step(kt)
                if kt % 2 == 1:
                    sidx = pair_slot[kt // 2]
                    k0, k1 = (kt - 1) % 3, kt % 3
                    ctx.comp("dve", lambda v, sidx=sidx, k0=k0, k1=k1: v.tensor_tensor(
                        pacc[:, sidx * 1024:(sidx + 1) * 1024],
                        ptile[0][:, k0 * 1024:(k0 + 1) * 1024],
                        ptile[0][:, k1 * 1024:(k1 + 1) * 1024], op=ALU.add),
                        reads=[("pt", k0), ("pt", k1)],
                        writes=[("pacc", sidx)])
                    for (a, b2) in folds_after.get(kt, ()):
                        ctx.comp("dve", lambda v, a=a, b2=b2: v.tensor_tensor(
                            pacc[:, a * 1024:(a + 1) * 1024],
                            pacc[:, a * 1024:(a + 1) * 1024],
                            pacc[:, b2 * 1024:(b2 + 1) * 1024], op=ALU.add),
                            reads=[("pacc", a), ("pacc", b2)],
                            writes=[("pacc", a)])
            for f in qtail:
                f()
            for s, h in ((0, h0), (1, h1)):
                lbank = s
                mm(pb[lbank], ones128[:], pacc[:, s * 512:(s + 1) * 512], True, True,
                   reads=[("ones128",), ("pacc", 0)],
                   war=[(f"pb{lbank}",)], writes=[(f"pb{lbank}",)], inc=True)
                ctx.comp("dve", lambda v, lbank=lbank: v.reciprocal(linv[0][:], pb[lbank]),
                         reads=[(f"pb{lbank}",)], writes=[("linv",)])
                oslice = ptile[0][:, dead * 1024 + s * 512: dead * 1024 + (s + 1) * 512]
                ctx.comp("dve", lambda v, s=s, oslice=oslice: v.tensor_tensor(
                    oslice, pb[4 + s], linv[0][:], op=ALU.mult),
                    reads=[(f"pb{4 + s}",), ("linv",)],
                    writes=[("pt", dead)])
                ctx.dma("sp", lambda e, ch=ch, h=h, oslice=oslice: e.dma_start(
                    out=oall[ch, :, h, :], in_=oslice),
                    reads=[("pt", dead)], writes=[("oall", ch)], fam=f"sp{ch}")

    if STOP_PHASE >= 2:
        ln_quarter(0, slotA, "slotA")
    if STOP_PHASE >= 3:
        kv_units(0, slotA, "slotA")
    if STOP_PHASE >= 2:
        ln_quarter(1, slotB, "slotB")
    if STOP_PHASE >= 3:
        kv_units(1, slotB, "slotB")
    if STOP_PHASE >= 4:
        attention(0, slotB, "slotB")
    if STOP_PHASE >= 5:
        ln_quarter(2, slotA, "slotA")
        load_cos_half(1)
        kv_units(2, slotA, "slotA")
        ln_quarter(3, slotB, "slotB")
        kv_units(3, slotB, "slotB")
        attention(1, slotA, "slotA")

    ctx.fence()
    free("pacc", "linv0", "ptile0", "qh2", "qh1", "qh0",
         "vtb", "t1b", "ksb", "xbuf", "slotB", "slotA", "vall", "ktall",
         "sinh", "cosh", "biasv", "tri", "ones128")

    attnT = [sb(f"attnT{i}", (128, 512), BF16) for i in range(2)]
    x1 = sb("x1", (128, 8 * D), BF16)
    sc2 = sb("sc2", (128, D), F32)
    orhs = sb("orhs", (128, H * 512), BF16)
    atok = sb("atok", (128, 4 * D), BF16)
    xqb = sb("xqb", (128, D), F32)

    for ch in (range(2) if STOP_PHASE >= 6 else ()):
        ctx.dma("sp", lambda e, ch=ch: e.dma_start(
            out=orhs[:].rearrange("p (a m) -> p a m", a=H),
            in_=oall[ch]),
            reads=[("oall", ch)], writes=[("orhs",)], fam="orhs")

        def wo_prefetch(od, par):
            ctx.dma("pool", lambda e, od=od, par=par: e.dma_start(
                out=wbuf[par][:].rearrange("p (a m) -> p a m", a=DT),
                in_=wo4[od]),
                writes=[("wbuf", par)], fam=f"w{par}")

        for uu in range(3):
            wo_prefetch(uu, uu)
        pending = []
        for od in range(DT):
            par = od % 3
            bank = od % 2
            bk = (f"pb{bank}",)
            ap2 = od % 2
            for ct in range(H):
                mm(pb[bank],
                   wbuf[par][:, ct * 128:(ct + 1) * 128],
                   orhs[:, ct * 512:(ct + 1) * 512],
                   ct == 0, ct == H - 1,
                   reads=[("wbuf", par), ("orhs",)],
                   war=[bk] if ct == 0 else (),
                   writes=[bk] if ct == H - 1 else (),
                   inc=(ct == H - 1))
            if od + 3 < DT:
                wo_prefetch(od + 3, par)
            ctx.comp("act", lambda s, b=bank, ap2=ap2: s.copy(attnT[ap2][:], pb[b]),
                     reads=[bk], writes=[("attnT", ap2)])
            for fl in pending:
                fl()
            pending = [lambda od=od, ap2=ap2: transpose_batch(
                attnT[ap2], ("attnT", ap2), 1,
                lambda g, od=od: atok[:].rearrange(
                    "p (a m) -> p a m", a=4)[:, 0:4, od * 128:(od + 1) * 128],
                ("atok",), evac_eng="dve")]
        for fl in pending:
            fl()
        for j in range(4):
            tglob = ch * 4 + j
            ctx.dma("sp", lambda e, tglob=tglob: e.dma_start(
                out=xqb[:], in_=xkv[512 + tglob * 128: 512 + (tglob + 1) * 128, :]),
                writes=[("xqb",)], fam="xq")
            tile = atok[:, j * D:(j + 1) * D]
            ln_stats(tile, [("atok",)])
            ln_normalize(sc2[:], tile, [("atok",)], [("sc2",)])
            ctx.comp("dve", lambda v, tglob=tglob: v.tensor_tensor(
                x1[:, tglob * D:(tglob + 1) * D], sc2[:], xqb[:], op=ALU.add),
                reads=[("sc2",), ("xqb",)], writes=[("x1", tglob)])

    ctx.fence()
    free("xqb", "atok", "orhs", "sc2")

    xn3T = [sb("xn3T0", (128, DT * 512), BF16)] * 2
    ftok = sb("ftok", (128, 4 * D), BF16)
    lg4 = sb("lg4", (128, 64), F32)
    er4 = sb("er4", (128, 64), F32)
    rank4 = sb("rank4", (128, 64), F32)
    cmp4 = sb("cmp4", (128, 64), F32)
    sel4 = sb("sel4", (128, 64), F32)
    lr4 = sb("lr4", (128, 8), F32)
    pm4 = sb("pm4", (128, 64), BF16)
    ar15 = sb("ar15", (128, 64), F32)
    pmT = sb("pmT", (16, 512), BF16)
    hw_bf = sb("hw_bf", (128, 512), BF16)
    wr = sb("wr", (128, DT * 16), BF16)
    bt = sb("bt", (128, D), BF16)
    outb = sb("outb", (128, D), F32)

    ctx.dma("pool", lambda e: e.dma_start(
        out=wr[:].rearrange("p (a m) -> p a m", a=DT),
        in_=wrT.rearrange("(a p) o -> p a o", p=128)), writes=[("wr",)], fam="c2")
    ctx.dma("pool", lambda e: e.dma_start(out=bt[:], in_=btL), writes=[("bt",)], fam="c2")
    ctx.dma("sp", lambda e: e.dma_start(out=ar15[:], in_=ar15_in), writes=[("ar15",)], fam="xq")

    def seg(buf, j, w=16):
        return buf[:, j * w:(j + 1) * w]

    def seg3(buf, a, b):
        return buf[:].rearrange("p (a m) -> p a m", a=4)[:, :, a:b]

    def ln3_chunk(ch):
        for j in range(4):
            tglob = ch * 4 + j
            tile = x1[:, tglob * D:(tglob + 1) * D]
            ln_stats(tile, [("x1", tglob)])
            ln_normalize(scratch[:], tile, [("x1", tglob)], [("scratch",)])
            transpose_batch(
                scratch, ("scratch",), 8,
                lambda g, j=j, ch=ch: xn3T[ch][:].rearrange(
                    "p (a m) -> p a m", a=DT)[:, 4 * g:4 * g + 4, j * 128:(j + 1) * 128],
                ("xn3T",))

    def router_chunk(ch):
        xk = ("xn3T",)
        for j in range(4):
            rb_bank = 2 + (j % 2)
            rbk = (f"pb{rb_bank}",)
            for dt in range(DT):
                mm(pb[rb_bank][:, 0:16],
                   xn3T[ch][:, dt * 512 + j * 128: dt * 512 + (j + 1) * 128],
                   wr[:, dt * 16:(dt + 1) * 16],
                   dt == 0, dt == DT - 1,
                   reads=[xk, ("wr",)],
                   war=[rbk] if dt == 0 else (),
                   writes=[rbk] if dt == DT - 1 else (),
                   inc=(dt == DT - 1))
            ctx.comp("dve", lambda v, j=j, b=rb_bank: v.tensor_tensor(
                seg(lg4, j)[:, 0:15], pb[b][:, 0:15], rbb[:, 0:15], op=ALU.add),
                reads=[rbk, ("rbb",)], writes=[("lg4", j)])
            ctx.comp("act", lambda s, j=j: s.activation(
                seg(er4, j)[:, 0:15], seg(lg4, j)[:, 0:15], AF.Exp,
                accum_out=lr4[:, j:j + 1]),
                reads=[("lg4", j)], writes=[("er4", j), ("lr4",)])
        ctx.comp("dve", lambda v: v.tensor_copy(rank4[:], ar15[:]),
                 reads=[("ar15",)], writes=[("rank4",)])
        er_keys = [("er4", j) for j in range(4)]
        for s_ in range(1, 15):
            w = 15 - s_
            ctx.comp("dve", lambda v, s_=s_, w=w: v.tensor_tensor(
                seg3(cmp4, 0, w), seg3(er4, s_, 15), seg3(er4, 0, w), op=ALU.is_gt),
                reads=er_keys, writes=[("cmp4",)])
            ctx.comp("dve", lambda v, w=w: v.tensor_tensor(
                seg3(rank4, 0, w), seg3(rank4, 0, w), seg3(cmp4, 0, w), op=ALU.add),
                reads=[("cmp4",)], writes=[("rank4",)])
            ctx.comp("dve", lambda v, s_=s_, w=w: v.tensor_tensor(
                seg3(rank4, s_, 15), seg3(rank4, s_, 15), seg3(cmp4, 0, w), op=ALU.subtract),
                reads=[("cmp4",)], writes=[("rank4",)])
        ctx.comp("dve", lambda v: v.tensor_single_scalar(
            sel4[:], rank4[:], 7.5, op=ALU.is_lt),
            reads=[("rank4",)], writes=[("sel4",)])
        ctx.comp("dve", lambda v: v.reciprocal(lr4[:, 4:8], lr4[:, 0:4]),
                 reads=[("lr4",)], writes=[("lr4i",)])
        ctx.comp("dve", lambda v: v.memset(seg3(pm4, 15, 16), 0.0), writes=[("pm4z",)])
        for j in range(4):
            ctx.comp("dve", lambda v, j=j: v.tensor_scalar(
                seg(pm4, j)[:, 0:15], seg(er4, j)[:, 0:15], lr4[:, 4 + j:5 + j],
                None, op0=ALU.mult),
                reads=[("er4", j), ("lr4i",)], writes=[("pm4", j, "a")])
            ctx.comp("dve", lambda v, j=j: v.tensor_tensor(
                seg(pm4, j)[:, 0:15], seg(pm4, j)[:, 0:15], seg(sel4, j)[:, 0:15],
                op=ALU.mult),
                reads=[("pm4", j, "a"), ("sel4",)], writes=[("pm4", j)])
            mm(pb[5][0:16, j * 128:(j + 1) * 128], seg(pm4, j), ident[:], True, True,
               reads=[("pm4", j), ("pm4z",), ("ident",)],
               war=[("pb5",)] if j == 0 else (),
               writes=[("pb5",)] if j == 3 else (),
               inc=(j == 3))
        ctx.comp("act", lambda s: s.copy(pmT[:], pb[5][0:16, :]),
                 reads=[("pb5",)], writes=[("pmT",)])
        mm(pb[3], m1[:], pmT[:], True, True,
           reads=[("m1",), ("pmT",)], war=[("pb3",)], writes=[("pb3",)], inc=True)
        ctx.dma("pool", lambda e: e.dma_start(
            out=wbuf[2][:].rearrange("p (a m) -> p a m", a=DT),
            in_=atT.rearrange("(a p) o -> p a o", p=128)),
            writes=[("wbuf", 2)], fam="w2")
        for dt in range(DT):
            mm(pb[4],
               wbuf[2][:, dt * 128:(dt + 1) * 128],
               xn3T[ch][:, dt * 512:(dt + 1) * 512],
               dt == 0, dt == DT - 1,
               reads=[("wbuf", 2), xk],
               war=[("pb4",)] if dt == 0 else (),
               writes=[("pb4",)] if dt == DT - 1 else (),
               inc=(dt == DT - 1))
        ctx.comp("act", lambda s: s.mul(scratch[:, 0:512], pb[4], LORA_SCALING),
                 reads=[("pb4",)], writes=[("scratch",)])
        ctx.comp("dve", lambda v: v.tensor_tensor(
            hw_bf[:], scratch[:, 0:512], pb[3], op=ALU.mult),
            reads=[("scratch",), ("pb3",)], writes=[("hw",)])

    def base_chunk(ch):
        xk = ("xn3T",)

        def wb_prefetch(ot, par):
            ctx.dma("pool", lambda e, ot=ot, par=par: e.dma_start(
                out=wbuf[par][:].rearrange("p (a m) -> p a m", a=DT),
                in_=wb4[ot]),
                writes=[("wbuf", par)], fam=f"w{par}")

        for uu in range(3):
            wb_prefetch(uu, uu)
        pending = []
        for ot in range(DT):
            par = ot % 3
            bank = ot % 2
            bk = (f"pb{bank}",)
            ap2 = ot % 2
            for dt in range(DT):
                mm(pb[bank],
                   wbuf[par][:, dt * 128:(dt + 1) * 128],
                   xn3T[ch][:, dt * 512:(dt + 1) * 512],
                   dt == 0, False,
                   reads=[("wbuf", par), xk],
                   war=[bk] if dt == 0 else (),
                   inc=False)
            mm(pb[bank], bt[:, ot * 128:(ot + 1) * 128], hw_bf[:],
               False, True,
               reads=[("bt",), ("hw",)],
               writes=[bk], inc=True)
            if ot + 3 < DT:
                wb_prefetch(ot + 3, par)
            ctx.comp("act", lambda s, b=bank, ot=ot, ap2=ap2: s.activation(
                attnT[ap2][:], pb[b], AF.Identity, bias=bbs[:, ot:ot + 1]),
                reads=[bk, ("bbs",)], writes=[("attnT", ap2)])
            for fl in pending:
                fl()
            pending = [lambda ot=ot, ap2=ap2: transpose_batch(
                attnT[ap2], ("attnT", ap2), 1,
                lambda g, ot=ot: ftok[:].rearrange(
                    "p (a m) -> p a m", a=4)[:, 0:4, ot * 128:(ot + 1) * 128],
                ("ftok",), evac_eng="dve")]
        for fl in pending:
            fl()

    def ln4_chunk(ch):
        for j in range(4):
            tglob = ch * 4 + j
            tile = ftok[:, j * D:(j + 1) * D]
            ln_stats(tile, [("ftok",)])
            ctx.comp("dve", lambda v, tile=tile, tglob=tglob: v.scalar_tensor_tensor(
                outb[:], tile, stats[:, 6:7], x1[:, tglob * D:(tglob + 1) * D],
                op0=ALU.mult, op1=ALU.add),
                reads=[("ftok",), ("st", 6), ("x1", tglob)], writes=[("outb",)])
            ctx.comp("dve", lambda v: v.tensor_scalar(
                outb[:], outb[:], stats[:, 7:8], None, op0=ALU.subtract),
                reads=[("st", 7), ("outb",)], writes=[("outb",)])
            ctx.dma("sp", lambda e, tglob=tglob: e.dma_start(
                out=out[tglob * 128:(tglob + 1) * 128, :], in_=outb[:]),
                reads=[("outb",)], writes=[("out", tglob)], fam="outq")

    if STOP_PHASE >= 7:
        ln3_chunk(0)
        router_chunk(0)
        base_chunk(0)
        ln3_chunk(1)
        ln4_chunk(0)
        router_chunk(1)
        base_chunk(1)
        ln4_chunk(1)

    ctx.q["sp"].append(lambda eng, v=ctx.cnt["outq"], s=ctx.sems["outq"]: eng.wait_ge(s, v))

    with nc.Block() as block:
        @block.tensor
        def _(t):
            for f in ctx.q["pe"]:
                f(t)

        @block.scalar
        def _(s):
            for f in ctx.q["act"]:
                f(s)

        @block.vector
        def _(v):
            for f in ctx.q["dve"]:
                f(v)

        @block.sync
        def _(s):
            for f in ctx.q["sp"]:
                f(s)

        @block.gpsimd
        def _(g):
            for f in ctx.q["pool"]:
                f(g)

    for g in list(live.values())[::-1]:
        g.__exit__(None, None, None)
    for g in sem_guards[::-1]:
        g.__exit__(None, None, None)
    return nc



_NC_CACHE = None


def kernel(L0_attn_x, L0_attn_Wq, L0_attn_Wk, L0_attn_Wv, L0_attn_Wo,
           L0_ffn_base_W, L0_ffn_base_b, L0_ffn_router_W, L0_ffn_router_b,
           L0_ffn_A, L0_ffn_B, L0_ffn_expert_map):
    global _NC_CACHE
    x = np.asarray(L0_attn_x, dtype=np.float32)
    wq = np.asarray(L0_attn_Wq, dtype=np.float32)
    wk = np.asarray(L0_attn_Wk, dtype=np.float32)
    wv = np.asarray(L0_attn_Wv, dtype=np.float32)
    wo = np.asarray(L0_attn_Wo, dtype=np.float32)
    wb = np.asarray(L0_ffn_base_W, dtype=np.float32)
    bb = np.asarray(L0_ffn_base_b, dtype=np.float32)
    wr = np.asarray(L0_ffn_router_W, dtype=np.float32)
    rb = np.asarray(L0_ffn_router_b, dtype=np.float32)
    A = np.asarray(L0_ffn_A, dtype=np.float32)
    Bm = np.asarray(L0_ffn_B, dtype=np.float32)
    em = np.asarray(L0_ffn_expert_map, dtype=np.int32)

    cos_tab, sin_tab = _rope_tables()

    def tile4(wT, nu):
        return np.ascontiguousarray(
            wT.reshape(DT, 128, nu, 128).transpose(2, 1, 0, 3))

    shared = {
        "wq4": tile4(wq.T, H),
        "wk4": tile4(wk.T, KVH),
        "wv4": tile4(wv.T, KVH),
        "wo4": tile4(wo.T, DT),
        "wb4": tile4(wb.T, DT),
        "wrT": np.ascontiguousarray(
            np.concatenate([wr.T, np.zeros((D, 1), np.float32)], axis=1)),
        "atT": np.ascontiguousarray(A.reshape(E * R, D).T),
        "btL": np.ascontiguousarray(Bm.transpose(0, 2, 1).reshape(E * R, D)),
        "bbT": np.ascontiguousarray(bb.reshape(DT, 128).T),
        "rb_row": np.ascontiguousarray(rb.reshape(1, 15)),
        "em_row": np.ascontiguousarray(em.reshape(1, 15)),
        "ident_in": np.eye(128, dtype=BF),
        "ones128_in": np.ones((128, 128), dtype=BF),
        "ones1_in": np.ones((1, 128), dtype=np.float32),
        "ar8_in": np.arange(8, dtype=np.float32).reshape(8, 1),
        "expand_in": np.kron(np.eye(E), np.ones((1, R))).astype(BF),
        "ar15_in": np.tile(
            np.concatenate([np.arange(15, dtype=np.float32), [99.0]]),
            (128, 4)).astype(np.float32),
    }
    tri = (np.arange(512)[:, None] <= np.arange(512)[None, :]).astype(BF)
    shared["tri_in"] = np.ascontiguousarray(
        tri.reshape(4, 128, 512).transpose(1, 0, 2).reshape(128, 4 * 512))

    in_maps = []
    for core in range(8):
        b, s = core // 2, core % 2
        perm = _core_perm(s)
        m = dict(shared)
        m["xkv"] = np.ascontiguousarray(x[b][perm])
        m["coskT"] = np.ascontiguousarray(cos_tab[perm].T)
        m["sinkT"] = np.ascontiguousarray(sin_tab[perm].T)
        bv = np.full((128, 32), MASKV, dtype=np.float32)
        for ch in range(2):
            nkt = 8 if ch == 0 else 16
            diag0 = 4 if ch == 0 else 8
            qpos = perm[512 + ch * 512: 512 + (ch + 1) * 512]
            for kt in range(nkt):
                col = ch * 16 + kt
                if diag0 <= kt < diag0 + 4:
                    bv[:, col] = NEGC
                    continue
                kpos = perm[kt * 128:(kt + 1) * 128]
                vis = bool(np.all(kpos[:, None] <= qpos[None, :]))
                novis = bool(np.all(kpos[:, None] > qpos[None, :]))
                assert vis or novis, (core, ch, kt)
                bv[:, col] = NEGC if vis else MASKV
        m["biasv_in"] = bv
        in_maps.append(m)

    if _NC_CACHE is None:
        _NC_CACHE = build_program()
    res = run_bass_kernel_spmd(_NC_CACHE, in_maps, core_ids=list(range(8)))

    outp = np.empty((B, T, D), dtype=np.float32)
    for core in range(8):
        b, s = core // 2, core % 2
        o = res.results[core]["out"]
        if s == 0:
            outp[b, 0:512] = o[0:512]
            outp[b, 1536:2048] = o[512:1024]
        else:
            outp[b, 512:1536] = o
    return outp


# version 14
# speedup vs baseline: 1.0358x; 1.0100x over previous
import math
import numpy as np
import ml_dtypes
import concourse.bass as bass
import concourse.mybir as mybir
from concourse.bass_utils import run_bass_kernel_spmd

F32 = mybir.dt.float32
BF16 = mybir.dt.bfloat16
I32 = mybir.dt.int32
AF = mybir.ActivationFunctionType
ALU = mybir.AluOpType
AX = mybir.AxisListType

BF = ml_dtypes.bfloat16

B, T, D = 4, 2048, 4096
H, KVH, HD = 32, 8, 128
DT = D // 128
NKT = 16
SCALE = 1.0 / math.sqrt(HD)
NEGC = -10.0
MASKV = -30000.0
EPS = 1e-5
E, R = 8, 16
LORA_SCALING = 2.0
STOP_PHASE = 99
KV_KINDS = "kv"
KV_COUNT = 99


def _rope_tables():
    base = 500000.0
    inv_freq = 1.0 / base ** (np.arange(0, HD, 2, dtype=np.float32) / HD)
    factor, low_f, high_f, old_ctx = 32.0, 1.0, 4.0, 8192
    low_wl, high_wl = old_ctx / low_f, old_ctx / high_f
    wavelen = 2 * math.pi / inv_freq
    inv_llama = np.where(wavelen > low_wl, inv_freq / factor, inv_freq)
    smooth = (old_ctx / wavelen - low_f) / (high_f - low_f)
    smoothed = (1 - smooth) * inv_llama / factor + smooth * inv_llama
    is_med = ~(wavelen < high_wl) & ~(wavelen > low_wl)
    inv = np.where(is_med, smoothed, inv_llama).astype(np.float32)
    pos = np.arange(T, dtype=np.float32)
    freqs = pos[:, None] * inv[None, :]
    emb = np.concatenate([freqs, freqs], axis=-1)
    return np.cos(emb).astype(np.float32), np.sin(emb).astype(np.float32)


def _core_perm(s):
    p = np.arange(T)
    if s == 0:
        return np.concatenate([p[512:1024], p[0:512], p[1536:2048], p[1024:1536]])
    return np.concatenate([p[0:512], p[512:1536], p[1536:2048]])


ENGS = ("pe", "act", "dve", "sp", "pool")


class Ctx:

    DMA_FAMS = ("cst", "cs", "xb", "xq", "orhs", "sp0", "sp1", "outq",
                "w0", "w1", "w2", "c2")

    def __init__(self, nc):
        self.nc = nc
        self.q = {e: [] for e in ENGS}
        self.sem_names = ("pe", "act", "dve") + self.DMA_FAMS
        self.sems = {}
        self.cnt = {n: 0 for n in self.sem_names}
        self.hw = {e: {} for e in ENGS}
        self.buf = {}
        self.trace = {e: [] for e in ENGS}

    def make_sems(self, enter):
        for n in self.sem_names:
            self.sems[n] = enter(self.nc.semaphore(n))

    LOG = []

    def _need(self, e, sem, val):
        Ctx.LOG.append((e, "wait", sem, val, self.cnt.get(sem)))
        if sem in self.DMA_FAMS:
            val = self.cnt[sem]
        if self.hw[e].get(sem, 0) >= val:
            return
        self.hw[e][sem] = val
        s = self.sems[sem]
        self.q[e].append(lambda eng, s=s, val=val: eng.wait_ge(s, val))
        self.trace[e].append(("w", sem, val))

    def _entry(self, k):
        if k not in self.buf:
            self.buf[k] = {"w": None, "r": []}
        return self.buf[k]

    def _waits(self, e, reads, writes, war):
        for k in reads:
            ent = self._entry(k)
            if ent["w"]:
                self._need(e, *ent["w"])
            if isinstance(k[0], str) and k[0].startswith("pb"):
                for r in ent["r"]:
                    if r[0] != e:
                        self._need(e, *r)
        for k in tuple(writes) + tuple(war):
            ent = self._entry(k)
            if ent["w"]:
                self._need(e, *ent["w"])
            for r in ent["r"]:
                self._need(e, *r)

    def fence(self):
        Ctx.LOG.append(("ALL", "FENCE", None, None, None))
        """Full barrier: every engine waits everything issued so far. Emitted at
        SBUF free points, where later allocations reuse freed addresses that the
        logical-key tracker cannot see."""
        snap = dict(self.cnt)
        for e in ENGS:
            for sem, val in snap.items():
                if val > 0:
                    self._need(e, sem, val)

    def comp(self, e, fn, reads=(), writes=(), war=(), inc=True):
        Ctx.LOG.append((e, "comp", self.cnt[e] + 1, tuple(writes), tuple(reads)))
        self._waits(e, reads, writes, war)
        sem = e
        val = self.cnt[sem] + 1
        if inc:
            self.cnt[sem] = val
            s = self.sems[sem]
            self.q[e].append(lambda eng, fn=fn, s=s: fn(eng).then_inc(s, 1))
            self.trace[e].append(("i", sem, 1))
        else:
            self.q[e].append(lambda eng, fn=fn: fn(eng))
        for k in reads:
            self._entry(k)["r"].append((sem, val))
        for k in writes:
            ent = self._entry(k)
            ent["w"] = (sem, val)
            ent["r"] = []
        return (sem, val)

    def dma(self, queue, fn, reads=(), writes=(), fam="cst"):
        Ctx.LOG.append((queue, "dma", fam, tuple(writes), tuple(reads)))
        """Emit a DMA on queue 'sp'|'pool' using semaphore family `fam`.

        DMA completions on one queue interleave across in-flight transfers, so
        cumulative counts are only sound per family where buffer reuse is
        WAR-gated (each family = one rotating buffer, or a write-once batch).
        """
        e = queue
        self._waits(e, reads, writes, ())
        sem = fam
        val = self.cnt[sem] + 16
        self.cnt[sem] = val
        s = self.sems[sem]
        self.q[e].append(lambda eng, fn=fn, s=s: fn(eng).then_inc(s, 16))
        self.trace[e].append(("i", sem, 16))
        for k in reads:
            self._entry(k)["r"].append((sem, val))
        for k in writes:
            ent = self._entry(k)
            ent["w"] = (sem, val)
            ent["r"] = []
        return (sem, val)

    def check(self):
        ptr = {e: 0 for e in ENGS}
        cnt = {n: 0 for n in self.sem_names}
        progress = True
        while progress:
            progress = False
            for e in ENGS:
                tr = self.trace[e]
                while ptr[e] < len(tr):
                    kind, sem, val = tr[ptr[e]]
                    if kind == "w" and cnt[sem] < val:
                        break
                    if kind == "i":
                        cnt[sem] += val
                    ptr[e] += 1
                    progress = True
        stuck = {e: (ptr[e], len(self.trace[e]), self.trace[e][ptr[e]])
                 for e in ENGS if ptr[e] < len(self.trace[e])}
        if stuck:
            raise RuntimeError(f"schedule deadlock: {stuck} cnt={cnt}")



def build_program():
    nc = bass.Bass("TRN2", target_bir_lowering=False)

    def din(name, shape, dt=F32):
        return nc.dram_tensor(name, list(shape), dt, kind="ExternalInput").ap()

    xkv = din("xkv", (T, D))
    wq4 = din("wq4", (H, 128, DT, 128))
    wk4 = din("wk4", (KVH, 128, DT, 128))
    wv4 = din("wv4", (KVH, 128, DT, 128))
    wo4 = din("wo4", (DT, 128, DT, 128))
    wb4 = din("wb4", (DT, 128, DT, 128))
    wrT = din("wrT", (D, 16))
    atT = din("atT", (D, 128))
    btL = din("btL", (128, D))
    bbT = din("bbT", (128, DT))
    rb_row = din("rb_row", (1, 15))
    em_row = din("em_row", (1, 15), I32)
    coskT = din("coskT", (128, T))
    sinkT = din("sinkT", (128, T))
    tri_in = din("tri_in", (128, 4 * 512), BF16)
    biasv_in = din("biasv_in", (128, 32))
    ident_in = din("ident_in", (128, 128), BF16)
    ones128_in = din("ones128_in", (128, 128), BF16)
    ones1_in = din("ones1_in", (1, 128))
    ar8_in = din("ar8_in", (8, 1))
    expand_in = din("expand_in", (8, 128), BF16)
    ar15_in = din("ar15_in", (128, 64))

    out = nc.dram_tensor("out", [1024, D], F32, kind="ExternalOutput").ap()
    oall = nc.dram_tensor("oall", [2, 128, H, 512], BF16, kind="Internal").ap()

    live = {}
    done = []

    def enter(g):
        return g.__enter__()

    def sb(name, shape, dt):
        g = nc.sbuf_tensor(name, list(shape), dt)
        live[name] = g
        return g.__enter__()

    def ps(name, shape=(128, 512), dt=F32):
        g = nc.psum_tensor(name, list(shape), dt)
        live[name] = g
        return g.__enter__()

    def free(*names):
        for n in names:
            g = live.pop(n)
            g.__exit__(None, None, None)
            done.append(n)

    ctx = Ctx(nc)
    sem_guards = []
    ctx.make_sems(lambda g: (sem_guards.append(g), g.__enter__())[1])

    ident = sb("ident", (128, 128), BF16)
    stats = sb("stats", (128, 8), F32)
    scratch = sb("scratch", (128, D), BF16)
    wbuf = [sb(f"wbuf{i}", (128, DT * 128), BF16) for i in range(3)]
    m1 = sb("m1", (16, 128), BF16)
    rbb = sb("rbb", (128, 16), F32)
    bbs = sb("bbs", (128, DT), F32)
    epsb = sb("epsb", (128, 1), F32)

    ones128 = sb("ones128", (128, 128), BF16)
    tri = sb("tri", (128, 4 * 512), BF16)
    biasv = sb("biasv", (128, 32), F32)
    cosh = sb("cosh", (128, 1024), BF16)
    sinh = sb("sinh", (128, 1024), BF16)
    ktall = sb("ktall", (128, KVH * T), BF16)
    vall = sb("vall", (128, NKT * 1024), BF16)
    slotA = sb("slotA", (128, DT * 512), BF16)
    slotB = sb("slotB", (128, DT * 512), BF16)
    xbuf = sb("xbuf", (128, D), F32)
    ksb = sb("ksb", (128, 512), F32)
    t1b = sb("t1b", (128, 512), F32)
    vtb = sb("vtb", (128, 512), BF16)
    qh = [sb(f"qh{i}", (128, 512), BF16) for i in range(3)]
    ptile = [sb("ptile0", (128, 6 * 512), BF16)] * 2
    linv = [sb("linv0", (128, 512), F32)] * 2
    pacc = sb("pacc", (128, 4 * 512), BF16)
    r_misc = sb("r_misc", (16, 64), F32)
    ones1 = sb("ones1", (1, 128), F32)
    ar8 = sb("ar8", (8, 1), F32)
    expand = sb("expand", (8, 128), BF16)
    emi = sb("emi", (1, 15), I32)
    rb_sb = sb("rb_sb", (1, 15), F32)
    ohT_bf = sb("ohT_bf", (8, 15), BF16)

    pbA = ps("pbA", (128, 1024))
    pbB = ps("pbB", (128, 1024))
    pb47 = [ps(f"pbt{i}") for i in range(4, 8)]
    pb = [pbA[:, 0:512], pbA[:, 512:1024], pbB[:, 0:512], pbB[:, 512:1024],
          pb47[0][:], pb47[1][:], pb47[2][:], pb47[3][:]]
    pbS = [pbA, pbB]

    def mm(dst_ap, lhs_ap, rhs_ap, start, stop, reads, writes=(), war=(), inc=False):
        return ctx.comp(
            "pe",
            lambda t, d=dst_ap, l=lhs_ap, r=rhs_ap, s1=start, s2=stop: t.matmul(
                d, l, r, start=s1, stop=s2),
            reads=reads, writes=writes, war=war, inc=inc)

    ctx.dma("sp", lambda e: e.dma_start(out=ident[:], in_=ident_in), writes=[("ident",)])
    ctx.dma("sp", lambda e: e.dma_start(out=ones128[:], in_=ones128_in), writes=[("ones128",)])
    ctx.dma("sp", lambda e: e.dma_start(out=tri[:], in_=tri_in), writes=[("tri",)])
    ctx.dma("sp", lambda e: e.dma_start(out=biasv[:], in_=biasv_in), writes=[("biasv",)])
    ctx.dma("sp", lambda e: e.dma_start(out=bbs[:], in_=bbT), writes=[("bbs",)])
    ctx.dma("sp", lambda e: e.dma_start(out=ones1[:], in_=ones1_in), writes=[("ones1",)])
    ctx.dma("sp", lambda e: e.dma_start(out=ar8[:], in_=ar8_in), writes=[("ar8",)])
    ctx.dma("sp", lambda e: e.dma_start(out=expand[:], in_=expand_in), writes=[("expand",)])
    ctx.dma("sp", lambda e: e.dma_start(out=emi[:], in_=em_row), writes=[("emi",)])
    ctx.dma("sp", lambda e: e.dma_start(out=rb_sb[:], in_=rb_row), writes=[("rb_sb",)])

    def load_cos_half(hf):
        ctx.dma("pool", lambda e, hf=hf: e.dma_start(
            out=cosh[:], in_=coskT[:, hf * 1024:(hf + 1) * 1024]), writes=[("cosh",)], fam="cs")
        ctx.dma("pool", lambda e, hf=hf: e.dma_start(
            out=sinh[:], in_=sinkT[:, hf * 1024:(hf + 1) * 1024]), writes=[("sinh",)], fam="cs")

    load_cos_half(0)
    ctx.comp("dve", lambda v: v.memset(epsb[:], EPS), writes=[("epsb",)])

    mm(pb[2][:, 0:15], ones1[:, 0:128], rb_sb[:], True, True,
       reads=[("ones1",), ("rb_sb",)], war=[("pb2",)], writes=[("pb2",)], inc=True)
    ctx.comp("act", lambda s: s.copy(rbb[:, 0:15], pb[2][:, 0:15]),
             reads=[("pb2",)], writes=[("rbb",)])
    ctx.comp("dve", lambda v: v.tensor_copy(r_misc[0:1, 0:15], emi[:]),
             reads=[("emi",)], writes=[("emf",)])
    mm(pb[3][0:8, 0:15], ones1[:, 0:8], r_misc[0:1, 0:15], True, True,
       reads=[("emf",), ("ones1",)], war=[("pb3",)], writes=[("pb3",)], inc=True)
    ctx.comp("act", lambda s: s.copy(r_misc[0:8, 16:31], pb[3][0:8, 0:15]),
             reads=[("pb3",)], writes=[("emb8",)])
    ctx.comp("dve", lambda v: v.tensor_scalar(
        r_misc[0:8, 32:47], r_misc[0:8, 16:31], ar8[:, 0:1], None, op0=ALU.is_equal),
        reads=[("emb8",), ("ar8",)], writes=[("ohT",)])
    ctx.comp("dve", lambda v: v.tensor_copy(ohT_bf[:], r_misc[0:8, 32:47]),
             reads=[("ohT",)], writes=[("ohT_bf",)])
    mm(pb[4][0:15, 0:128], ohT_bf[:], expand[:], True, True,
       reads=[("ohT_bf",), ("expand",)], war=[("pb4",)], writes=[("pb4",)], inc=True)
    ctx.comp("dve", lambda v: v.memset(m1[:], 0.0), writes=[("m1",)])
    ctx.comp("act", lambda s: s.copy(m1[0:15, :], pb[4][0:15, 0:128]),
             reads=[("pb4",)], writes=[("m1",)])
    ctx.fence()
    free("ohT_bf", "rb_sb", "emi", "expand", "ar8", "ones1", "r_misc")

    def ln_stats(src_ap, src_keys):
        stc = lambda c: stats[:, c:c + 1]
        K = lambda c: ("st", c)
        ctx.comp("dve", lambda v: v.reduce_sum(stc(0), src_ap, axis=AX.X),
                 reads=list(src_keys), writes=[K(0)])
        ctx.comp("act", lambda s: s.activation(
            scratch[:], src_ap, AF.Square, accum_out=stc(1)),
            reads=list(src_keys), writes=[("scratch",), K(1)])
        ctx.comp("dve", lambda v: v.tensor_scalar_mul(stc(2), stc(0), 1.0 / D),
                 reads=[K(0)], writes=[K(2)])
        ctx.comp("dve", lambda v: v.tensor_tensor(stc(3), stc(2), stc(2), op=ALU.mult),
                 reads=[K(2)], writes=[K(3)])
        ctx.comp("dve", lambda v: v.tensor_scalar(
            stc(4), stc(1), 1.0 / D, None, op0=ALU.mult),
            reads=[K(1)], writes=[K(4)])
        ctx.comp("dve", lambda v: v.tensor_tensor(stc(4), stc(4), stc(3), op=ALU.subtract),
                 reads=[K(4), K(3)], writes=[K(4)])
        ctx.comp("act", lambda s: s.activation(stc(5), stc(4), AF.Ln, bias=epsb[:, 0:1]),
                 reads=[K(4), ("epsb",)], writes=[K(5)])
        ctx.comp("act", lambda s: s.activation(stc(6), stc(5), AF.Exp, scale=-0.5),
                 reads=[K(5)], writes=[K(6)])
        ctx.comp("dve", lambda v: v.tensor_tensor(stc(7), stc(2), stc(6), op=ALU.mult),
                 reads=[K(2), K(6)], writes=[K(7)])
        return stc(6), stc(7)

    def ln_normalize(dst_ap, src_ap, src_keys, dst_keys):
        rstd, nmr = stats[:, 6:7], stats[:, 7:8]
        ctx.comp("dve", lambda v: v.tensor_scalar(
            dst_ap, src_ap, rstd, nmr, op0=ALU.mult, op1=ALU.subtract),
            reads=list(src_keys) + [("st", 6), ("st", 7)], writes=list(dst_keys))

    def transpose_batch(src, src_key, n_groups, dest_fn, dest_key, evac_eng="act"):
        for g in range(n_groups):
            for j in range(4):
                idx = 4 * g + j
                last = (j == 3)
                mm(pb[6][:, j * 128:(j + 1) * 128],
                   src[:, idx * 128:(idx + 1) * 128], ident[:],
                   True, True,
                   reads=[src_key, ("ident",)],
                   war=[("pb6",)] if j == 0 else (),
                   writes=[("pb6",)] if last else (),
                   inc=last)
            if evac_eng == "act":
                ctx.comp("act", lambda e, g=g: e.copy(dest_fn(g), pb[6]),
                         reads=[("pb6",)], writes=[dest_key])
            else:
                ctx.comp("dve", lambda e, g=g: e.tensor_copy(dest_fn(g), pb[6]),
                         reads=[("pb6",)], writes=[dest_key])

    def ln_quarter(qi, slot, slot_name):
        for tl in range(4):
            t = qi * 4 + tl
            ctx.dma("sp", lambda e, t=t: e.dma_start(
                out=xbuf[:], in_=xkv[t * 128:(t + 1) * 128, :]), writes=[("xbuf",)], fam="xb")
            ln_stats(xbuf[:], [("xbuf",)])
            ln_normalize(scratch[:], xbuf[:], [("xbuf",)], [("scratch",)])
            transpose_batch(
                scratch, ("scratch",), 8,
                lambda g, tl=tl, slot=slot: slot[:].rearrange(
                    "p (a m) -> p a m", a=DT)[:, 4 * g:4 * g + 4, tl * 128:(tl + 1) * 128],
                (slot_name,))

    def kv_units(c, slot, slot_name):
        units = [("k", i) for i in range(KVH) if "k" in KV_KINDS] + \
                [("v", i) for i in range(KVH) if "v" in KV_KINDS]
        units = units[:KV_COUNT]

        def prefetch(u, par):
            kind, i = units[u]
            w = wk4 if kind == "k" else wv4
            ctx.dma("pool", lambda e, w=w, i=i, par=par: e.dma_start(
                out=wbuf[par][:].rearrange("p (a m) -> p a m", a=DT),
                in_=w[i]),
                writes=[("wbuf", par)], fam=f"w{par}")

        for uu in range(min(3, len(units))):
            prefetch(uu, uu % 3)
        co = c % 2
        for u, (kind, i) in enumerate(units):
            par = u % 3
            bank = 2 * (u % 2) + co
            bk = (f"pb{bank}",)
            for dt in range(DT):
                mm(pb[bank],
                   wbuf[par][:, dt * 128:(dt + 1) * 128],
                   slot[:, dt * 512:(dt + 1) * 512],
                   dt == 0, dt == DT - 1,
                   reads=[("wbuf", par), (slot_name,)],
                   war=[bk] if dt == 0 else (),
                   writes=[bk] if dt == DT - 1 else (),
                   inc=(dt == DT - 1))
            if kind == "k":
                ctx.comp("act", lambda s, b=bank: s.mul(ksb[0:64, :], pb[b][64:128, :], -1.0),
                         reads=[bk], writes=[("ksb",)], war=[("ksb2",)])
                ctx.comp("act", lambda s, b=bank: s.copy(ksb[64:128, :], pb[b][0:64, :]),
                         reads=[bk], writes=[("ksb2",)])
                ctx.comp("dve", lambda v, b=bank, co=co: v.tensor_tensor(
                    t1b[:], pb[b], cosh[:, co * 512:(co + 1) * 512], op=ALU.mult),
                    reads=[bk, ("cosh",)], writes=[("t1b",)])
                ctx.comp("dve", lambda v, co=co: v.tensor_tensor(
                    ksb[:], ksb[:], sinh[:, co * 512:(co + 1) * 512], op=ALU.mult),
                    reads=[("ksb",), ("ksb2",), ("sinh",)],
                    writes=[("ksb",), ("ksb2",)])
                dst = ktall[:, i * T + c * 512: i * T + (c + 1) * 512]
                ctx.comp("dve", lambda v, dst=dst: v.tensor_tensor(
                    dst, t1b[:], ksb[:], op=ALU.add),
                    reads=[("t1b",), ("ksb",), ("ksb2",)],
                    writes=[("ktall",)])
            else:
                ctx.comp("act", lambda s, b=bank: s.copy(vtb[:], pb[b]),
                         reads=[bk], writes=[("vtb",)])
                transpose_batch(
                    vtb, ("vtb",), 1,
                    lambda g, c=c, i=i: vall[:].rearrange(
                        "p (a m) -> p a m", a=NKT)[:, 4 * c:4 * c + 4, i * 128:(i + 1) * 128],
                    ("vall",))
            if u + 3 < len(units):
                prefetch(u + 3, par)

    def attention(ch, qslot, qslot_name):
        qht = qh + [vtb]
        qhk = lambda i: ("qh", i) if i < 3 else ("vtb",)
        dead = 2 if ch == 0 else 1
        nkt = 8 if ch == 0 else 16
        diag0 = 4 if ch == 0 else 8
        qcos0 = 512 if ch == 0 else 0
        pair_slot = [0] + [1] * (nkt // 2 - 1)
        folds_after = {kt: [(0, 1)] for kt in range(3, nkt, 2)}

        def wq_prefetch(h, par):
            ctx.dma("pool", lambda e, h=h, par=par: e.dma_start(
                out=wbuf[par][:].rearrange("p (a m) -> p a m", a=DT),
                in_=wq4[h]),
                writes=[("wbuf", par)], fam=f"w{par}")

        def q_proj_part(h, par, qbank, d0, dn):
            qk = (f"pb{qbank}",)
            for dt in range(d0, d0 + dn):
                mm(pb[qbank],
                   wbuf[par][:, dt * 128:(dt + 1) * 128],
                   qslot[:, dt * 512:(dt + 1) * 512],
                   dt == 0, dt == DT - 1,
                   reads=[("wbuf", par), (qslot_name,)],
                   war=[qk] if dt == 0 else (),
                   writes=[qk] if dt == DT - 1 else (),
                   inc=(dt == DT - 1))

        def q_proj(h, par, qbank):
            q_proj_part(h, par, qbank, 0, DT)

        def rope_q(qbank, s):
            qk = (f"pb{qbank}",)
            ctx.comp("act", lambda sc, b=qbank: sc.mul(ksb[0:64, :], pb[b][64:128, :], -1.0),
                     reads=[qk], writes=[("ksb",)], war=[("ksb2",)])
            ctx.comp("act", lambda sc, b=qbank: sc.copy(ksb[64:128, :], pb[b][0:64, :]),
                     reads=[qk], writes=[("ksb2",)])
            ctx.comp("dve", lambda v, b=qbank: v.tensor_tensor(
                t1b[:], pb[b], cosh[:, qcos0:qcos0 + 512], op=ALU.mult),
                reads=[qk, ("cosh",)], writes=[("t1b",)])
            ctx.comp("dve", lambda v: v.tensor_tensor(
                ksb[:], ksb[:], sinh[:, qcos0:qcos0 + 512], op=ALU.mult),
                reads=[("ksb",), ("ksb2",), ("sinh",)],
                writes=[("ksb",), ("ksb2",)])
            ctx.comp("dve", lambda v, s=s: v.tensor_tensor(
                qht[s][:], t1b[:], ksb[:], op=ALU.add),
                reads=[("t1b",), ("ksb",), ("ksb2",)],
                writes=[qhk(s)])

        npair = H // 2

        def q_pair_steps(p):
            if p >= npair:
                return [], []
            steps, tail = [], []
            for i, (h, qbank) in enumerate(((2 * p, 6), (2 * p + 1, 7))):
                par, sq = (2 * p + i) % 3, (2 * p + i) % 4
                for d0 in range(0, DT, 4):
                    steps.append(lambda h=h, par=par, qbank=qbank, d0=d0:
                                 q_proj_part(h, par, qbank, d0, 4))
                tail.append(lambda qbank=qbank, sq=sq: rope_q(qbank, sq))
            if p + 1 < npair:
                tail.append(lambda p=p: (wq_prefetch(2 * p + 2, (2 * p + 2) % 3),
                                         wq_prefetch(2 * p + 3, (2 * p + 3) % 3)))
            return steps, tail

        wq_prefetch(0, 0)
        wq_prefetch(1, 1)
        for p0 in (0, 1):
            a, b = q_pair_steps(p0)
            for f in a + b:
                f()
        for hp in range(npair):
            h0, h1 = 2 * hp, 2 * hp + 1
            g = h0 // 4
            s0q, s1q = (2 * hp) % 4, (2 * hp + 1) % 4
            qsteps, qtail = q_pair_steps(hp + 2)
            qdone = 0
            def av_step(kt):
                kslot = kt % 3
                pk = ("pt", kslot)
                for s in range(2):
                    mm(pb[4 + s],
                       vall[:, kt * 1024 + g * 128: kt * 1024 + g * 128 + 128],
                       ptile[0][:, kslot * 1024 + s * 512: kslot * 1024 + (s + 1) * 512],
                       kt == 0, kt == nkt - 1,
                       reads=[("vall",), pk],
                       war=[(f"pb{4 + s}",)] if kt == 0 else (),
                       writes=[(f"pb{4 + s}",)] if kt == nkt - 1 else (),
                       inc=(kt == nkt - 1))

            for kt in range(nkt):
                kslot = kt % 3
                sb2 = 2 * (kt % 2)
                pslot = ptile[0][:, kslot * 1024:(kslot + 1) * 1024]
                pk = ("pt", kslot)
                for s, sq in ((0, s0q), (1, s1q)):
                    mm(pb[sb2 + s],
                       ktall[:, g * T + kt * 128: g * T + kt * 128 + 128],
                       qht[sq][:],
                       True, True,
                       reads=[("ktall",), qhk(sq)],
                       war=[(f"pb{sb2 + s}",)], writes=[(f"pb{sb2 + s}",)], inc=True)
                pcol = biasv[:, ch * 16 + kt: ch * 16 + kt + 1]
                ctx.comp("act", lambda sc, pslot=pslot, pcol=pcol, kt=kt: sc.activation(
                    pslot, pbS[kt % 2][:], AF.Exp, bias=pcol, scale=SCALE),
                    reads=[(f"pb{sb2}",), (f"pb{sb2 + 1}",), ("biasv",)],
                    writes=[pk])
                if diag0 <= kt < diag0 + 4:
                    ktr = kt - diag0
                    for s in range(2):
                        ctx.comp("dve", lambda v, kslot=kslot, s=s, ktr=ktr: v.tensor_tensor(
                            ptile[0][:, kslot * 1024 + s * 512: kslot * 1024 + (s + 1) * 512],
                            ptile[0][:, kslot * 1024 + s * 512: kslot * 1024 + (s + 1) * 512],
                            tri[:, ktr * 512:(ktr + 1) * 512], op=ALU.mult),
                            reads=[pk, ("tri",)],
                            writes=[pk])
                if kt > 0:
                    av_step(kt - 1)
                want = (len(qsteps) * (kt + 1)) // nkt
                while qdone < want:
                    qsteps[qdone]()
                    qdone += 1
                if kt == nkt - 1:
                    av_step(kt)
                if kt % 2 == 1:
                    sidx = pair_slot[kt // 2]
                    k0, k1 = (kt - 1) % 3, kt % 3
                    ctx.comp("dve", lambda v, sidx=sidx, k0=k0, k1=k1: v.tensor_tensor(
                        pacc[:, sidx * 1024:(sidx + 1) * 1024],
                        ptile[0][:, k0 * 1024:(k0 + 1) * 1024],
                        ptile[0][:, k1 * 1024:(k1 + 1) * 1024], op=ALU.add),
                        reads=[("pt", k0), ("pt", k1)],
                        writes=[("pacc", sidx)])
                    for (a, b2) in folds_after.get(kt, ()):
                        ctx.comp("dve", lambda v, a=a, b2=b2: v.tensor_tensor(
                            pacc[:, a * 1024:(a + 1) * 1024],
                            pacc[:, a * 1024:(a + 1) * 1024],
                            pacc[:, b2 * 1024:(b2 + 1) * 1024], op=ALU.add),
                            reads=[("pacc", a), ("pacc", b2)],
                            writes=[("pacc", a)])
            for f in qtail:
                f()
            for s, h in ((0, h0), (1, h1)):
                lbank = s
                mm(pb[lbank], ones128[:], pacc[:, s * 512:(s + 1) * 512], True, True,
                   reads=[("ones128",), ("pacc", 0)],
                   war=[(f"pb{lbank}",)], writes=[(f"pb{lbank}",)], inc=True)
                ctx.comp("dve", lambda v, lbank=lbank: v.reciprocal(linv[0][:], pb[lbank]),
                         reads=[(f"pb{lbank}",)], writes=[("linv",)])
                oslice = ptile[0][:, dead * 1024 + s * 512: dead * 1024 + (s + 1) * 512]
                ctx.comp("dve", lambda v, s=s, oslice=oslice: v.tensor_tensor(
                    oslice, pb[4 + s], linv[0][:], op=ALU.mult),
                    reads=[(f"pb{4 + s}",), ("linv",)],
                    writes=[("pt", dead)])
                ctx.dma("sp", lambda e, ch=ch, h=h, oslice=oslice: e.dma_start(
                    out=oall[ch, :, h, :], in_=oslice),
                    reads=[("pt", dead)], writes=[("oall", ch)], fam=f"sp{ch}")

    if STOP_PHASE >= 2:
        ln_quarter(0, slotA, "slotA")
    if STOP_PHASE >= 3:
        kv_units(0, slotA, "slotA")
    if STOP_PHASE >= 2:
        ln_quarter(1, slotB, "slotB")
    if STOP_PHASE >= 3:
        kv_units(1, slotB, "slotB")
    if STOP_PHASE >= 5:
        ln_quarter(2, slotA, "slotA")
    if STOP_PHASE >= 4:
        attention(0, slotB, "slotB")
    if STOP_PHASE >= 5:
        load_cos_half(1)
        kv_units(2, slotA, "slotA")
        ln_quarter(3, slotB, "slotB")
        kv_units(3, slotB, "slotB")
        attention(1, slotA, "slotA")

    ctx.fence()
    free("pacc", "linv0", "ptile0", "qh2", "qh1", "qh0",
         "vtb", "t1b", "ksb", "xbuf", "slotB", "slotA", "vall", "ktall",
         "sinh", "cosh", "biasv", "tri", "ones128")

    attnT = [sb(f"attnT{i}", (128, 512), BF16) for i in range(2)]
    x1 = sb("x1", (128, 8 * D), BF16)
    sc2 = sb("sc2", (128, D), F32)
    orhs = sb("orhs", (128, H * 512), BF16)
    atok = sb("atok", (128, 4 * D), BF16)
    xqb = sb("xqb", (128, D), F32)

    def orhs_load(ch):
        ctx.dma("sp", lambda e, ch=ch: e.dma_start(
            out=orhs[:].rearrange("p (a m) -> p a m", a=H),
            in_=oall[ch]),
            reads=[("oall", ch)], writes=[("orhs",)], fam="orhs")

    def wo_prefetch(od, par):
        ctx.dma("pool", lambda e, od=od, par=par: e.dma_start(
            out=wbuf[par][:].rearrange("p (a m) -> p a m", a=DT),
            in_=wo4[od]),
            writes=[("wbuf", par)], fam=f"w{par}")

    def oproj_units(ch):
        for uu in range(3):
            wo_prefetch(uu, uu)
        pending = []
        for od in range(DT):
            par = od % 3
            bank = od % 2
            bk = (f"pb{bank}",)
            ap2 = od % 2
            for ct in range(H):
                mm(pb[bank],
                   wbuf[par][:, ct * 128:(ct + 1) * 128],
                   orhs[:, ct * 512:(ct + 1) * 512],
                   ct == 0, ct == H - 1,
                   reads=[("wbuf", par), ("orhs",)],
                   war=[bk] if ct == 0 else (),
                   writes=[bk] if ct == H - 1 else (),
                   inc=(ct == H - 1))
            if od + 3 < DT:
                wo_prefetch(od + 3, par)
            ctx.comp("act", lambda s, b=bank, ap2=ap2: s.copy(attnT[ap2][:], pb[b]),
                     reads=[bk], writes=[("attnT", ap2)])
            for fl in pending:
                fl()
            pending = [lambda od=od, ap2=ap2: transpose_batch(
                attnT[ap2], ("attnT", ap2), 1,
                lambda g, od=od: atok[:].rearrange(
                    "p (a m) -> p a m", a=4)[:, 0:4, od * 128:(od + 1) * 128],
                ("atok",), evac_eng="dve")]
        for fl in pending:
            fl()

    def ln2_chunk(ch):
        for j in range(4):
            tglob = ch * 4 + j
            ctx.dma("sp", lambda e, tglob=tglob: e.dma_start(
                out=xqb[:], in_=xkv[512 + tglob * 128: 512 + (tglob + 1) * 128, :]),
                writes=[("xqb",)], fam="xq")
            tile = atok[:, j * D:(j + 1) * D]
            ln_stats(tile, [("atok",)])
            ln_normalize(sc2[:], tile, [("atok",)], [("sc2",)])
            ctx.comp("dve", lambda v, tglob=tglob: v.tensor_tensor(
                x1[:, tglob * D:(tglob + 1) * D], sc2[:], xqb[:], op=ALU.add),
                reads=[("sc2",), ("xqb",)], writes=[("x1", tglob)])

    if STOP_PHASE >= 6:
        orhs_load(0)
        oproj_units(0)
        orhs_load(1)
        ln2_chunk(0)
        oproj_units(1)
        ln2_chunk(1)

    ctx.fence()
    free("xqb", "atok", "orhs", "sc2")

    xn3T = [sb("xn3T0", (128, DT * 512), BF16)] * 2
    ftok = sb("ftok", (128, 4 * D), BF16)
    lg4 = sb("lg4", (128, 64), F32)
    er4 = sb("er4", (128, 64), F32)
    rank4 = sb("rank4", (128, 64), F32)
    cmp4 = sb("cmp4", (128, 64), F32)
    sel4 = sb("sel4", (128, 64), F32)
    lr4 = sb("lr4", (128, 8), F32)
    pm4 = sb("pm4", (128, 64), BF16)
    ar15 = sb("ar15", (128, 64), F32)
    pmT = sb("pmT", (16, 512), BF16)
    hw_bf = sb("hw_bf", (128, 512), BF16)
    wr = sb("wr", (128, DT * 16), BF16)
    bt = sb("bt", (128, D), BF16)
    outb = sb("outb", (128, D), F32)

    ctx.dma("pool", lambda e: e.dma_start(
        out=wr[:].rearrange("p (a m) -> p a m", a=DT),
        in_=wrT.rearrange("(a p) o -> p a o", p=128)), writes=[("wr",)], fam="c2")
    ctx.dma("pool", lambda e: e.dma_start(out=bt[:], in_=btL), writes=[("bt",)], fam="c2")
    ctx.dma("sp", lambda e: e.dma_start(out=ar15[:], in_=ar15_in), writes=[("ar15",)], fam="xq")

    def seg(buf, j, w=16):
        return buf[:, j * w:(j + 1) * w]

    def seg3(buf, a, b):
        return buf[:].rearrange("p (a m) -> p a m", a=4)[:, :, a:b]

    def ln3_chunk(ch):
        for j in range(4):
            tglob = ch * 4 + j
            tile = x1[:, tglob * D:(tglob + 1) * D]
            ln_stats(tile, [("x1", tglob)])
            ln_normalize(scratch[:], tile, [("x1", tglob)], [("scratch",)])
            transpose_batch(
                scratch, ("scratch",), 8,
                lambda g, j=j, ch=ch: xn3T[ch][:].rearrange(
                    "p (a m) -> p a m", a=DT)[:, 4 * g:4 * g + 4, j * 128:(j + 1) * 128],
                ("xn3T",))

    def router_chunk(ch):
        xk = ("xn3T",)
        for j in range(4):
            rb_bank = 2 + (j % 2)
            rbk = (f"pb{rb_bank}",)
            for dt in range(DT):
                mm(pb[rb_bank][:, 0:16],
                   xn3T[ch][:, dt * 512 + j * 128: dt * 512 + (j + 1) * 128],
                   wr[:, dt * 16:(dt + 1) * 16],
                   dt == 0, dt == DT - 1,
                   reads=[xk, ("wr",)],
                   war=[rbk] if dt == 0 else (),
                   writes=[rbk] if dt == DT - 1 else (),
                   inc=(dt == DT - 1))
            ctx.comp("dve", lambda v, j=j, b=rb_bank: v.tensor_tensor(
                seg(lg4, j)[:, 0:15], pb[b][:, 0:15], rbb[:, 0:15], op=ALU.add),
                reads=[rbk, ("rbb",)], writes=[("lg4", j)])
            ctx.comp("act", lambda s, j=j: s.activation(
                seg(er4, j)[:, 0:15], seg(lg4, j)[:, 0:15], AF.Exp,
                accum_out=lr4[:, j:j + 1]),
                reads=[("lg4", j)], writes=[("er4", j), ("lr4",)])
        ctx.comp("dve", lambda v: v.tensor_copy(rank4[:], ar15[:]),
                 reads=[("ar15",)], writes=[("rank4",)])
        er_keys = [("er4", j) for j in range(4)]
        for s_ in range(1, 15):
            w = 15 - s_
            ctx.comp("dve", lambda v, s_=s_, w=w: v.tensor_tensor(
                seg3(cmp4, 0, w), seg3(er4, s_, 15), seg3(er4, 0, w), op=ALU.is_gt),
                reads=er_keys, writes=[("cmp4",)])
            ctx.comp("dve", lambda v, w=w: v.tensor_tensor(
                seg3(rank4, 0, w), seg3(rank4, 0, w), seg3(cmp4, 0, w), op=ALU.add),
                reads=[("cmp4",)], writes=[("rank4",)])
            ctx.comp("dve", lambda v, s_=s_, w=w: v.tensor_tensor(
                seg3(rank4, s_, 15), seg3(rank4, s_, 15), seg3(cmp4, 0, w), op=ALU.subtract),
                reads=[("cmp4",)], writes=[("rank4",)])
        ctx.comp("dve", lambda v: v.tensor_single_scalar(
            sel4[:], rank4[:], 7.5, op=ALU.is_lt),
            reads=[("rank4",)], writes=[("sel4",)])
        ctx.comp("dve", lambda v: v.reciprocal(lr4[:, 4:8], lr4[:, 0:4]),
                 reads=[("lr4",)], writes=[("lr4i",)])
        ctx.comp("dve", lambda v: v.memset(seg3(pm4, 15, 16), 0.0), writes=[("pm4z",)])
        for j in range(4):
            ctx.comp("dve", lambda v, j=j: v.tensor_scalar(
                seg(pm4, j)[:, 0:15], seg(er4, j)[:, 0:15], lr4[:, 4 + j:5 + j],
                None, op0=ALU.mult),
                reads=[("er4", j), ("lr4i",)], writes=[("pm4", j, "a")])
            ctx.comp("dve", lambda v, j=j: v.tensor_tensor(
                seg(pm4, j)[:, 0:15], seg(pm4, j)[:, 0:15], seg(sel4, j)[:, 0:15],
                op=ALU.mult),
                reads=[("pm4", j, "a"), ("sel4",)], writes=[("pm4", j)])
            mm(pb[5][0:16, j * 128:(j + 1) * 128], seg(pm4, j), ident[:], True, True,
               reads=[("pm4", j), ("pm4z",), ("ident",)],
               war=[("pb5",)] if j == 0 else (),
               writes=[("pb5",)] if j == 3 else (),
               inc=(j == 3))
        ctx.comp("act", lambda s: s.copy(pmT[:], pb[5][0:16, :]),
                 reads=[("pb5",)], writes=[("pmT",)])
        mm(pb[3], m1[:], pmT[:], True, True,
           reads=[("m1",), ("pmT",)], war=[("pb3",)], writes=[("pb3",)], inc=True)
        ctx.dma("pool", lambda e: e.dma_start(
            out=wbuf[2][:].rearrange("p (a m) -> p a m", a=DT),
            in_=atT.rearrange("(a p) o -> p a o", p=128)),
            writes=[("wbuf", 2)], fam="w2")
        for dt in range(DT):
            mm(pb[4],
               wbuf[2][:, dt * 128:(dt + 1) * 128],
               xn3T[ch][:, dt * 512:(dt + 1) * 512],
               dt == 0, dt == DT - 1,
               reads=[("wbuf", 2), xk],
               war=[("pb4",)] if dt == 0 else (),
               writes=[("pb4",)] if dt == DT - 1 else (),
               inc=(dt == DT - 1))
        ctx.comp("act", lambda s: s.mul(scratch[:, 0:512], pb[4], LORA_SCALING),
                 reads=[("pb4",)], writes=[("scratch",)])
        ctx.comp("dve", lambda v: v.tensor_tensor(
            hw_bf[:], scratch[:, 0:512], pb[3], op=ALU.mult),
            reads=[("scratch",), ("pb3",)], writes=[("hw",)])

    def base_chunk(ch):
        xk = ("xn3T",)

        def wb_prefetch(ot, par):
            ctx.dma("pool", lambda e, ot=ot, par=par: e.dma_start(
                out=wbuf[par][:].rearrange("p (a m) -> p a m", a=DT),
                in_=wb4[ot]),
                writes=[("wbuf", par)], fam=f"w{par}")

        for uu in range(3):
            wb_prefetch(uu, uu)
        pending = []
        for ot in range(DT):
            par = ot % 3
            bank = ot % 2
            bk = (f"pb{bank}",)
            ap2 = ot % 2
            for dt in range(DT):
                mm(pb[bank],
                   wbuf[par][:, dt * 128:(dt + 1) * 128],
                   xn3T[ch][:, dt * 512:(dt + 1) * 512],
                   dt == 0, False,
                   reads=[("wbuf", par), xk],
                   war=[bk] if dt == 0 else (),
                   inc=False)
            mm(pb[bank], bt[:, ot * 128:(ot + 1) * 128], hw_bf[:],
               False, True,
               reads=[("bt",), ("hw",)],
               writes=[bk], inc=True)
            if ot + 3 < DT:
                wb_prefetch(ot + 3, par)
            ctx.comp("act", lambda s, b=bank, ot=ot, ap2=ap2: s.activation(
                attnT[ap2][:], pb[b], AF.Identity, bias=bbs[:, ot:ot + 1]),
                reads=[bk, ("bbs",)], writes=[("attnT", ap2)])
            for fl in pending:
                fl()
            pending = [lambda ot=ot, ap2=ap2: transpose_batch(
                attnT[ap2], ("attnT", ap2), 1,
                lambda g, ot=ot: ftok[:].rearrange(
                    "p (a m) -> p a m", a=4)[:, 0:4, ot * 128:(ot + 1) * 128],
                ("ftok",), evac_eng="dve")]
        for fl in pending:
            fl()

    def ln4_chunk(ch):
        for j in range(4):
            tglob = ch * 4 + j
            tile = ftok[:, j * D:(j + 1) * D]
            ln_stats(tile, [("ftok",)])
            ctx.comp("dve", lambda v, tile=tile, tglob=tglob: v.scalar_tensor_tensor(
                outb[:], tile, stats[:, 6:7], x1[:, tglob * D:(tglob + 1) * D],
                op0=ALU.mult, op1=ALU.add),
                reads=[("ftok",), ("st", 6), ("x1", tglob)], writes=[("outb",)])
            ctx.comp("dve", lambda v: v.tensor_scalar(
                outb[:], outb[:], stats[:, 7:8], None, op0=ALU.subtract),
                reads=[("st", 7), ("outb",)], writes=[("outb",)])
            ctx.dma("sp", lambda e, tglob=tglob: e.dma_start(
                out=out[tglob * 128:(tglob + 1) * 128, :], in_=outb[:]),
                reads=[("outb",)], writes=[("out", tglob)], fam="outq")

    if STOP_PHASE >= 7:
        ln3_chunk(0)
        router_chunk(0)
        base_chunk(0)
        ln3_chunk(1)
        router_chunk(1)
        ln4_chunk(0)
        base_chunk(1)
        ln4_chunk(1)

    ctx.q["sp"].append(lambda eng, v=ctx.cnt["outq"], s=ctx.sems["outq"]: eng.wait_ge(s, v))
    ctx.trace["sp"].append(("w", "outq", ctx.cnt["outq"]))
    ctx.check()

    with nc.Block() as block:
        @block.tensor
        def _(t):
            for f in ctx.q["pe"]:
                f(t)

        @block.scalar
        def _(s):
            for f in ctx.q["act"]:
                f(s)

        @block.vector
        def _(v):
            for f in ctx.q["dve"]:
                f(v)

        @block.sync
        def _(s):
            for f in ctx.q["sp"]:
                f(s)

        @block.gpsimd
        def _(g):
            for f in ctx.q["pool"]:
                f(g)

    for g in list(live.values())[::-1]:
        g.__exit__(None, None, None)
    for g in sem_guards[::-1]:
        g.__exit__(None, None, None)
    return nc



_NC_CACHE = None


def kernel(L0_attn_x, L0_attn_Wq, L0_attn_Wk, L0_attn_Wv, L0_attn_Wo,
           L0_ffn_base_W, L0_ffn_base_b, L0_ffn_router_W, L0_ffn_router_b,
           L0_ffn_A, L0_ffn_B, L0_ffn_expert_map):
    global _NC_CACHE
    x = np.asarray(L0_attn_x, dtype=np.float32)
    wq = np.asarray(L0_attn_Wq, dtype=np.float32)
    wk = np.asarray(L0_attn_Wk, dtype=np.float32)
    wv = np.asarray(L0_attn_Wv, dtype=np.float32)
    wo = np.asarray(L0_attn_Wo, dtype=np.float32)
    wb = np.asarray(L0_ffn_base_W, dtype=np.float32)
    bb = np.asarray(L0_ffn_base_b, dtype=np.float32)
    wr = np.asarray(L0_ffn_router_W, dtype=np.float32)
    rb = np.asarray(L0_ffn_router_b, dtype=np.float32)
    A = np.asarray(L0_ffn_A, dtype=np.float32)
    Bm = np.asarray(L0_ffn_B, dtype=np.float32)
    em = np.asarray(L0_ffn_expert_map, dtype=np.int32)

    cos_tab, sin_tab = _rope_tables()

    def tile4(wT, nu):
        return np.ascontiguousarray(
            wT.reshape(DT, 128, nu, 128).transpose(2, 1, 0, 3))

    shared = {
        "wq4": tile4(wq.T, H),
        "wk4": tile4(wk.T, KVH),
        "wv4": tile4(wv.T, KVH),
        "wo4": tile4(wo.T, DT),
        "wb4": tile4(wb.T, DT),
        "wrT": np.ascontiguousarray(
            np.concatenate([wr.T, np.zeros((D, 1), np.float32)], axis=1)),
        "atT": np.ascontiguousarray(A.reshape(E * R, D).T),
        "btL": np.ascontiguousarray(Bm.transpose(0, 2, 1).reshape(E * R, D)),
        "bbT": np.ascontiguousarray(bb.reshape(DT, 128).T),
        "rb_row": np.ascontiguousarray(rb.reshape(1, 15)),
        "em_row": np.ascontiguousarray(em.reshape(1, 15)),
        "ident_in": np.eye(128, dtype=BF),
        "ones128_in": np.ones((128, 128), dtype=BF),
        "ones1_in": np.ones((1, 128), dtype=np.float32),
        "ar8_in": np.arange(8, dtype=np.float32).reshape(8, 1),
        "expand_in": np.kron(np.eye(E), np.ones((1, R))).astype(BF),
        "ar15_in": np.tile(
            np.concatenate([np.arange(15, dtype=np.float32), [99.0]]),
            (128, 4)).astype(np.float32),
    }
    tri = (np.arange(512)[:, None] <= np.arange(512)[None, :]).astype(BF)
    shared["tri_in"] = np.ascontiguousarray(
        tri.reshape(4, 128, 512).transpose(1, 0, 2).reshape(128, 4 * 512))

    in_maps = []
    for core in range(8):
        b, s = core // 2, core % 2
        perm = _core_perm(s)
        m = dict(shared)
        m["xkv"] = np.ascontiguousarray(x[b][perm])
        m["coskT"] = np.ascontiguousarray(cos_tab[perm].T)
        m["sinkT"] = np.ascontiguousarray(sin_tab[perm].T)
        bv = np.full((128, 32), MASKV, dtype=np.float32)
        for ch in range(2):
            nkt = 8 if ch == 0 else 16
            diag0 = 4 if ch == 0 else 8
            qpos = perm[512 + ch * 512: 512 + (ch + 1) * 512]
            for kt in range(nkt):
                col = ch * 16 + kt
                if diag0 <= kt < diag0 + 4:
                    bv[:, col] = NEGC
                    continue
                kpos = perm[kt * 128:(kt + 1) * 128]
                vis = bool(np.all(kpos[:, None] <= qpos[None, :]))
                novis = bool(np.all(kpos[:, None] > qpos[None, :]))
                assert vis or novis, (core, ch, kt)
                bv[:, col] = NEGC if vis else MASKV
        m["biasv_in"] = bv
        in_maps.append(m)

    if _NC_CACHE is None:
        _NC_CACHE = build_program()
    res = run_bass_kernel_spmd(_NC_CACHE, in_maps, core_ids=list(range(8)))

    outp = np.empty((B, T, D), dtype=np.float32)
    for core in range(8):
        b, s = core // 2, core % 2
        o = res.results[core]["out"]
        if s == 0:
            outp[b, 0:512] = o[0:512]
            outp[b, 1536:2048] = o[512:1024]
        else:
            outp[b, 512:1536] = o
    return outp


# version 15
# speedup vs baseline: 1.1302x; 1.0912x over previous
import math
import numpy as np
import ml_dtypes
import concourse.bass as bass
import concourse.mybir as mybir
from concourse.bass_utils import run_bass_kernel_spmd

F32 = mybir.dt.float32
BF16 = mybir.dt.bfloat16
I32 = mybir.dt.int32
AF = mybir.ActivationFunctionType
ALU = mybir.AluOpType
AX = mybir.AxisListType

BF = ml_dtypes.bfloat16

B, T, D = 4, 2048, 4096
H, KVH, HD = 32, 8, 128
DT = D // 128
NKT = 16
SCALE = 1.0 / math.sqrt(HD)
NEGC = -10.0
MASKV = -30000.0
EPS = 1e-5
E, R = 8, 16
LORA_SCALING = 2.0
STOP_PHASE = 99
KV_KINDS = "kv"
KV_COUNT = 99


def _rope_tables():
    base = 500000.0
    inv_freq = 1.0 / base ** (np.arange(0, HD, 2, dtype=np.float32) / HD)
    factor, low_f, high_f, old_ctx = 32.0, 1.0, 4.0, 8192
    low_wl, high_wl = old_ctx / low_f, old_ctx / high_f
    wavelen = 2 * math.pi / inv_freq
    inv_llama = np.where(wavelen > low_wl, inv_freq / factor, inv_freq)
    smooth = (old_ctx / wavelen - low_f) / (high_f - low_f)
    smoothed = (1 - smooth) * inv_llama / factor + smooth * inv_llama
    is_med = ~(wavelen < high_wl) & ~(wavelen > low_wl)
    inv = np.where(is_med, smoothed, inv_llama).astype(np.float32)
    pos = np.arange(T, dtype=np.float32)
    freqs = pos[:, None] * inv[None, :]
    emb = np.concatenate([freqs, freqs], axis=-1)
    return np.cos(emb).astype(np.float32), np.sin(emb).astype(np.float32)


def _core_perm(s):
    p = np.arange(T)
    if s == 0:
        return np.concatenate([p[512:1024], p[0:512], p[1536:2048], p[1024:1536]])
    return np.concatenate([p[0:512], p[512:1536], p[1536:2048]])


ENGS = ("pe", "act", "dve", "sp", "pool")


class Ctx:

    DMA_FAMS = ("cst", "cs", "xb", "xq", "orhs", "sp0", "sp1", "outq",
                "w0", "w1", "w2", "c2")

    def __init__(self, nc):
        self.nc = nc
        self.q = {e: [] for e in ENGS}
        self.sem_names = ("pe", "act", "dve") + self.DMA_FAMS
        self.sems = {}
        self.cnt = {n: 0 for n in self.sem_names}
        self.hw = {e: {} for e in ENGS}
        self.buf = {}
        self.trace = {e: [] for e in ENGS}

    def make_sems(self, enter):
        for n in self.sem_names:
            self.sems[n] = enter(self.nc.semaphore(n))

    LOG = []

    def _need(self, e, sem, val):
        Ctx.LOG.append((e, "wait", sem, val, self.cnt.get(sem)))
        if sem in self.DMA_FAMS:
            val = self.cnt[sem]
        if self.hw[e].get(sem, 0) >= val:
            return
        self.hw[e][sem] = val
        s = self.sems[sem]
        self.q[e].append(lambda eng, s=s, val=val: eng.wait_ge(s, val))
        self.trace[e].append(("w", sem, val))

    def _entry(self, k):
        if k not in self.buf:
            self.buf[k] = {"w": None, "r": []}
        return self.buf[k]

    def _waits(self, e, reads, writes, war):
        for k in reads:
            ent = self._entry(k)
            if ent["w"]:
                self._need(e, *ent["w"])
            if isinstance(k[0], str) and k[0].startswith("pb"):
                for r in ent["r"]:
                    if r[0] != e:
                        self._need(e, *r)
        for k in tuple(writes) + tuple(war):
            ent = self._entry(k)
            if ent["w"]:
                self._need(e, *ent["w"])
            for r in ent["r"]:
                self._need(e, *r)

    def fence(self):
        Ctx.LOG.append(("ALL", "FENCE", None, None, None))
        """Full barrier: every engine waits everything issued so far. Emitted at
        SBUF free points, where later allocations reuse freed addresses that the
        logical-key tracker cannot see."""
        snap = dict(self.cnt)
        for e in ENGS:
            for sem, val in snap.items():
                if val > 0:
                    self._need(e, sem, val)

    def comp(self, e, fn, reads=(), writes=(), war=(), inc=True):
        Ctx.LOG.append((e, "comp", self.cnt[e] + 1, tuple(writes), tuple(reads)))
        self._waits(e, reads, writes, war)
        sem = e
        val = self.cnt[sem] + 1
        if inc:
            self.cnt[sem] = val
            s = self.sems[sem]
            self.q[e].append(lambda eng, fn=fn, s=s: fn(eng).then_inc(s, 1))
            self.trace[e].append(("i", sem, 1))
        else:
            self.q[e].append(lambda eng, fn=fn: fn(eng))
        for k in reads:
            self._entry(k)["r"].append((sem, val))
        for k in writes:
            ent = self._entry(k)
            ent["w"] = (sem, val)
            ent["r"] = []
        return (sem, val)

    def dma(self, queue, fn, reads=(), writes=(), fam="cst"):
        Ctx.LOG.append((queue, "dma", fam, tuple(writes), tuple(reads)))
        """Emit a DMA on queue 'sp'|'pool' using semaphore family `fam`.

        DMA completions on one queue interleave across in-flight transfers, so
        cumulative counts are only sound per family where buffer reuse is
        WAR-gated (each family = one rotating buffer, or a write-once batch).
        """
        e = queue
        self._waits(e, reads, writes, ())
        sem = fam
        val = self.cnt[sem] + 16
        self.cnt[sem] = val
        s = self.sems[sem]
        self.q[e].append(lambda eng, fn=fn, s=s: fn(eng).then_inc(s, 16))
        self.trace[e].append(("i", sem, 16))
        for k in reads:
            self._entry(k)["r"].append((sem, val))
        for k in writes:
            ent = self._entry(k)
            ent["w"] = (sem, val)
            ent["r"] = []
        return (sem, val)

    def check(self):
        ptr = {e: 0 for e in ENGS}
        cnt = {n: 0 for n in self.sem_names}
        progress = True
        while progress:
            progress = False
            for e in ENGS:
                tr = self.trace[e]
                while ptr[e] < len(tr):
                    kind, sem, val = tr[ptr[e]]
                    if kind == "w" and cnt[sem] < val:
                        break
                    if kind == "i":
                        cnt[sem] += val
                    ptr[e] += 1
                    progress = True
        stuck = {e: (ptr[e], len(self.trace[e]), self.trace[e][ptr[e]])
                 for e in ENGS if ptr[e] < len(self.trace[e])}
        if stuck:
            raise RuntimeError(f"schedule deadlock: {stuck} cnt={cnt}")



def build_program():
    nc = bass.Bass("TRN2", target_bir_lowering=False)

    def din(name, shape, dt=F32):
        return nc.dram_tensor(name, list(shape), dt, kind="ExternalInput").ap()

    xkv = din("xkv", (T, D))
    wq4 = din("wq4", (H, 128, DT, 128))
    wk4 = din("wk4", (KVH, 128, DT, 128))
    wv4 = din("wv4", (KVH, 128, DT, 128))
    wo4 = din("wo4", (DT, 128, DT, 128))
    wb4 = din("wb4", (DT, 128, DT, 128))
    wrT = din("wrT", (D, 16))
    atT = din("atT", (D, 128))
    btL = din("btL", (128, D))
    bbT = din("bbT", (128, DT))
    rb_row = din("rb_row", (1, 15))
    em_row = din("em_row", (1, 15), I32)
    coskT = din("coskT", (128, T))
    sinkT = din("sinkT", (128, T))
    tri_in = din("tri_in", (128, 4 * 512), BF16)
    biasv_in = din("biasv_in", (128, 32))
    ident_in = din("ident_in", (128, 128), BF16)
    ones128_in = din("ones128_in", (128, 128), BF16)
    ones1_in = din("ones1_in", (1, 128))
    ar8_in = din("ar8_in", (8, 1))
    expand_in = din("expand_in", (8, 128), BF16)
    ar15_in = din("ar15_in", (128, 64))

    out = nc.dram_tensor("out", [1024, D], F32, kind="ExternalOutput").ap()
    oall = nc.dram_tensor("oall", [2, 128, H, 512], BF16, kind="Internal").ap()

    live = {}
    done = []

    def enter(g):
        return g.__enter__()

    def sb(name, shape, dt):
        g = nc.sbuf_tensor(name, list(shape), dt)
        live[name] = g
        return g.__enter__()

    def ps(name, shape=(128, 512), dt=F32):
        g = nc.psum_tensor(name, list(shape), dt)
        live[name] = g
        return g.__enter__()

    def free(*names):
        for n in names:
            g = live.pop(n)
            g.__exit__(None, None, None)
            done.append(n)

    ctx = Ctx(nc)
    sem_guards = []
    ctx.make_sems(lambda g: (sem_guards.append(g), g.__enter__())[1])

    ident = sb("ident", (128, 128), BF16)
    stats = sb("stats", (128, 8), F32)
    scratch = sb("scratch", (128, D), BF16)
    wbuf = [sb(f"wbuf{i}", (128, DT * 128), BF16) for i in range(3)]
    m1 = sb("m1", (16, 128), BF16)
    rbb = sb("rbb", (128, 16), F32)
    bbs = sb("bbs", (128, DT), F32)
    epsb = sb("epsb", (128, 1), F32)

    r_misc = sb("r_misc", (16, 64), F32)
    ones1 = sb("ones1", (1, 128), F32)
    ar8 = sb("ar8", (8, 1), F32)
    expand = sb("expand", (8, 128), BF16)
    emi = sb("emi", (1, 15), I32)
    rb_sb = sb("rb_sb", (1, 15), F32)
    ohT_bf = sb("ohT_bf", (8, 15), BF16)

    pbA = ps("pbA", (128, 1024))
    pbB = ps("pbB", (128, 1024))
    pb47 = [ps(f"pbt{i}") for i in range(4, 8)]
    pb = [pbA[:, 0:512], pbA[:, 512:1024], pbB[:, 0:512], pbB[:, 512:1024],
          pb47[0][:], pb47[1][:], pb47[2][:], pb47[3][:]]
    pbS = [pbA, pbB]

    def mm(dst_ap, lhs_ap, rhs_ap, start, stop, reads, writes=(), war=(), inc=False):
        return ctx.comp(
            "pe",
            lambda t, d=dst_ap, l=lhs_ap, r=rhs_ap, s1=start, s2=stop: t.matmul(
                d, l, r, start=s1, stop=s2),
            reads=reads, writes=writes, war=war, inc=inc)

    ctx.dma("sp", lambda e: e.dma_start(out=ident[:], in_=ident_in), writes=[("ident",)])
    ctx.dma("sp", lambda e: e.dma_start(out=bbs[:], in_=bbT), writes=[("bbs",)])
    ctx.dma("sp", lambda e: e.dma_start(out=ones1[:], in_=ones1_in), writes=[("ones1",)])
    ctx.dma("sp", lambda e: e.dma_start(out=ar8[:], in_=ar8_in), writes=[("ar8",)])
    ctx.dma("sp", lambda e: e.dma_start(out=expand[:], in_=expand_in), writes=[("expand",)])
    ctx.dma("sp", lambda e: e.dma_start(out=emi[:], in_=em_row), writes=[("emi",)])
    ctx.dma("sp", lambda e: e.dma_start(out=rb_sb[:], in_=rb_row), writes=[("rb_sb",)])

    ctx.comp("dve", lambda v: v.memset(epsb[:], EPS), writes=[("epsb",)])

    mm(pb[2][:, 0:15], ones1[:, 0:128], rb_sb[:], True, True,
       reads=[("ones1",), ("rb_sb",)], war=[("pb2",)], writes=[("pb2",)], inc=True)
    ctx.comp("act", lambda s: s.copy(rbb[:, 0:15], pb[2][:, 0:15]),
             reads=[("pb2",)], writes=[("rbb",)])
    ctx.comp("dve", lambda v: v.tensor_copy(r_misc[0:1, 0:15], emi[:]),
             reads=[("emi",)], writes=[("emf",)])
    mm(pb[3][0:8, 0:15], ones1[:, 0:8], r_misc[0:1, 0:15], True, True,
       reads=[("emf",), ("ones1",)], war=[("pb3",)], writes=[("pb3",)], inc=True)
    ctx.comp("act", lambda s: s.copy(r_misc[0:8, 16:31], pb[3][0:8, 0:15]),
             reads=[("pb3",)], writes=[("emb8",)])
    ctx.comp("dve", lambda v: v.tensor_scalar(
        r_misc[0:8, 32:47], r_misc[0:8, 16:31], ar8[:, 0:1], None, op0=ALU.is_equal),
        reads=[("emb8",), ("ar8",)], writes=[("ohT",)])
    ctx.comp("dve", lambda v: v.tensor_copy(ohT_bf[:], r_misc[0:8, 32:47]),
             reads=[("ohT",)], writes=[("ohT_bf",)])
    mm(pb[4][0:15, 0:128], ohT_bf[:], expand[:], True, True,
       reads=[("ohT_bf",), ("expand",)], war=[("pb4",)], writes=[("pb4",)], inc=True)
    ctx.comp("dve", lambda v: v.memset(m1[:], 0.0), writes=[("m1",)])
    ctx.comp("act", lambda s: s.copy(m1[0:15, :], pb[4][0:15, 0:128]),
             reads=[("pb4",)], writes=[("m1",)])
    ctx.fence()
    free("ohT_bf", "rb_sb", "emi", "expand", "ar8", "ones1", "r_misc")

    ones128 = sb("ones128", (128, 128), BF16)
    tri = sb("tri", (128, 4 * 512), BF16)
    biasv = sb("biasv", (128, 32), F32)
    cosh = sb("cosh", (128, 1024), BF16)
    sinh = sb("sinh", (128, 1024), BF16)
    ktall = sb("ktall", (128, KVH * T), BF16)
    vall = sb("vall", (128, NKT * 1024), BF16)
    slotA = sb("slotA", (128, DT * 512), BF16)
    slotB = sb("slotB", (128, DT * 512), BF16)
    xbuf = sb("xbuf", (128, D), F32)
    ksb = sb("ksb", (128, 512), F32)
    t1b = sb("t1b", (128, 512), F32)
    vtb = sb("vtb", (128, 512), BF16)
    qh = [sb(f"qh{i}", (128, 512), BF16) for i in range(3)]
    ptile = [sb("ptile0", (128, 6 * 512), BF16)] * 2
    linv = sb("linv", (128, 512), F32)
    pacc = sb("pacc", (128, 6 * 512), BF16)
    ctx.dma("sp", lambda e: e.dma_start(out=ones128[:], in_=ones128_in), writes=[("ones128",)])
    ctx.dma("sp", lambda e: e.dma_start(out=tri[:], in_=tri_in), writes=[("tri",)])
    ctx.dma("sp", lambda e: e.dma_start(out=biasv[:], in_=biasv_in), writes=[("biasv",)])
    def load_cos_half(hf):
        ctx.dma("pool", lambda e, hf=hf: e.dma_start(
            out=cosh[:], in_=coskT[:, hf * 1024:(hf + 1) * 1024]), writes=[("cosh",)], fam="cs")
        ctx.dma("pool", lambda e, hf=hf: e.dma_start(
            out=sinh[:], in_=sinkT[:, hf * 1024:(hf + 1) * 1024]), writes=[("sinh",)], fam="cs")

    load_cos_half(0)

    def ln_stats(src_ap, src_keys):
        stc = lambda c: stats[:, c:c + 1]
        K = lambda c: ("st", c)
        ctx.comp("dve", lambda v: v.reduce_sum(stc(0), src_ap, axis=AX.X),
                 reads=list(src_keys), writes=[K(0)])
        ctx.comp("act", lambda s: s.activation(
            scratch[:], src_ap, AF.Square, accum_out=stc(1)),
            reads=list(src_keys), writes=[("scratch",), K(1)])
        ctx.comp("dve", lambda v: v.tensor_scalar_mul(stc(2), stc(0), 1.0 / D),
                 reads=[K(0)], writes=[K(2)])
        ctx.comp("dve", lambda v: v.tensor_tensor(stc(3), stc(2), stc(2), op=ALU.mult),
                 reads=[K(2)], writes=[K(3)])
        ctx.comp("dve", lambda v: v.tensor_scalar(
            stc(4), stc(1), 1.0 / D, None, op0=ALU.mult),
            reads=[K(1)], writes=[K(4)])
        ctx.comp("dve", lambda v: v.tensor_tensor(stc(4), stc(4), stc(3), op=ALU.subtract),
                 reads=[K(4), K(3)], writes=[K(4)])
        ctx.comp("act", lambda s: s.activation(stc(5), stc(4), AF.Ln, bias=epsb[:, 0:1]),
                 reads=[K(4), ("epsb",)], writes=[K(5)])
        ctx.comp("act", lambda s: s.activation(stc(6), stc(5), AF.Exp, scale=-0.5),
                 reads=[K(5)], writes=[K(6)])
        ctx.comp("dve", lambda v: v.tensor_tensor(stc(7), stc(2), stc(6), op=ALU.mult),
                 reads=[K(2), K(6)], writes=[K(7)])
        return stc(6), stc(7)

    def ln_normalize(dst_ap, src_ap, src_keys, dst_keys):
        rstd, nmr = stats[:, 6:7], stats[:, 7:8]
        ctx.comp("dve", lambda v: v.tensor_scalar(
            dst_ap, src_ap, rstd, nmr, op0=ALU.mult, op1=ALU.subtract),
            reads=list(src_keys) + [("st", 6), ("st", 7)], writes=list(dst_keys))

    def transpose_batch(src, src_key, n_groups, dest_fn, dest_key, evac_eng="act"):
        for g in range(n_groups):
            for j in range(4):
                idx = 4 * g + j
                last = (j == 3)
                mm(pb[6][:, j * 128:(j + 1) * 128],
                   src[:, idx * 128:(idx + 1) * 128], ident[:],
                   True, True,
                   reads=[src_key, ("ident",)],
                   war=[("pb6",)] if j == 0 else (),
                   writes=[("pb6",)] if last else (),
                   inc=last)
            if evac_eng == "act":
                ctx.comp("act", lambda e, g=g: e.copy(dest_fn(g), pb[6]),
                         reads=[("pb6",)], writes=[dest_key])
            else:
                ctx.comp("dve", lambda e, g=g: e.tensor_copy(dest_fn(g), pb[6]),
                         reads=[("pb6",)], writes=[dest_key])

    def ln_quarter(qi, slot, slot_name):
        for tl in range(4):
            t = qi * 4 + tl
            ctx.dma("sp", lambda e, t=t: e.dma_start(
                out=xbuf[:], in_=xkv[t * 128:(t + 1) * 128, :]), writes=[("xbuf",)], fam="xb")
            ln_stats(xbuf[:], [("xbuf",)])
            ln_normalize(scratch[:], xbuf[:], [("xbuf",)], [("scratch",)])
            transpose_batch(
                scratch, ("scratch",), 8,
                lambda g, tl=tl, slot=slot: slot[:].rearrange(
                    "p (a m) -> p a m", a=DT)[:, 4 * g:4 * g + 4, tl * 128:(tl + 1) * 128],
                (slot_name,))

    def kv_units(c, slot, slot_name):
        units = [("k", i) for i in range(KVH) if "k" in KV_KINDS] + \
                [("v", i) for i in range(KVH) if "v" in KV_KINDS]
        units = units[:KV_COUNT]

        def prefetch(u, par):
            kind, i = units[u]
            w = wk4 if kind == "k" else wv4
            ctx.dma("pool", lambda e, w=w, i=i, par=par: e.dma_start(
                out=wbuf[par][:].rearrange("p (a m) -> p a m", a=DT),
                in_=w[i]),
                writes=[("wbuf", par)], fam=f"w{par}")

        for uu in range(min(3, len(units))):
            prefetch(uu, uu % 3)
        co = c % 2
        for u, (kind, i) in enumerate(units):
            par = u % 3
            bank = 2 * (u % 2) + co
            bk = (f"pb{bank}",)
            for dt in range(DT):
                mm(pb[bank],
                   wbuf[par][:, dt * 128:(dt + 1) * 128],
                   slot[:, dt * 512:(dt + 1) * 512],
                   dt == 0, dt == DT - 1,
                   reads=[("wbuf", par), (slot_name,)],
                   war=[bk] if dt == 0 else (),
                   writes=[bk] if dt == DT - 1 else (),
                   inc=(dt == DT - 1))
            if kind == "k":
                ctx.comp("act", lambda s, b=bank: s.mul(ksb[0:64, :], pb[b][64:128, :], -1.0),
                         reads=[bk], writes=[("ksb",)], war=[("ksb2",)])
                ctx.comp("act", lambda s, b=bank: s.copy(ksb[64:128, :], pb[b][0:64, :]),
                         reads=[bk], writes=[("ksb2",)])
                ctx.comp("dve", lambda v, b=bank, co=co: v.tensor_tensor(
                    t1b[:], pb[b], cosh[:, co * 512:(co + 1) * 512], op=ALU.mult),
                    reads=[bk, ("cosh",)], writes=[("t1b",)])
                ctx.comp("dve", lambda v, co=co: v.tensor_tensor(
                    ksb[:], ksb[:], sinh[:, co * 512:(co + 1) * 512], op=ALU.mult),
                    reads=[("ksb",), ("ksb2",), ("sinh",)],
                    writes=[("ksb",), ("ksb2",)])
                dst = ktall[:, i * T + c * 512: i * T + (c + 1) * 512]
                ctx.comp("dve", lambda v, dst=dst: v.tensor_tensor(
                    dst, t1b[:], ksb[:], op=ALU.add),
                    reads=[("t1b",), ("ksb",), ("ksb2",)],
                    writes=[("ktall",)])
            else:
                ctx.comp("act", lambda s, b=bank: s.copy(vtb[:], pb[b]),
                         reads=[bk], writes=[("vtb",)])
                transpose_batch(
                    vtb, ("vtb",), 1,
                    lambda g, c=c, i=i: vall[:].rearrange(
                        "p (a m) -> p a m", a=NKT)[:, 4 * c:4 * c + 4, i * 128:(i + 1) * 128],
                    ("vall",))
            if u + 3 < len(units):
                prefetch(u + 3, par)

    def attention(ch, qslot, qslot_name):
        qht = qh + [vtb]
        qhk = lambda i: ("qh", i) if i < 3 else ("vtb",)
        nkt = 8 if ch == 0 else 16
        diag0 = 4 if ch == 0 else 8
        qcos0 = 512 if ch == 0 else 0
        pair_slot = [0] + [1] * (nkt // 2 - 1)
        folds_after = {kt: [(0, 1)] for kt in range(3, nkt, 2)}

        def wq_prefetch(h, par):
            ctx.dma("pool", lambda e, h=h, par=par: e.dma_start(
                out=wbuf[par][:].rearrange("p (a m) -> p a m", a=DT),
                in_=wq4[h]),
                writes=[("wbuf", par)], fam=f"w{par}")

        def q_proj_part(h, par, qbank, d0, dn):
            qk = (f"pb{qbank}",)
            for dt in range(d0, d0 + dn):
                mm(pb[qbank],
                   wbuf[par][:, dt * 128:(dt + 1) * 128],
                   qslot[:, dt * 512:(dt + 1) * 512],
                   dt == 0, dt == DT - 1,
                   reads=[("wbuf", par), (qslot_name,)],
                   war=[qk] if dt == 0 else (),
                   writes=[qk] if dt == DT - 1 else (),
                   inc=(dt == DT - 1))

        def q_proj(h, par, qbank):
            q_proj_part(h, par, qbank, 0, DT)

        def rope_q(qbank, s):
            qk = (f"pb{qbank}",)
            ctx.comp("act", lambda sc, b=qbank: sc.mul(ksb[0:64, :], pb[b][64:128, :], -1.0),
                     reads=[qk], writes=[("ksb",)], war=[("ksb2",)])
            ctx.comp("act", lambda sc, b=qbank: sc.copy(ksb[64:128, :], pb[b][0:64, :]),
                     reads=[qk], writes=[("ksb2",)])
            ctx.comp("dve", lambda v, b=qbank: v.tensor_tensor(
                t1b[:], pb[b], cosh[:, qcos0:qcos0 + 512], op=ALU.mult),
                reads=[qk, ("cosh",)], writes=[("t1b",)])
            ctx.comp("dve", lambda v: v.tensor_tensor(
                ksb[:], ksb[:], sinh[:, qcos0:qcos0 + 512], op=ALU.mult),
                reads=[("ksb",), ("ksb2",), ("sinh",)],
                writes=[("ksb",), ("ksb2",)])
            ctx.comp("dve", lambda v, s=s: v.tensor_tensor(
                qht[s][:], t1b[:], ksb[:], op=ALU.add),
                reads=[("t1b",), ("ksb",), ("ksb2",)],
                writes=[qhk(s)])

        npair = H // 2

        def q_pair_steps(p):
            if p >= npair:
                return [], []
            steps, tail = [], []
            for i, (h, qbank) in enumerate(((2 * p, 6), (2 * p + 1, 7))):
                par, sq = (2 * p + i) % 3, (2 * p + i) % 4
                for d0 in range(0, DT, 4):
                    steps.append(lambda h=h, par=par, qbank=qbank, d0=d0:
                                 q_proj_part(h, par, qbank, d0, 4))
                tail.append(lambda qbank=qbank, sq=sq: rope_q(qbank, sq))
            if p + 1 < npair:
                tail.append(lambda p=p: (wq_prefetch(2 * p + 2, (2 * p + 2) % 3),
                                         wq_prefetch(2 * p + 3, (2 * p + 3) % 3)))
            return steps, tail

        def pair_tail_steps(hp):
            pi = hp % 2
            a = pi
            h0, h1 = 2 * hp, 2 * hp + 1

            def l_mms():
                for s in range(2):
                    mm(pb[s], ones128[:],
                       pacc[:, a * 1024 + s * 512: a * 1024 + (s + 1) * 512],
                       True, True,
                       reads=[("ones128",), ("pacc", a)],
                       war=[(f"pb{s}",)], writes=[(f"pb{s}",)], inc=True)


            def ln_l():
                ctx.comp("act", lambda sc: sc.activation(t1b[:], pb[0], AF.Ln),
                         reads=[("pb0",)], writes=[("t1b",)])
                ctx.comp("act", lambda sc: sc.activation(ksb[:], pb[1], AF.Ln),
                         reads=[("pb1",)], writes=[("ksb",), ("ksb2",)])

            def exp_evac():
                for s, h, srck in ((0, h0, [("t1b",)]), (1, h1, [("ksb",), ("ksb2",)])):
                    src_ap = t1b if s == 0 else ksb
                    ctx.comp("act", lambda sc, src_ap=src_ap: sc.activation(
                        linv[:], src_ap[:], AF.Exp, scale=-1.0),
                        reads=srck, writes=[("linv",)])
                    sq = (2 * hp + s) % 4
                    osl = qht[sq][:]
                    ctx.comp("dve", lambda v, s=s, osl=osl: v.tensor_tensor(
                        osl, pb[4 + s], linv[:], op=ALU.mult),
                        reads=[(f"pb{4 + s}",), ("linv",)],
                        writes=[qhk(sq)])
                    ctx.dma("sp", lambda e, h=h, osl=osl: e.dma_start(
                        out=oall[ch, :, h, :], in_=osl),
                        reads=[qhk(sq)], writes=[("oall", ch)], fam=f"sp{ch}")

            return [l_mms, ln_l, exp_evac]

        wq_prefetch(0, 0)
        wq_prefetch(1, 1)
        for p0 in (0, 1):
            a, b = q_pair_steps(p0)
            for f in a + b:
                f()
        carry = []
        for hp in range(npair):
            h0, h1 = 2 * hp, 2 * hp + 1
            g = h0 // 4
            s0q, s1q = (2 * hp) % 4, (2 * hp + 1) % 4
            qsteps, qtail = q_pair_steps(hp + 2)
            qdone = 0
            acur = hp % 2
            def av_step(kt):
                kslot = kt % 3
                pk = ("pt", kslot)
                for s in range(2):
                    mm(pb[4 + s],
                       vall[:, kt * 1024 + g * 128: kt * 1024 + g * 128 + 128],
                       ptile[0][:, kslot * 1024 + s * 512: kslot * 1024 + (s + 1) * 512],
                       kt == 0, kt == nkt - 1,
                       reads=[("vall",), pk],
                       war=[(f"pb{4 + s}",)] if kt == 0 else (),
                       writes=[(f"pb{4 + s}",)] if kt == nkt - 1 else (),
                       inc=(kt == nkt - 1))

            for kt in range(nkt):
                kslot = kt % 3
                sb2 = 2 * (kt % 2)
                pslot = ptile[0][:, kslot * 1024:(kslot + 1) * 1024]
                pk = ("pt", kslot)
                for s, sq in ((0, s0q), (1, s1q)):
                    mm(pb[sb2 + s],
                       ktall[:, g * T + kt * 128: g * T + kt * 128 + 128],
                       qht[sq][:],
                       True, True,
                       reads=[("ktall",), qhk(sq)],
                       war=[(f"pb{sb2 + s}",)], writes=[(f"pb{sb2 + s}",)], inc=True)
                pcol = biasv[:, ch * 16 + kt: ch * 16 + kt + 1]
                ctx.comp("act", lambda sc, pslot=pslot, pcol=pcol, kt=kt: sc.activation(
                    pslot, pbS[kt % 2][:], AF.Exp, bias=pcol, scale=SCALE),
                    reads=[(f"pb{sb2}",), (f"pb{sb2 + 1}",), ("biasv",)],
                    writes=[pk])
                if diag0 <= kt < diag0 + 4:
                    ktr = kt - diag0
                    for s in range(2):
                        ctx.comp("dve", lambda v, kslot=kslot, s=s, ktr=ktr: v.tensor_tensor(
                            ptile[0][:, kslot * 1024 + s * 512: kslot * 1024 + (s + 1) * 512],
                            ptile[0][:, kslot * 1024 + s * 512: kslot * 1024 + (s + 1) * 512],
                            tri[:, ktr * 512:(ktr + 1) * 512], op=ALU.mult),
                            reads=[pk, ("tri",)],
                            writes=[pk])
                if kt == 1:
                    for f in carry[0:3]:
                        f()
                elif kt == 2:
                    for f in carry[3:4]:
                        f()
                elif kt == 3:
                    for f in carry[4:]:
                        f()
                if kt > 1:
                    av_step(kt - 2)
                if kt >= 4:
                    want = (len(qsteps) * (kt - 3)) // (nkt - 4)
                    while qdone < want:
                        qsteps[qdone]()
                        qdone += 1
                if kt % 2 == 1:
                    sidx = acur if pair_slot[kt // 2] == 0 else 2
                    k0, k1 = (kt - 1) % 3, kt % 3
                    ctx.comp("dve", lambda v, sidx=sidx, k0=k0, k1=k1: v.tensor_tensor(
                        pacc[:, sidx * 1024:(sidx + 1) * 1024],
                        ptile[0][:, k0 * 1024:(k0 + 1) * 1024],
                        ptile[0][:, k1 * 1024:(k1 + 1) * 1024], op=ALU.add),
                        reads=[("pt", k0), ("pt", k1)],
                        writes=[("pacc", sidx)])
                    if kt in folds_after:
                        ctx.comp("dve", lambda v, acur=acur: v.tensor_tensor(
                            pacc[:, acur * 1024:(acur + 1) * 1024],
                            pacc[:, acur * 1024:(acur + 1) * 1024],
                            pacc[:, 2 * 1024:3 * 1024], op=ALU.add),
                            reads=[("pacc", acur), ("pacc", 2)],
                            writes=[("pacc", acur)])
            av_step(nkt - 2)
            av_step(nkt - 1)
            carry = pair_tail_steps(hp) + qtail
        for f in carry:
            f()

    if STOP_PHASE >= 2:
        ln_quarter(0, slotA, "slotA")
    if STOP_PHASE >= 3:
        kv_units(0, slotA, "slotA")
    if STOP_PHASE >= 2:
        ln_quarter(1, slotB, "slotB")
    if STOP_PHASE >= 3:
        kv_units(1, slotB, "slotB")
    if STOP_PHASE >= 5:
        ln_quarter(2, slotA, "slotA")
    if STOP_PHASE >= 4:
        attention(0, slotB, "slotB")
    if STOP_PHASE >= 5:
        load_cos_half(1)
        kv_units(2, slotA, "slotA")
        ln_quarter(3, slotB, "slotB")
        kv_units(3, slotB, "slotB")
        attention(1, slotA, "slotA")

    ctx.fence()
    free("pacc", "linv", "ptile0", "qh2", "qh1", "qh0",
         "vtb", "t1b", "ksb", "xbuf", "slotB", "slotA", "vall", "ktall",
         "sinh", "cosh", "biasv", "tri", "ones128")

    attnT = [sb(f"attnT{i}", (128, 512), BF16) for i in range(2)]
    x1 = sb("x1", (128, 8 * D), BF16)
    sc2 = sb("sc2", (128, D), F32)
    orhs = sb("orhs", (128, H * 512), BF16)
    atok = sb("atok", (128, 4 * D), BF16)
    xqb = sb("xqb", (128, D), F32)

    def orhs_load(ch):
        ctx.dma("sp", lambda e, ch=ch: e.dma_start(
            out=orhs[:].rearrange("p (a m) -> p a m", a=H),
            in_=oall[ch]),
            reads=[("oall", ch)], writes=[("orhs",)], fam="orhs")

    def wo_prefetch(od, par):
        ctx.dma("pool", lambda e, od=od, par=par: e.dma_start(
            out=wbuf[par][:].rearrange("p (a m) -> p a m", a=DT),
            in_=wo4[od]),
            writes=[("wbuf", par)], fam=f"w{par}")

    def oproj_units(ch):
        for uu in range(3):
            wo_prefetch(uu, uu)
        pending = []
        for od in range(DT):
            par = od % 3
            bank = od % 2
            bk = (f"pb{bank}",)
            ap2 = od % 2
            for ct in range(H):
                mm(pb[bank],
                   wbuf[par][:, ct * 128:(ct + 1) * 128],
                   orhs[:, ct * 512:(ct + 1) * 512],
                   ct == 0, ct == H - 1,
                   reads=[("wbuf", par), ("orhs",)],
                   war=[bk] if ct == 0 else (),
                   writes=[bk] if ct == H - 1 else (),
                   inc=(ct == H - 1))
            if od + 3 < DT:
                wo_prefetch(od + 3, par)
            ctx.comp("act", lambda s, b=bank, ap2=ap2: s.copy(attnT[ap2][:], pb[b]),
                     reads=[bk], writes=[("attnT", ap2)])
            for fl in pending:
                fl()
            pending = [lambda od=od, ap2=ap2: transpose_batch(
                attnT[ap2], ("attnT", ap2), 1,
                lambda g, od=od: atok[:].rearrange(
                    "p (a m) -> p a m", a=4)[:, 0:4, od * 128:(od + 1) * 128],
                ("atok",), evac_eng="dve")]
        for fl in pending:
            fl()

    def ln2_chunk(ch):
        for j in range(4):
            tglob = ch * 4 + j
            ctx.dma("sp", lambda e, tglob=tglob: e.dma_start(
                out=xqb[:], in_=xkv[512 + tglob * 128: 512 + (tglob + 1) * 128, :]),
                writes=[("xqb",)], fam="xq")
            tile = atok[:, j * D:(j + 1) * D]
            ln_stats(tile, [("atok",)])
            ln_normalize(sc2[:], tile, [("atok",)], [("sc2",)])
            ctx.comp("dve", lambda v, tglob=tglob: v.tensor_tensor(
                x1[:, tglob * D:(tglob + 1) * D], sc2[:], xqb[:], op=ALU.add),
                reads=[("sc2",), ("xqb",)], writes=[("x1", tglob)])

    if STOP_PHASE >= 6:
        orhs_load(0)
        oproj_units(0)
        orhs_load(1)
        ln2_chunk(0)
        oproj_units(1)
        ln2_chunk(1)

    ctx.fence()
    free("xqb", "atok", "orhs", "sc2")

    xn3T = [sb("xn3T0", (128, DT * 512), BF16)] * 2
    ftok = sb("ftok", (128, 4 * D), BF16)
    lg4 = sb("lg4", (128, 64), F32)
    er4 = sb("er4", (128, 64), F32)
    rank4 = sb("rank4", (128, 64), F32)
    cmp4 = sb("cmp4", (128, 64), F32)
    sel4 = sb("sel4", (128, 64), F32)
    lr4 = sb("lr4", (128, 8), F32)
    pm4 = sb("pm4", (128, 64), BF16)
    ar15 = sb("ar15", (128, 64), F32)
    pmT = sb("pmT", (16, 512), BF16)
    hw_bf = sb("hw_bf", (128, 512), BF16)
    wr = sb("wr", (128, DT * 16), BF16)
    bt = sb("bt", (128, D), BF16)
    outb = sb("outb", (128, D), F32)

    ctx.dma("pool", lambda e: e.dma_start(
        out=wr[:].rearrange("p (a m) -> p a m", a=DT),
        in_=wrT.rearrange("(a p) o -> p a o", p=128)), writes=[("wr",)], fam="c2")
    ctx.dma("pool", lambda e: e.dma_start(out=bt[:], in_=btL), writes=[("bt",)], fam="c2")
    ctx.dma("sp", lambda e: e.dma_start(out=ar15[:], in_=ar15_in), writes=[("ar15",)], fam="xq")

    def seg(buf, j, w=16):
        return buf[:, j * w:(j + 1) * w]

    def seg3(buf, a, b):
        return buf[:].rearrange("p (a m) -> p a m", a=4)[:, :, a:b]

    def ln3_chunk(ch):
        for j in range(4):
            tglob = ch * 4 + j
            tile = x1[:, tglob * D:(tglob + 1) * D]
            ln_stats(tile, [("x1", tglob)])
            ln_normalize(scratch[:], tile, [("x1", tglob)], [("scratch",)])
            transpose_batch(
                scratch, ("scratch",), 8,
                lambda g, j=j, ch=ch: xn3T[ch][:].rearrange(
                    "p (a m) -> p a m", a=DT)[:, 4 * g:4 * g + 4, j * 128:(j + 1) * 128],
                ("xn3T",))

    def router_chunk(ch):
        xk = ("xn3T",)
        for j in range(4):
            rb_bank = 2 + (j % 2)
            rbk = (f"pb{rb_bank}",)
            for dt in range(DT):
                mm(pb[rb_bank][:, 0:16],
                   xn3T[ch][:, dt * 512 + j * 128: dt * 512 + (j + 1) * 128],
                   wr[:, dt * 16:(dt + 1) * 16],
                   dt == 0, dt == DT - 1,
                   reads=[xk, ("wr",)],
                   war=[rbk] if dt == 0 else (),
                   writes=[rbk] if dt == DT - 1 else (),
                   inc=(dt == DT - 1))
            ctx.comp("dve", lambda v, j=j, b=rb_bank: v.tensor_tensor(
                seg(lg4, j)[:, 0:15], pb[b][:, 0:15], rbb[:, 0:15], op=ALU.add),
                reads=[rbk, ("rbb",)], writes=[("lg4", j)])
            ctx.comp("act", lambda s, j=j: s.activation(
                seg(er4, j)[:, 0:15], seg(lg4, j)[:, 0:15], AF.Exp,
                accum_out=lr4[:, j:j + 1]),
                reads=[("lg4", j)], writes=[("er4", j), ("lr4",)])
        ctx.comp("dve", lambda v: v.tensor_copy(rank4[:], ar15[:]),
                 reads=[("ar15",)], writes=[("rank4",)])
        er_keys = [("er4", j) for j in range(4)]
        for s_ in range(1, 15):
            w = 15 - s_
            ctx.comp("dve", lambda v, s_=s_, w=w: v.tensor_tensor(
                seg3(cmp4, 0, w), seg3(er4, s_, 15), seg3(er4, 0, w), op=ALU.is_gt),
                reads=er_keys, writes=[("cmp4",)])
            ctx.comp("dve", lambda v, w=w: v.tensor_tensor(
                seg3(rank4, 0, w), seg3(rank4, 0, w), seg3(cmp4, 0, w), op=ALU.add),
                reads=[("cmp4",)], writes=[("rank4",)])
            ctx.comp("dve", lambda v, s_=s_, w=w: v.tensor_tensor(
                seg3(rank4, s_, 15), seg3(rank4, s_, 15), seg3(cmp4, 0, w), op=ALU.subtract),
                reads=[("cmp4",)], writes=[("rank4",)])
        ctx.comp("dve", lambda v: v.tensor_single_scalar(
            sel4[:], rank4[:], 7.5, op=ALU.is_lt),
            reads=[("rank4",)], writes=[("sel4",)])
        ctx.comp("dve", lambda v: v.reciprocal(lr4[:, 4:8], lr4[:, 0:4]),
                 reads=[("lr4",)], writes=[("lr4i",)])
        ctx.comp("dve", lambda v: v.memset(seg3(pm4, 15, 16), 0.0), writes=[("pm4z",)])
        for j in range(4):
            ctx.comp("dve", lambda v, j=j: v.tensor_scalar(
                seg(pm4, j)[:, 0:15], seg(er4, j)[:, 0:15], lr4[:, 4 + j:5 + j],
                None, op0=ALU.mult),
                reads=[("er4", j), ("lr4i",)], writes=[("pm4", j, "a")])
            ctx.comp("dve", lambda v, j=j: v.tensor_tensor(
                seg(pm4, j)[:, 0:15], seg(pm4, j)[:, 0:15], seg(sel4, j)[:, 0:15],
                op=ALU.mult),
                reads=[("pm4", j, "a"), ("sel4",)], writes=[("pm4", j)])
            mm(pb[5][0:16, j * 128:(j + 1) * 128], seg(pm4, j), ident[:], True, True,
               reads=[("pm4", j), ("pm4z",), ("ident",)],
               war=[("pb5",)] if j == 0 else (),
               writes=[("pb5",)] if j == 3 else (),
               inc=(j == 3))
        ctx.comp("act", lambda s: s.copy(pmT[:], pb[5][0:16, :]),
                 reads=[("pb5",)], writes=[("pmT",)])
        mm(pb[3], m1[:], pmT[:], True, True,
           reads=[("m1",), ("pmT",)], war=[("pb3",)], writes=[("pb3",)], inc=True)
        ctx.dma("pool", lambda e: e.dma_start(
            out=wbuf[2][:].rearrange("p (a m) -> p a m", a=DT),
            in_=atT.rearrange("(a p) o -> p a o", p=128)),
            writes=[("wbuf", 2)], fam="w2")
        for dt in range(DT):
            mm(pb[4],
               wbuf[2][:, dt * 128:(dt + 1) * 128],
               xn3T[ch][:, dt * 512:(dt + 1) * 512],
               dt == 0, dt == DT - 1,
               reads=[("wbuf", 2), xk],
               war=[("pb4",)] if dt == 0 else (),
               writes=[("pb4",)] if dt == DT - 1 else (),
               inc=(dt == DT - 1))
        ctx.comp("act", lambda s: s.mul(scratch[:, 0:512], pb[4], LORA_SCALING),
                 reads=[("pb4",)], writes=[("scratch",)])
        ctx.comp("dve", lambda v: v.tensor_tensor(
            hw_bf[:], scratch[:, 0:512], pb[3], op=ALU.mult),
            reads=[("scratch",), ("pb3",)], writes=[("hw",)])

    def base_chunk(ch):
        xk = ("xn3T",)

        def wb_prefetch(ot, par):
            ctx.dma("pool", lambda e, ot=ot, par=par: e.dma_start(
                out=wbuf[par][:].rearrange("p (a m) -> p a m", a=DT),
                in_=wb4[ot]),
                writes=[("wbuf", par)], fam=f"w{par}")

        for uu in range(3):
            wb_prefetch(uu, uu)
        pending = []
        for ot in range(DT):
            par = ot % 3
            bank = ot % 2
            bk = (f"pb{bank}",)
            ap2 = ot % 2
            for dt in range(DT):
                mm(pb[bank],
                   wbuf[par][:, dt * 128:(dt + 1) * 128],
                   xn3T[ch][:, dt * 512:(dt + 1) * 512],
                   dt == 0, False,
                   reads=[("wbuf", par), xk],
                   war=[bk] if dt == 0 else (),
                   inc=False)
            mm(pb[bank], bt[:, ot * 128:(ot + 1) * 128], hw_bf[:],
               False, True,
               reads=[("bt",), ("hw",)],
               writes=[bk], inc=True)
            if ot + 3 < DT:
                wb_prefetch(ot + 3, par)
            ctx.comp("act", lambda s, b=bank, ot=ot, ap2=ap2: s.activation(
                attnT[ap2][:], pb[b], AF.Identity, bias=bbs[:, ot:ot + 1]),
                reads=[bk, ("bbs",)], writes=[("attnT", ap2)])
            for fl in pending:
                fl()
            pending = [lambda ot=ot, ap2=ap2: transpose_batch(
                attnT[ap2], ("attnT", ap2), 1,
                lambda g, ot=ot: ftok[:].rearrange(
                    "p (a m) -> p a m", a=4)[:, 0:4, ot * 128:(ot + 1) * 128],
                ("ftok",), evac_eng="dve")]
        for fl in pending:
            fl()

    def ln4_chunk(ch):
        for j in range(4):
            tglob = ch * 4 + j
            tile = ftok[:, j * D:(j + 1) * D]
            ln_stats(tile, [("ftok",)])
            ctx.comp("dve", lambda v, tile=tile, tglob=tglob: v.scalar_tensor_tensor(
                outb[:], tile, stats[:, 6:7], x1[:, tglob * D:(tglob + 1) * D],
                op0=ALU.mult, op1=ALU.add),
                reads=[("ftok",), ("st", 6), ("x1", tglob)], writes=[("outb",)])
            ctx.comp("dve", lambda v: v.tensor_scalar(
                outb[:], outb[:], stats[:, 7:8], None, op0=ALU.subtract),
                reads=[("st", 7), ("outb",)], writes=[("outb",)])
            ctx.dma("sp", lambda e, tglob=tglob: e.dma_start(
                out=out[tglob * 128:(tglob + 1) * 128, :], in_=outb[:]),
                reads=[("outb",)], writes=[("out", tglob)], fam="outq")

    if STOP_PHASE >= 7:
        ln3_chunk(0)
        router_chunk(0)
        base_chunk(0)
        ln3_chunk(1)
        router_chunk(1)
        ln4_chunk(0)
        base_chunk(1)
        ln4_chunk(1)

    ctx.q["sp"].append(lambda eng, v=ctx.cnt["outq"], s=ctx.sems["outq"]: eng.wait_ge(s, v))
    ctx.trace["sp"].append(("w", "outq", ctx.cnt["outq"]))
    ctx.check()

    with nc.Block() as block:
        @block.tensor
        def _(t):
            for f in ctx.q["pe"]:
                f(t)

        @block.scalar
        def _(s):
            for f in ctx.q["act"]:
                f(s)

        @block.vector
        def _(v):
            for f in ctx.q["dve"]:
                f(v)

        @block.sync
        def _(s):
            for f in ctx.q["sp"]:
                f(s)

        @block.gpsimd
        def _(g):
            for f in ctx.q["pool"]:
                f(g)

    for g in list(live.values())[::-1]:
        g.__exit__(None, None, None)
    for g in sem_guards[::-1]:
        g.__exit__(None, None, None)
    return nc



_NC_CACHE = None


def kernel(L0_attn_x, L0_attn_Wq, L0_attn_Wk, L0_attn_Wv, L0_attn_Wo,
           L0_ffn_base_W, L0_ffn_base_b, L0_ffn_router_W, L0_ffn_router_b,
           L0_ffn_A, L0_ffn_B, L0_ffn_expert_map):
    global _NC_CACHE
    x = np.asarray(L0_attn_x, dtype=np.float32)
    wq = np.asarray(L0_attn_Wq, dtype=np.float32)
    wk = np.asarray(L0_attn_Wk, dtype=np.float32)
    wv = np.asarray(L0_attn_Wv, dtype=np.float32)
    wo = np.asarray(L0_attn_Wo, dtype=np.float32)
    wb = np.asarray(L0_ffn_base_W, dtype=np.float32)
    bb = np.asarray(L0_ffn_base_b, dtype=np.float32)
    wr = np.asarray(L0_ffn_router_W, dtype=np.float32)
    rb = np.asarray(L0_ffn_router_b, dtype=np.float32)
    A = np.asarray(L0_ffn_A, dtype=np.float32)
    Bm = np.asarray(L0_ffn_B, dtype=np.float32)
    em = np.asarray(L0_ffn_expert_map, dtype=np.int32)

    cos_tab, sin_tab = _rope_tables()

    def tile4(wT, nu):
        return np.ascontiguousarray(
            wT.reshape(DT, 128, nu, 128).transpose(2, 1, 0, 3))

    shared = {
        "wq4": tile4(wq.T, H),
        "wk4": tile4(wk.T, KVH),
        "wv4": tile4(wv.T, KVH),
        "wo4": tile4(wo.T, DT),
        "wb4": tile4(wb.T, DT),
        "wrT": np.ascontiguousarray(
            np.concatenate([wr.T, np.zeros((D, 1), np.float32)], axis=1)),
        "atT": np.ascontiguousarray(A.reshape(E * R, D).T),
        "btL": np.ascontiguousarray(Bm.transpose(0, 2, 1).reshape(E * R, D)),
        "bbT": np.ascontiguousarray(bb.reshape(DT, 128).T),
        "rb_row": np.ascontiguousarray(rb.reshape(1, 15)),
        "em_row": np.ascontiguousarray(em.reshape(1, 15)),
        "ident_in": np.eye(128, dtype=BF),
        "ones128_in": np.ones((128, 128), dtype=BF),
        "ones1_in": np.ones((1, 128), dtype=np.float32),
        "ar8_in": np.arange(8, dtype=np.float32).reshape(8, 1),
        "expand_in": np.kron(np.eye(E), np.ones((1, R))).astype(BF),
        "ar15_in": np.tile(
            np.concatenate([np.arange(15, dtype=np.float32), [99.0]]),
            (128, 4)).astype(np.float32),
    }
    tri = (np.arange(512)[:, None] <= np.arange(512)[None, :]).astype(BF)
    shared["tri_in"] = np.ascontiguousarray(
        tri.reshape(4, 128, 512).transpose(1, 0, 2).reshape(128, 4 * 512))

    in_maps = []
    for core in range(8):
        b, s = core // 2, core % 2
        perm = _core_perm(s)
        m = dict(shared)
        m["xkv"] = np.ascontiguousarray(x[b][perm])
        m["coskT"] = np.ascontiguousarray(cos_tab[perm].T)
        m["sinkT"] = np.ascontiguousarray(sin_tab[perm].T)
        bv = np.full((128, 32), MASKV, dtype=np.float32)
        for ch in range(2):
            nkt = 8 if ch == 0 else 16
            diag0 = 4 if ch == 0 else 8
            qpos = perm[512 + ch * 512: 512 + (ch + 1) * 512]
            for kt in range(nkt):
                col = ch * 16 + kt
                if diag0 <= kt < diag0 + 4:
                    bv[:, col] = NEGC
                    continue
                kpos = perm[kt * 128:(kt + 1) * 128]
                vis = bool(np.all(kpos[:, None] <= qpos[None, :]))
                novis = bool(np.all(kpos[:, None] > qpos[None, :]))
                assert vis or novis, (core, ch, kt)
                bv[:, col] = NEGC if vis else MASKV
        m["biasv_in"] = bv
        in_maps.append(m)

    if _NC_CACHE is None:
        _NC_CACHE = build_program()
    res = run_bass_kernel_spmd(_NC_CACHE, in_maps, core_ids=list(range(8)))

    outp = np.empty((B, T, D), dtype=np.float32)
    for core in range(8):
        b, s = core // 2, core % 2
        o = res.results[core]["out"]
        if s == 0:
            outp[b, 0:512] = o[0:512]
            outp[b, 1536:2048] = o[512:1024]
        else:
            outp[b, 512:1536] = o
    return outp


# version 16
# speedup vs baseline: 1.1461x; 1.0140x over previous
import math
import numpy as np
import ml_dtypes
import concourse.bass as bass
import concourse.mybir as mybir
from concourse.bass_utils import run_bass_kernel_spmd

F32 = mybir.dt.float32
BF16 = mybir.dt.bfloat16
I32 = mybir.dt.int32
AF = mybir.ActivationFunctionType
ALU = mybir.AluOpType
AX = mybir.AxisListType

BF = ml_dtypes.bfloat16

B, T, D = 4, 2048, 4096
H, KVH, HD = 32, 8, 128
DT = D // 128
NKT = 16
SCALE = 1.0 / math.sqrt(HD)
NEGC = -10.0
MASKV = -30000.0
EPS = 1e-5
E, R = 8, 16
LORA_SCALING = 2.0
STOP_PHASE = 99
KV_KINDS = "kv"
KV_COUNT = 99


def _rope_tables():
    base = 500000.0
    inv_freq = 1.0 / base ** (np.arange(0, HD, 2, dtype=np.float32) / HD)
    factor, low_f, high_f, old_ctx = 32.0, 1.0, 4.0, 8192
    low_wl, high_wl = old_ctx / low_f, old_ctx / high_f
    wavelen = 2 * math.pi / inv_freq
    inv_llama = np.where(wavelen > low_wl, inv_freq / factor, inv_freq)
    smooth = (old_ctx / wavelen - low_f) / (high_f - low_f)
    smoothed = (1 - smooth) * inv_llama / factor + smooth * inv_llama
    is_med = ~(wavelen < high_wl) & ~(wavelen > low_wl)
    inv = np.where(is_med, smoothed, inv_llama).astype(np.float32)
    pos = np.arange(T, dtype=np.float32)
    freqs = pos[:, None] * inv[None, :]
    emb = np.concatenate([freqs, freqs], axis=-1)
    return np.cos(emb).astype(np.float32), np.sin(emb).astype(np.float32)


def _core_perm(s):
    p = np.arange(T)
    if s == 0:
        return np.concatenate([p[512:1024], p[0:512], p[1536:2048], p[1024:1536]])
    return np.concatenate([p[0:512], p[512:1536], p[1536:2048]])


ENGS = ("pe", "act", "dve", "sp", "pool")


class Ctx:

    DMA_FAMS = ("cst", "cs", "xb", "xq", "orhs", "sp0", "sp1", "outq",
                "w0", "w1", "w2", "c2")

    def __init__(self, nc):
        self.nc = nc
        self.q = {e: [] for e in ENGS}
        self.sem_names = ("pe", "act", "dve") + self.DMA_FAMS
        self.sems = {}
        self.cnt = {n: 0 for n in self.sem_names}
        self.hw = {e: {} for e in ENGS}
        self.buf = {}
        self.trace = {e: [] for e in ENGS}

    def make_sems(self, enter):
        for n in self.sem_names:
            self.sems[n] = enter(self.nc.semaphore(n))

    LOG = []

    def _need(self, e, sem, val):
        Ctx.LOG.append((e, "wait", sem, val, self.cnt.get(sem)))
        if sem in self.DMA_FAMS:
            val = self.cnt[sem]
        if self.hw[e].get(sem, 0) >= val:
            return
        self.hw[e][sem] = val
        s = self.sems[sem]
        self.q[e].append(lambda eng, s=s, val=val: eng.wait_ge(s, val))
        self.trace[e].append(("w", sem, val))

    def _entry(self, k):
        if k not in self.buf:
            self.buf[k] = {"w": None, "r": []}
        return self.buf[k]

    def _waits(self, e, reads, writes, war):
        for k in reads:
            ent = self._entry(k)
            if ent["w"]:
                self._need(e, *ent["w"])
            if isinstance(k[0], str) and k[0].startswith("pb"):
                for r in ent["r"]:
                    if r[0] != e:
                        self._need(e, *r)
        for k in tuple(writes) + tuple(war):
            ent = self._entry(k)
            if ent["w"]:
                self._need(e, *ent["w"])
            for r in ent["r"]:
                self._need(e, *r)

    def fence(self):
        Ctx.LOG.append(("ALL", "FENCE", None, None, None))
        """Full barrier: every engine waits everything issued so far. Emitted at
        SBUF free points, where later allocations reuse freed addresses that the
        logical-key tracker cannot see."""
        snap = dict(self.cnt)
        for e in ENGS:
            for sem, val in snap.items():
                if val > 0:
                    self._need(e, sem, val)

    def comp(self, e, fn, reads=(), writes=(), war=(), inc=True):
        Ctx.LOG.append((e, "comp", self.cnt[e] + 1, tuple(writes), tuple(reads)))
        self._waits(e, reads, writes, war)
        sem = e
        val = self.cnt[sem] + 1
        if inc:
            self.cnt[sem] = val
            s = self.sems[sem]
            self.q[e].append(lambda eng, fn=fn, s=s: fn(eng).then_inc(s, 1))
            self.trace[e].append(("i", sem, 1))
        else:
            self.q[e].append(lambda eng, fn=fn: fn(eng))
        for k in reads:
            self._entry(k)["r"].append((sem, val))
        for k in writes:
            ent = self._entry(k)
            ent["w"] = (sem, val)
            ent["r"] = []
        return (sem, val)

    def dma(self, queue, fn, reads=(), writes=(), fam="cst"):
        Ctx.LOG.append((queue, "dma", fam, tuple(writes), tuple(reads)))
        """Emit a DMA on queue 'sp'|'pool' using semaphore family `fam`.

        DMA completions on one queue interleave across in-flight transfers, so
        cumulative counts are only sound per family where buffer reuse is
        WAR-gated (each family = one rotating buffer, or a write-once batch).
        """
        e = queue
        self._waits(e, reads, writes, ())
        sem = fam
        val = self.cnt[sem] + 16
        self.cnt[sem] = val
        s = self.sems[sem]
        self.q[e].append(lambda eng, fn=fn, s=s: fn(eng).then_inc(s, 16))
        self.trace[e].append(("i", sem, 16))
        for k in reads:
            self._entry(k)["r"].append((sem, val))
        for k in writes:
            ent = self._entry(k)
            ent["w"] = (sem, val)
            ent["r"] = []
        return (sem, val)

    def check(self):
        ptr = {e: 0 for e in ENGS}
        cnt = {n: 0 for n in self.sem_names}
        progress = True
        while progress:
            progress = False
            for e in ENGS:
                tr = self.trace[e]
                while ptr[e] < len(tr):
                    kind, sem, val = tr[ptr[e]]
                    if kind == "w" and cnt[sem] < val:
                        break
                    if kind == "i":
                        cnt[sem] += val
                    ptr[e] += 1
                    progress = True
        stuck = {e: (ptr[e], len(self.trace[e]), self.trace[e][ptr[e]])
                 for e in ENGS if ptr[e] < len(self.trace[e])}
        if stuck:
            raise RuntimeError(f"schedule deadlock: {stuck} cnt={cnt}")



def build_program():
    nc = bass.Bass("TRN2", target_bir_lowering=False)

    def din(name, shape, dt=F32):
        return nc.dram_tensor(name, list(shape), dt, kind="ExternalInput").ap()

    xkv = din("xkv", (T, D))
    wq4 = din("wq4", (H, 128, DT, 128))
    wk4 = din("wk4", (KVH, 128, DT, 128))
    wv4 = din("wv4", (KVH, 128, DT, 128))
    wo4 = din("wo4", (DT, 128, DT, 128))
    wb4 = din("wb4", (DT, 128, DT, 128))
    wrT = din("wrT", (D, 16))
    atT = din("atT", (D, 128))
    btL = din("btL", (128, D))
    bbT = din("bbT", (128, DT))
    rb_row = din("rb_row", (1, 15))
    em_row = din("em_row", (1, 15), I32)
    coskT = din("coskT", (128, T))
    sinkT = din("sinkT", (128, T))
    tri_in = din("tri_in", (128, 4 * 512), BF16)
    biasv_in = din("biasv_in", (128, 32))
    ident_in = din("ident_in", (128, 128), BF16)
    ones128_in = din("ones128_in", (128, 128), BF16)
    ones1_in = din("ones1_in", (1, 128))
    ar8_in = din("ar8_in", (8, 1))
    expand_in = din("expand_in", (8, 128), BF16)
    ar15_in = din("ar15_in", (128, 64))

    out = nc.dram_tensor("out", [1024, D], F32, kind="ExternalOutput").ap()
    oall = nc.dram_tensor("oall", [2, 128, H, 512], BF16, kind="Internal").ap()

    live = {}
    done = []

    def enter(g):
        return g.__enter__()

    def sb(name, shape, dt):
        g = nc.sbuf_tensor(name, list(shape), dt)
        live[name] = g
        return g.__enter__()

    def ps(name, shape=(128, 512), dt=F32):
        g = nc.psum_tensor(name, list(shape), dt)
        live[name] = g
        return g.__enter__()

    def free(*names):
        for n in names:
            g = live.pop(n)
            g.__exit__(None, None, None)
            done.append(n)

    ctx = Ctx(nc)
    sem_guards = []
    ctx.make_sems(lambda g: (sem_guards.append(g), g.__enter__())[1])

    ident = sb("ident", (128, 128), BF16)
    stats = sb("stats", (128, 8), F32)
    scratch = sb("scratch", (128, D), BF16)
    wbuf = [sb(f"wbuf{i}", (128, DT * 128), BF16) for i in range(3)]
    m1 = sb("m1", (16, 128), BF16)
    rbb = sb("rbb", (128, 16), F32)
    bbs = sb("bbs", (128, DT), F32)
    epsb = sb("epsb", (128, 1), F32)
    xbuf = sb("xbuf", (128, D), F32)

    r_misc = sb("r_misc", (16, 64), F32)
    ones1 = sb("ones1", (1, 128), F32)
    ar8 = sb("ar8", (8, 1), F32)
    expand = sb("expand", (8, 128), BF16)
    emi = sb("emi", (1, 15), I32)
    rb_sb = sb("rb_sb", (1, 15), F32)
    ohT_bf = sb("ohT_bf", (8, 15), BF16)

    pbA = ps("pbA", (128, 1024))
    pbB = ps("pbB", (128, 1024))
    pb47 = [ps(f"pbt{i}") for i in range(4, 8)]
    pb = [pbA[:, 0:512], pbA[:, 512:1024], pbB[:, 0:512], pbB[:, 512:1024],
          pb47[0][:], pb47[1][:], pb47[2][:], pb47[3][:]]
    pbS = [pbA, pbB]

    def mm(dst_ap, lhs_ap, rhs_ap, start, stop, reads, writes=(), war=(), inc=False):
        return ctx.comp(
            "pe",
            lambda t, d=dst_ap, l=lhs_ap, r=rhs_ap, s1=start, s2=stop: t.matmul(
                d, l, r, start=s1, stop=s2),
            reads=reads, writes=writes, war=war, inc=inc)

    ctx.dma("sp", lambda e: e.dma_start(out=ident[:], in_=ident_in), writes=[("ident",)])
    ctx.dma("sp", lambda e: e.dma_start(out=xbuf[:], in_=xkv[0:128, :]),
            writes=[("xbuf",)], fam="xb")
    ctx.dma("sp", lambda e: e.dma_start(out=bbs[:], in_=bbT), writes=[("bbs",)])
    ctx.dma("sp", lambda e: e.dma_start(out=ones1[:], in_=ones1_in), writes=[("ones1",)])
    ctx.dma("sp", lambda e: e.dma_start(out=ar8[:], in_=ar8_in), writes=[("ar8",)])
    ctx.dma("sp", lambda e: e.dma_start(out=expand[:], in_=expand_in), writes=[("expand",)])
    ctx.dma("sp", lambda e: e.dma_start(out=emi[:], in_=em_row), writes=[("emi",)])
    ctx.dma("sp", lambda e: e.dma_start(out=rb_sb[:], in_=rb_row), writes=[("rb_sb",)])

    ctx.comp("dve", lambda v: v.memset(epsb[:], EPS), writes=[("epsb",)])

    mm(pb[2][:, 0:15], ones1[:, 0:128], rb_sb[:], True, True,
       reads=[("ones1",), ("rb_sb",)], war=[("pb2",)], writes=[("pb2",)], inc=True)
    ctx.comp("act", lambda s: s.copy(rbb[:, 0:15], pb[2][:, 0:15]),
             reads=[("pb2",)], writes=[("rbb",)])
    ctx.comp("dve", lambda v: v.tensor_copy(r_misc[0:1, 0:15], emi[:]),
             reads=[("emi",)], writes=[("emf",)])
    mm(pb[3][0:8, 0:15], ones1[:, 0:8], r_misc[0:1, 0:15], True, True,
       reads=[("emf",), ("ones1",)], war=[("pb3",)], writes=[("pb3",)], inc=True)
    ctx.comp("act", lambda s: s.copy(r_misc[0:8, 16:31], pb[3][0:8, 0:15]),
             reads=[("pb3",)], writes=[("emb8",)])
    ctx.comp("dve", lambda v: v.tensor_scalar(
        r_misc[0:8, 32:47], r_misc[0:8, 16:31], ar8[:, 0:1], None, op0=ALU.is_equal),
        reads=[("emb8",), ("ar8",)], writes=[("ohT",)])
    ctx.comp("dve", lambda v: v.tensor_copy(ohT_bf[:], r_misc[0:8, 32:47]),
             reads=[("ohT",)], writes=[("ohT_bf",)])
    mm(pb[4][0:15, 0:128], ohT_bf[:], expand[:], True, True,
       reads=[("ohT_bf",), ("expand",)], war=[("pb4",)], writes=[("pb4",)], inc=True)
    ctx.comp("dve", lambda v: v.memset(m1[:], 0.0), writes=[("m1",)])
    ctx.comp("act", lambda s: s.copy(m1[0:15, :], pb[4][0:15, 0:128]),
             reads=[("pb4",)], writes=[("m1",)])
    ctx.fence()
    free("ohT_bf", "rb_sb", "emi", "expand", "ar8", "ones1", "r_misc")

    ones128 = sb("ones128", (128, 128), BF16)
    tri = sb("tri", (128, 4 * 512), BF16)
    biasv = sb("biasv", (128, 32), F32)
    cosh = sb("cosh", (128, 1024), BF16)
    sinh = sb("sinh", (128, 1024), BF16)
    ktall = sb("ktall", (128, KVH * T), BF16)
    vall = sb("vall", (128, NKT * 1024), BF16)
    slotA = sb("slotA", (128, DT * 512), BF16)
    slotB = sb("slotB", (128, DT * 512), BF16)
    ksb = sb("ksb", (128, 512), F32)
    t1b = sb("t1b", (128, 512), F32)
    vtb = sb("vtb", (128, 512), BF16)
    qh = [sb(f"qh{i}", (128, 512), BF16) for i in range(3)]
    ptile = [sb("ptile0", (128, 6 * 512), BF16)] * 2
    linv = sb("linv", (128, 512), F32)
    pacc = sb("pacc", (128, 6 * 512), BF16)
    ctx.dma("sp", lambda e: e.dma_start(out=ones128[:], in_=ones128_in), writes=[("ones128",)])
    ctx.dma("pool", lambda e: e.dma_start(out=tri[:], in_=tri_in), writes=[("tri",)], fam="cs")
    ctx.dma("sp", lambda e: e.dma_start(out=biasv[:], in_=biasv_in), writes=[("biasv",)])
    def load_cos_half(hf):
        ctx.dma("pool", lambda e, hf=hf: e.dma_start(
            out=cosh[:], in_=coskT[:, hf * 1024:(hf + 1) * 1024]), writes=[("cosh",)], fam="cs")
        ctx.dma("pool", lambda e, hf=hf: e.dma_start(
            out=sinh[:], in_=sinkT[:, hf * 1024:(hf + 1) * 1024]), writes=[("sinh",)], fam="cs")

    load_cos_half(0)

    def ln_stats(src_ap, src_keys):
        stc = lambda c: stats[:, c:c + 1]
        K = lambda c: ("st", c)
        ctx.comp("dve", lambda v: v.reduce_sum(stc(0), src_ap, axis=AX.X),
                 reads=list(src_keys), writes=[K(0)])
        ctx.comp("act", lambda s: s.activation(
            scratch[:], src_ap, AF.Square, accum_out=stc(1)),
            reads=list(src_keys), writes=[("scratch",), K(1)])
        ctx.comp("dve", lambda v: v.tensor_scalar_mul(stc(2), stc(0), 1.0 / D),
                 reads=[K(0)], writes=[K(2)])
        ctx.comp("dve", lambda v: v.tensor_tensor(stc(3), stc(2), stc(2), op=ALU.mult),
                 reads=[K(2)], writes=[K(3)])
        ctx.comp("dve", lambda v: v.tensor_scalar(
            stc(4), stc(1), 1.0 / D, None, op0=ALU.mult),
            reads=[K(1)], writes=[K(4)])
        ctx.comp("dve", lambda v: v.tensor_tensor(stc(4), stc(4), stc(3), op=ALU.subtract),
                 reads=[K(4), K(3)], writes=[K(4)])
        ctx.comp("act", lambda s: s.activation(stc(5), stc(4), AF.Ln, bias=epsb[:, 0:1]),
                 reads=[K(4), ("epsb",)], writes=[K(5)])
        ctx.comp("act", lambda s: s.activation(stc(6), stc(5), AF.Exp, scale=-0.5),
                 reads=[K(5)], writes=[K(6)])
        ctx.comp("dve", lambda v: v.tensor_tensor(stc(7), stc(2), stc(6), op=ALU.mult),
                 reads=[K(2), K(6)], writes=[K(7)])
        return stc(6), stc(7)

    def ln_normalize(dst_ap, src_ap, src_keys, dst_keys):
        rstd, nmr = stats[:, 6:7], stats[:, 7:8]
        ctx.comp("dve", lambda v: v.tensor_scalar(
            dst_ap, src_ap, rstd, nmr, op0=ALU.mult, op1=ALU.subtract),
            reads=list(src_keys) + [("st", 6), ("st", 7)], writes=list(dst_keys))

    def transpose_batch(src, src_key, n_groups, dest_fn, dest_key, evac_eng="act",
                        groups=None):
        for g in (range(n_groups) if groups is None else groups):
            for j in range(4):
                idx = 4 * g + j
                last = (j == 3)
                mm(pb[6][:, j * 128:(j + 1) * 128],
                   src[:, idx * 128:(idx + 1) * 128], ident[:],
                   True, True,
                   reads=[src_key, ("ident",)],
                   war=[("pb6",)] if j == 0 else (),
                   writes=[("pb6",)] if last else (),
                   inc=last)
            if evac_eng == "act":
                ctx.comp("act", lambda e, g=g: e.copy(dest_fn(g), pb[6]),
                         reads=[("pb6",)], writes=[dest_key])
            else:
                ctx.comp("dve", lambda e, g=g: e.tensor_copy(dest_fn(g), pb[6]),
                         reads=[("pb6",)], writes=[dest_key])

    def ln_quarter_steps(qi, slot, slot_name, skip_dma0=False):
        steps = []
        for tl in range(4):
            t = qi * 4 + tl
            if not (skip_dma0 and tl == 0):
                steps.append(lambda t=t: ctx.dma("sp", lambda e, t=t: e.dma_start(
                    out=xbuf[:], in_=xkv[t * 128:(t + 1) * 128, :]),
                    writes=[("xbuf",)], fam="xb"))
            steps.append(lambda: ln_stats(xbuf[:], [("xbuf",)]))
            steps.append(lambda: ln_normalize(
                scratch[:], xbuf[:], [("xbuf",)], [("scratch",)]))
            for g0 in range(0, 8, 2):
                steps.append(lambda g0=g0, tl=tl, slot=slot: transpose_batch(
                    scratch, ("scratch",), 8,
                    lambda g, tl=tl, slot=slot: slot[:].rearrange(
                        "p (a m) -> p a m", a=DT)[:, 4 * g:4 * g + 4,
                                                  tl * 128:(tl + 1) * 128],
                    (slot_name,), groups=range(g0, g0 + 2)))
        return steps

    def kv_units(c, slot, slot_name, extra=()):
        units = [("k", i) for i in range(KVH) if "k" in KV_KINDS] + \
                [("v", i) for i in range(KVH) if "v" in KV_KINDS]
        units = units[:KV_COUNT]
        ex, edone = list(extra), 0

        def prefetch(u, par):
            kind, i = units[u]
            w = wk4 if kind == "k" else wv4
            ctx.dma("pool", lambda e, w=w, i=i, par=par: e.dma_start(
                out=wbuf[par][:].rearrange("p (a m) -> p a m", a=DT),
                in_=w[i]),
                writes=[("wbuf", par)], fam=f"w{par}")

        for uu in range(min(3, len(units))):
            prefetch(uu, uu % 3)
        co = c % 2
        for u, (kind, i) in enumerate(units):
            par = u % 3
            bank = 2 * (u % 2) + co
            bk = (f"pb{bank}",)
            for dt in range(DT):
                mm(pb[bank],
                   wbuf[par][:, dt * 128:(dt + 1) * 128],
                   slot[:, dt * 512:(dt + 1) * 512],
                   dt == 0, dt == DT - 1,
                   reads=[("wbuf", par), (slot_name,)],
                   war=[bk] if dt == 0 else (),
                   writes=[bk] if dt == DT - 1 else (),
                   inc=(dt == DT - 1))
            if kind == "k":
                ctx.comp("act", lambda s, b=bank: s.mul(ksb[0:64, :], pb[b][64:128, :], -1.0),
                         reads=[bk], writes=[("ksb",)], war=[("ksb2",)])
                ctx.comp("act", lambda s, b=bank: s.copy(ksb[64:128, :], pb[b][0:64, :]),
                         reads=[bk], writes=[("ksb2",)])
                ctx.comp("dve", lambda v, b=bank, co=co: v.tensor_tensor(
                    t1b[:], pb[b], cosh[:, co * 512:(co + 1) * 512], op=ALU.mult),
                    reads=[bk, ("cosh",)], writes=[("t1b",)])
                ctx.comp("dve", lambda v, co=co: v.tensor_tensor(
                    ksb[:], ksb[:], sinh[:, co * 512:(co + 1) * 512], op=ALU.mult),
                    reads=[("ksb",), ("ksb2",), ("sinh",)],
                    writes=[("ksb",), ("ksb2",)])
                dst = ktall[:, i * T + c * 512: i * T + (c + 1) * 512]
                ctx.comp("dve", lambda v, dst=dst: v.tensor_tensor(
                    dst, t1b[:], ksb[:], op=ALU.add),
                    reads=[("t1b",), ("ksb",), ("ksb2",)],
                    writes=[("ktall",)])
            else:
                ctx.comp("act", lambda s, b=bank: s.copy(vtb[:], pb[b]),
                         reads=[bk], writes=[("vtb",)])
                transpose_batch(
                    vtb, ("vtb",), 1,
                    lambda g, c=c, i=i: vall[:].rearrange(
                        "p (a m) -> p a m", a=NKT)[:, 4 * c:4 * c + 4, i * 128:(i + 1) * 128],
                    ("vall",))
            if u + 3 < len(units):
                prefetch(u + 3, par)
            want = (len(ex) * (u + 1)) // len(units)
            while edone < want:
                ex[edone]()
                edone += 1

    def attention(ch, qslot, qslot_name):
        qht = qh + [vtb]
        qhk = lambda i: ("qh", i) if i < 3 else ("vtb",)
        nkt = 8 if ch == 0 else 16
        diag0 = 4 if ch == 0 else 8
        qcos0 = 512 if ch == 0 else 0
        pair_slot = [0] + [1] * (nkt // 2 - 1)
        folds_after = {kt: [(0, 1)] for kt in range(3, nkt, 2)}

        def wq_prefetch(h, par):
            ctx.dma("pool", lambda e, h=h, par=par: e.dma_start(
                out=wbuf[par][:].rearrange("p (a m) -> p a m", a=DT),
                in_=wq4[h]),
                writes=[("wbuf", par)], fam=f"w{par}")

        def q_proj_part(h, par, qbank, d0, dn):
            qk = (f"pb{qbank}",)
            for dt in range(d0, d0 + dn):
                mm(pb[qbank],
                   wbuf[par][:, dt * 128:(dt + 1) * 128],
                   qslot[:, dt * 512:(dt + 1) * 512],
                   dt == 0, dt == DT - 1,
                   reads=[("wbuf", par), (qslot_name,)],
                   war=[qk] if dt == 0 else (),
                   writes=[qk] if dt == DT - 1 else (),
                   inc=(dt == DT - 1))

        def q_proj(h, par, qbank):
            q_proj_part(h, par, qbank, 0, DT)

        def rope_q(qbank, s):
            qk = (f"pb{qbank}",)
            ctx.comp("act", lambda sc, b=qbank: sc.mul(ksb[0:64, :], pb[b][64:128, :], -1.0),
                     reads=[qk], writes=[("ksb",)], war=[("ksb2",)])
            ctx.comp("act", lambda sc, b=qbank: sc.copy(ksb[64:128, :], pb[b][0:64, :]),
                     reads=[qk], writes=[("ksb2",)])
            ctx.comp("dve", lambda v, b=qbank: v.tensor_tensor(
                t1b[:], pb[b], cosh[:, qcos0:qcos0 + 512], op=ALU.mult),
                reads=[qk, ("cosh",)], writes=[("t1b",)])
            ctx.comp("dve", lambda v: v.tensor_tensor(
                ksb[:], ksb[:], sinh[:, qcos0:qcos0 + 512], op=ALU.mult),
                reads=[("ksb",), ("ksb2",), ("sinh",)],
                writes=[("ksb",), ("ksb2",)])
            ctx.comp("dve", lambda v, s=s: v.tensor_tensor(
                qht[s][:], t1b[:], ksb[:], op=ALU.add),
                reads=[("t1b",), ("ksb",), ("ksb2",)],
                writes=[qhk(s)])

        npair = H // 2

        def q_pair_steps(p):
            if p >= npair:
                return [], []
            steps, tail = [], []
            for i, (h, qbank) in enumerate(((2 * p, 6), (2 * p + 1, 7))):
                par, sq = (2 * p + i) % 3, (2 * p + i) % 4
                for d0 in range(0, DT, 4):
                    steps.append(lambda h=h, par=par, qbank=qbank, d0=d0:
                                 q_proj_part(h, par, qbank, d0, 4))
                tail.append(lambda qbank=qbank, sq=sq: rope_q(qbank, sq))
            if p + 1 < npair:
                tail.append(lambda p=p: (wq_prefetch(2 * p + 2, (2 * p + 2) % 3),
                                         wq_prefetch(2 * p + 3, (2 * p + 3) % 3)))
            return steps, tail

        def pair_tail_steps(hp):
            pi = hp % 2
            a = pi
            h0, h1 = 2 * hp, 2 * hp + 1

            def l_mms():
                for s in range(2):
                    mm(pb[s], ones128[:],
                       pacc[:, a * 1024 + s * 512: a * 1024 + (s + 1) * 512],
                       True, True,
                       reads=[("ones128",), ("pacc", a)],
                       war=[(f"pb{s}",)], writes=[(f"pb{s}",)], inc=True)


            def ln_l():
                ctx.comp("act", lambda sc: sc.activation(t1b[:], pb[0], AF.Ln),
                         reads=[("pb0",)], writes=[("t1b",)])
                ctx.comp("act", lambda sc: sc.activation(ksb[:], pb[1], AF.Ln),
                         reads=[("pb1",)], writes=[("ksb",), ("ksb2",)])

            def exp_evac():
                for s, h, srck in ((0, h0, [("t1b",)]), (1, h1, [("ksb",), ("ksb2",)])):
                    src_ap = t1b if s == 0 else ksb
                    ctx.comp("act", lambda sc, src_ap=src_ap: sc.activation(
                        linv[:], src_ap[:], AF.Exp, scale=-1.0),
                        reads=srck, writes=[("linv",)])
                    sq = (2 * hp + s) % 4
                    osl = qht[sq][:]
                    ctx.comp("dve", lambda v, s=s, osl=osl: v.tensor_tensor(
                        osl, pb[4 + s], linv[:], op=ALU.mult),
                        reads=[(f"pb{4 + s}",), ("linv",)],
                        writes=[qhk(sq)])
                    ctx.dma("sp", lambda e, h=h, osl=osl: e.dma_start(
                        out=oall[ch, :, h, :], in_=osl),
                        reads=[qhk(sq)], writes=[("oall", ch)], fam=f"sp{ch}")

            return [l_mms, ln_l, exp_evac]

        wq_prefetch(0, 0)
        wq_prefetch(1, 1)
        for p0 in (0, 1):
            a, b = q_pair_steps(p0)
            for f in a + b:
                f()
        carry = []
        for hp in range(npair):
            h0, h1 = 2 * hp, 2 * hp + 1
            g = h0 // 4
            s0q, s1q = (2 * hp) % 4, (2 * hp + 1) % 4
            qsteps, qtail = q_pair_steps(hp + 2)
            qdone = 0
            acur = hp % 2
            def av_step(kt):
                kslot = kt % 3
                pk = ("pt", kslot)
                for s in range(2):
                    mm(pb[4 + s],
                       vall[:, kt * 1024 + g * 128: kt * 1024 + g * 128 + 128],
                       ptile[0][:, kslot * 1024 + s * 512: kslot * 1024 + (s + 1) * 512],
                       kt == 0, kt == nkt - 1,
                       reads=[("vall",), pk],
                       war=[(f"pb{4 + s}",)] if kt == 0 else (),
                       writes=[(f"pb{4 + s}",)] if kt == nkt - 1 else (),
                       inc=(kt == nkt - 1))

            for kt in range(nkt):
                kslot = kt % 3
                sb2 = 2 * (kt % 2)
                pslot = ptile[0][:, kslot * 1024:(kslot + 1) * 1024]
                pk = ("pt", kslot)
                for s, sq in ((0, s0q), (1, s1q)):
                    mm(pb[sb2 + s],
                       ktall[:, g * T + kt * 128: g * T + kt * 128 + 128],
                       qht[sq][:],
                       True, True,
                       reads=[("ktall",), qhk(sq)],
                       war=[(f"pb{sb2 + s}",)], writes=[(f"pb{sb2 + s}",)], inc=True)
                pcol = biasv[:, ch * 16 + kt: ch * 16 + kt + 1]
                ctx.comp("act", lambda sc, pslot=pslot, pcol=pcol, kt=kt: sc.activation(
                    pslot, pbS[kt % 2][:], AF.Exp, bias=pcol, scale=SCALE),
                    reads=[(f"pb{sb2}",), (f"pb{sb2 + 1}",), ("biasv",)],
                    writes=[pk])
                if diag0 <= kt < diag0 + 4:
                    ktr = kt - diag0
                    for s in range(2):
                        ctx.comp("dve", lambda v, kslot=kslot, s=s, ktr=ktr: v.tensor_tensor(
                            ptile[0][:, kslot * 1024 + s * 512: kslot * 1024 + (s + 1) * 512],
                            ptile[0][:, kslot * 1024 + s * 512: kslot * 1024 + (s + 1) * 512],
                            tri[:, ktr * 512:(ktr + 1) * 512], op=ALU.mult),
                            reads=[pk, ("tri",)],
                            writes=[pk])
                if kt == 1:
                    for f in carry[0:3]:
                        f()
                elif kt == 2:
                    for f in carry[3:4]:
                        f()
                elif kt == 3:
                    for f in carry[4:]:
                        f()
                if kt > 1:
                    av_step(kt - 2)
                if kt >= 4:
                    want = (len(qsteps) * (kt - 3)) // (nkt - 4)
                    while qdone < want:
                        qsteps[qdone]()
                        qdone += 1
                if kt % 2 == 1:
                    sidx = acur if pair_slot[kt // 2] == 0 else 2
                    k0, k1 = (kt - 1) % 3, kt % 3
                    ctx.comp("dve", lambda v, sidx=sidx, k0=k0, k1=k1: v.tensor_tensor(
                        pacc[:, sidx * 1024:(sidx + 1) * 1024],
                        ptile[0][:, k0 * 1024:(k0 + 1) * 1024],
                        ptile[0][:, k1 * 1024:(k1 + 1) * 1024], op=ALU.add),
                        reads=[("pt", k0), ("pt", k1)],
                        writes=[("pacc", sidx)])
                    if kt in folds_after:
                        ctx.comp("dve", lambda v, acur=acur: v.tensor_tensor(
                            pacc[:, acur * 1024:(acur + 1) * 1024],
                            pacc[:, acur * 1024:(acur + 1) * 1024],
                            pacc[:, 2 * 1024:3 * 1024], op=ALU.add),
                            reads=[("pacc", acur), ("pacc", 2)],
                            writes=[("pacc", acur)])
            av_step(nkt - 2)
            av_step(nkt - 1)
            carry = pair_tail_steps(hp) + qtail
        for f in carry:
            f()

    if STOP_PHASE >= 2:
        for f in ln_quarter_steps(0, slotA, "slotA", skip_dma0=True):
            f()
    if STOP_PHASE >= 3:
        kv_units(0, slotA, "slotA", extra=ln_quarter_steps(1, slotB, "slotB"))
        kv_units(1, slotB, "slotB",
                 extra=ln_quarter_steps(2, slotA, "slotA") if STOP_PHASE >= 5 else ())
    if STOP_PHASE >= 4:
        attention(0, slotB, "slotB")
    if STOP_PHASE >= 5:
        load_cos_half(1)
        kv_units(2, slotA, "slotA", extra=ln_quarter_steps(3, slotB, "slotB"))
        kv_units(3, slotB, "slotB")
        attention(1, slotA, "slotA")

    ctx.fence()
    free("pacc", "linv", "ptile0", "qh2", "qh1", "qh0",
         "vtb", "t1b", "ksb", "slotB", "slotA", "vall", "ktall",
         "sinh", "cosh", "biasv", "tri", "ones128", "xbuf")

    attnT = [sb(f"attnT{i}", (128, 512), BF16) for i in range(2)]
    x1 = sb("x1", (128, 8 * D), BF16)
    sc2 = sb("sc2", (128, D), F32)
    orhs = sb("orhs", (128, H * 512), BF16)
    atok = sb("atok", (128, 4 * D), BF16)
    xqb = sb("xqb", (128, D), F32)

    def orhs_load(ch):
        ctx.dma("sp", lambda e, ch=ch: e.dma_start(
            out=orhs[:].rearrange("p (a m) -> p a m", a=H),
            in_=oall[ch]),
            reads=[("oall", ch)], writes=[("orhs",)], fam="orhs")

    def wo_prefetch(od, par):
        ctx.dma("pool", lambda e, od=od, par=par: e.dma_start(
            out=wbuf[par][:].rearrange("p (a m) -> p a m", a=DT),
            in_=wo4[od]),
            writes=[("wbuf", par)], fam=f"w{par}")

    def oproj_units(ch):
        for uu in range(3):
            wo_prefetch(uu, uu)
        pending = []
        for od in range(DT):
            par = od % 3
            bank = od % 2
            bk = (f"pb{bank}",)
            ap2 = od % 2
            for ct in range(H):
                mm(pb[bank],
                   wbuf[par][:, ct * 128:(ct + 1) * 128],
                   orhs[:, ct * 512:(ct + 1) * 512],
                   ct == 0, ct == H - 1,
                   reads=[("wbuf", par), ("orhs",)],
                   war=[bk] if ct == 0 else (),
                   writes=[bk] if ct == H - 1 else (),
                   inc=(ct == H - 1))
            if od + 3 < DT:
                wo_prefetch(od + 3, par)
            ctx.comp("act", lambda s, b=bank, ap2=ap2: s.copy(attnT[ap2][:], pb[b]),
                     reads=[bk], writes=[("attnT", ap2)])
            for fl in pending:
                fl()
            pending = [lambda od=od, ap2=ap2: transpose_batch(
                attnT[ap2], ("attnT", ap2), 1,
                lambda g, od=od: atok[:].rearrange(
                    "p (a m) -> p a m", a=4)[:, 0:4, od * 128:(od + 1) * 128],
                ("atok",), evac_eng="dve")]
        for fl in pending:
            fl()

    def ln2_chunk(ch):
        for j in range(4):
            tglob = ch * 4 + j
            ctx.dma("sp", lambda e, tglob=tglob: e.dma_start(
                out=xqb[:], in_=xkv[512 + tglob * 128: 512 + (tglob + 1) * 128, :]),
                writes=[("xqb",)], fam="xq")
            tile = atok[:, j * D:(j + 1) * D]
            ln_stats(tile, [("atok",)])
            ln_normalize(sc2[:], tile, [("atok",)], [("sc2",)])
            ctx.comp("dve", lambda v, tglob=tglob: v.tensor_tensor(
                x1[:, tglob * D:(tglob + 1) * D], sc2[:], xqb[:], op=ALU.add),
                reads=[("sc2",), ("xqb",)], writes=[("x1", tglob)])

    if STOP_PHASE >= 6:
        orhs_load(0)
        oproj_units(0)
        orhs_load(1)
        ln2_chunk(0)
        oproj_units(1)
        ln2_chunk(1)

    ctx.fence()
    free("xqb", "atok", "orhs", "sc2")

    xn3T = [sb("xn3T0", (128, DT * 512), BF16)] * 2
    ftok = sb("ftok", (128, 4 * D), BF16)
    lg4 = sb("lg4", (128, 64), F32)
    er4 = sb("er4", (128, 64), F32)
    rank4 = sb("rank4", (128, 64), F32)
    cmp4 = sb("cmp4", (128, 64), F32)
    sel4 = sb("sel4", (128, 64), F32)
    lr4 = sb("lr4", (128, 8), F32)
    pm4 = sb("pm4", (128, 64), BF16)
    ar15 = sb("ar15", (128, 64), F32)
    pmT = sb("pmT", (16, 512), BF16)
    hw_bf = sb("hw_bf", (128, 512), BF16)
    wr = sb("wr", (128, DT * 16), BF16)
    bt = sb("bt", (128, D), BF16)
    outb = sb("outb", (128, D), F32)

    ctx.dma("pool", lambda e: e.dma_start(
        out=wr[:].rearrange("p (a m) -> p a m", a=DT),
        in_=wrT.rearrange("(a p) o -> p a o", p=128)), writes=[("wr",)], fam="c2")
    ctx.dma("pool", lambda e: e.dma_start(out=bt[:], in_=btL), writes=[("bt",)], fam="c2")
    ctx.dma("sp", lambda e: e.dma_start(out=ar15[:], in_=ar15_in), writes=[("ar15",)], fam="xq")

    def seg(buf, j, w=16):
        return buf[:, j * w:(j + 1) * w]

    def seg3(buf, a, b):
        return buf[:].rearrange("p (a m) -> p a m", a=4)[:, :, a:b]

    def ln3_chunk(ch):
        for j in range(4):
            tglob = ch * 4 + j
            tile = x1[:, tglob * D:(tglob + 1) * D]
            ln_stats(tile, [("x1", tglob)])
            ln_normalize(scratch[:], tile, [("x1", tglob)], [("scratch",)])
            transpose_batch(
                scratch, ("scratch",), 8,
                lambda g, j=j, ch=ch: xn3T[ch][:].rearrange(
                    "p (a m) -> p a m", a=DT)[:, 4 * g:4 * g + 4, j * 128:(j + 1) * 128],
                ("xn3T",))

    def router_chunk(ch):
        xk = ("xn3T",)
        for j in range(4):
            rb_bank = 2 + (j % 2)
            rbk = (f"pb{rb_bank}",)
            for dt in range(DT):
                mm(pb[rb_bank][:, 0:16],
                   xn3T[ch][:, dt * 512 + j * 128: dt * 512 + (j + 1) * 128],
                   wr[:, dt * 16:(dt + 1) * 16],
                   dt == 0, dt == DT - 1,
                   reads=[xk, ("wr",)],
                   war=[rbk] if dt == 0 else (),
                   writes=[rbk] if dt == DT - 1 else (),
                   inc=(dt == DT - 1))
            ctx.comp("dve", lambda v, j=j, b=rb_bank: v.tensor_tensor(
                seg(lg4, j)[:, 0:15], pb[b][:, 0:15], rbb[:, 0:15], op=ALU.add),
                reads=[rbk, ("rbb",)], writes=[("lg4", j)])
            ctx.comp("act", lambda s, j=j: s.activation(
                seg(er4, j)[:, 0:15], seg(lg4, j)[:, 0:15], AF.Exp,
                accum_out=lr4[:, j:j + 1]),
                reads=[("lg4", j)], writes=[("er4", j), ("lr4",)])
        ctx.comp("dve", lambda v: v.tensor_copy(rank4[:], ar15[:]),
                 reads=[("ar15",)], writes=[("rank4",)])
        er_keys = [("er4", j) for j in range(4)]
        for s_ in range(1, 15):
            w = 15 - s_
            ctx.comp("dve", lambda v, s_=s_, w=w: v.tensor_tensor(
                seg3(cmp4, 0, w), seg3(er4, s_, 15), seg3(er4, 0, w), op=ALU.is_gt),
                reads=er_keys, writes=[("cmp4",)])
            ctx.comp("dve", lambda v, w=w: v.tensor_tensor(
                seg3(rank4, 0, w), seg3(rank4, 0, w), seg3(cmp4, 0, w), op=ALU.add),
                reads=[("cmp4",)], writes=[("rank4",)])
            ctx.comp("dve", lambda v, s_=s_, w=w: v.tensor_tensor(
                seg3(rank4, s_, 15), seg3(rank4, s_, 15), seg3(cmp4, 0, w), op=ALU.subtract),
                reads=[("cmp4",)], writes=[("rank4",)])
        ctx.comp("dve", lambda v: v.tensor_single_scalar(
            sel4[:], rank4[:], 7.5, op=ALU.is_lt),
            reads=[("rank4",)], writes=[("sel4",)])
        ctx.comp("dve", lambda v: v.reciprocal(lr4[:, 4:8], lr4[:, 0:4]),
                 reads=[("lr4",)], writes=[("lr4i",)])
        ctx.comp("dve", lambda v: v.memset(seg3(pm4, 15, 16), 0.0), writes=[("pm4z",)])
        for j in range(4):
            ctx.comp("dve", lambda v, j=j: v.tensor_scalar(
                seg(pm4, j)[:, 0:15], seg(er4, j)[:, 0:15], lr4[:, 4 + j:5 + j],
                None, op0=ALU.mult),
                reads=[("er4", j), ("lr4i",)], writes=[("pm4", j, "a")])
            ctx.comp("dve", lambda v, j=j: v.tensor_tensor(
                seg(pm4, j)[:, 0:15], seg(pm4, j)[:, 0:15], seg(sel4, j)[:, 0:15],
                op=ALU.mult),
                reads=[("pm4", j, "a"), ("sel4",)], writes=[("pm4", j)])
            mm(pb[5][0:16, j * 128:(j + 1) * 128], seg(pm4, j), ident[:], True, True,
               reads=[("pm4", j), ("pm4z",), ("ident",)],
               war=[("pb5",)] if j == 0 else (),
               writes=[("pb5",)] if j == 3 else (),
               inc=(j == 3))
        ctx.comp("act", lambda s: s.copy(pmT[:], pb[5][0:16, :]),
                 reads=[("pb5",)], writes=[("pmT",)])
        mm(pb[3], m1[:], pmT[:], True, True,
           reads=[("m1",), ("pmT",)], war=[("pb3",)], writes=[("pb3",)], inc=True)
        ctx.dma("pool", lambda e: e.dma_start(
            out=wbuf[2][:].rearrange("p (a m) -> p a m", a=DT),
            in_=atT.rearrange("(a p) o -> p a o", p=128)),
            writes=[("wbuf", 2)], fam="w2")
        for dt in range(DT):
            mm(pb[4],
               wbuf[2][:, dt * 128:(dt + 1) * 128],
               xn3T[ch][:, dt * 512:(dt + 1) * 512],
               dt == 0, dt == DT - 1,
               reads=[("wbuf", 2), xk],
               war=[("pb4",)] if dt == 0 else (),
               writes=[("pb4",)] if dt == DT - 1 else (),
               inc=(dt == DT - 1))
        ctx.comp("act", lambda s: s.mul(scratch[:, 0:512], pb[4], LORA_SCALING),
                 reads=[("pb4",)], writes=[("scratch",)])
        ctx.comp("dve", lambda v: v.tensor_tensor(
            hw_bf[:], scratch[:, 0:512], pb[3], op=ALU.mult),
            reads=[("scratch",), ("pb3",)], writes=[("hw",)])

    def base_chunk(ch):
        xk = ("xn3T",)

        def wb_prefetch(ot, par):
            ctx.dma("pool", lambda e, ot=ot, par=par: e.dma_start(
                out=wbuf[par][:].rearrange("p (a m) -> p a m", a=DT),
                in_=wb4[ot]),
                writes=[("wbuf", par)], fam=f"w{par}")

        for uu in range(3):
            wb_prefetch(uu, uu)
        pending = []
        for ot in range(DT):
            par = ot % 3
            bank = ot % 2
            bk = (f"pb{bank}",)
            ap2 = ot % 2
            for dt in range(DT):
                mm(pb[bank],
                   wbuf[par][:, dt * 128:(dt + 1) * 128],
                   xn3T[ch][:, dt * 512:(dt + 1) * 512],
                   dt == 0, False,
                   reads=[("wbuf", par), xk],
                   war=[bk] if dt == 0 else (),
                   inc=False)
            mm(pb[bank], bt[:, ot * 128:(ot + 1) * 128], hw_bf[:],
               False, True,
               reads=[("bt",), ("hw",)],
               writes=[bk], inc=True)
            if ot + 3 < DT:
                wb_prefetch(ot + 3, par)
            ctx.comp("act", lambda s, b=bank, ot=ot, ap2=ap2: s.activation(
                attnT[ap2][:], pb[b], AF.Identity, bias=bbs[:, ot:ot + 1]),
                reads=[bk, ("bbs",)], writes=[("attnT", ap2)])
            for fl in pending:
                fl()
            pending = [lambda ot=ot, ap2=ap2: transpose_batch(
                attnT[ap2], ("attnT", ap2), 1,
                lambda g, ot=ot: ftok[:].rearrange(
                    "p (a m) -> p a m", a=4)[:, 0:4, ot * 128:(ot + 1) * 128],
                ("ftok",), evac_eng="dve")]
        for fl in pending:
            fl()

    def ln4_chunk(ch):
        for j in range(4):
            tglob = ch * 4 + j
            tile = ftok[:, j * D:(j + 1) * D]
            ln_stats(tile, [("ftok",)])
            ctx.comp("dve", lambda v, tile=tile, tglob=tglob: v.scalar_tensor_tensor(
                outb[:], tile, stats[:, 6:7], x1[:, tglob * D:(tglob + 1) * D],
                op0=ALU.mult, op1=ALU.add),
                reads=[("ftok",), ("st", 6), ("x1", tglob)], writes=[("outb",)])
            ctx.comp("dve", lambda v: v.tensor_scalar(
                outb[:], outb[:], stats[:, 7:8], None, op0=ALU.subtract),
                reads=[("st", 7), ("outb",)], writes=[("outb",)])
            ctx.dma("sp", lambda e, tglob=tglob: e.dma_start(
                out=out[tglob * 128:(tglob + 1) * 128, :], in_=outb[:]),
                reads=[("outb",)], writes=[("out", tglob)], fam="outq")

    if STOP_PHASE >= 7:
        ln3_chunk(0)
        router_chunk(0)
        base_chunk(0)
        ln3_chunk(1)
        router_chunk(1)
        ln4_chunk(0)
        base_chunk(1)
        ln4_chunk(1)

    ctx.q["sp"].append(lambda eng, v=ctx.cnt["outq"], s=ctx.sems["outq"]: eng.wait_ge(s, v))
    ctx.trace["sp"].append(("w", "outq", ctx.cnt["outq"]))
    ctx.check()

    with nc.Block() as block:
        @block.tensor
        def _(t):
            for f in ctx.q["pe"]:
                f(t)

        @block.scalar
        def _(s):
            for f in ctx.q["act"]:
                f(s)

        @block.vector
        def _(v):
            for f in ctx.q["dve"]:
                f(v)

        @block.sync
        def _(s):
            for f in ctx.q["sp"]:
                f(s)

        @block.gpsimd
        def _(g):
            for f in ctx.q["pool"]:
                f(g)

    for g in list(live.values())[::-1]:
        g.__exit__(None, None, None)
    for g in sem_guards[::-1]:
        g.__exit__(None, None, None)
    return nc



_NC_CACHE = None


def kernel(L0_attn_x, L0_attn_Wq, L0_attn_Wk, L0_attn_Wv, L0_attn_Wo,
           L0_ffn_base_W, L0_ffn_base_b, L0_ffn_router_W, L0_ffn_router_b,
           L0_ffn_A, L0_ffn_B, L0_ffn_expert_map):
    global _NC_CACHE
    x = np.asarray(L0_attn_x, dtype=np.float32)
    wq = np.asarray(L0_attn_Wq, dtype=np.float32)
    wk = np.asarray(L0_attn_Wk, dtype=np.float32)
    wv = np.asarray(L0_attn_Wv, dtype=np.float32)
    wo = np.asarray(L0_attn_Wo, dtype=np.float32)
    wb = np.asarray(L0_ffn_base_W, dtype=np.float32)
    bb = np.asarray(L0_ffn_base_b, dtype=np.float32)
    wr = np.asarray(L0_ffn_router_W, dtype=np.float32)
    rb = np.asarray(L0_ffn_router_b, dtype=np.float32)
    A = np.asarray(L0_ffn_A, dtype=np.float32)
    Bm = np.asarray(L0_ffn_B, dtype=np.float32)
    em = np.asarray(L0_ffn_expert_map, dtype=np.int32)

    cos_tab, sin_tab = _rope_tables()

    def tile4(wT, nu):
        return np.ascontiguousarray(
            wT.reshape(DT, 128, nu, 128).transpose(2, 1, 0, 3))

    shared = {
        "wq4": tile4(wq.T, H),
        "wk4": tile4(wk.T, KVH),
        "wv4": tile4(wv.T, KVH),
        "wo4": tile4(wo.T, DT),
        "wb4": tile4(wb.T, DT),
        "wrT": np.ascontiguousarray(
            np.concatenate([wr.T, np.zeros((D, 1), np.float32)], axis=1)),
        "atT": np.ascontiguousarray(A.reshape(E * R, D).T),
        "btL": np.ascontiguousarray(Bm.transpose(0, 2, 1).reshape(E * R, D)),
        "bbT": np.ascontiguousarray(bb.reshape(DT, 128).T),
        "rb_row": np.ascontiguousarray(rb.reshape(1, 15)),
        "em_row": np.ascontiguousarray(em.reshape(1, 15)),
        "ident_in": np.eye(128, dtype=BF),
        "ones128_in": np.ones((128, 128), dtype=BF),
        "ones1_in": np.ones((1, 128), dtype=np.float32),
        "ar8_in": np.arange(8, dtype=np.float32).reshape(8, 1),
        "expand_in": np.kron(np.eye(E), np.ones((1, R))).astype(BF),
        "ar15_in": np.tile(
            np.concatenate([np.arange(15, dtype=np.float32), [99.0]]),
            (128, 4)).astype(np.float32),
    }
    tri = (np.arange(512)[:, None] <= np.arange(512)[None, :]).astype(BF)
    shared["tri_in"] = np.ascontiguousarray(
        tri.reshape(4, 128, 512).transpose(1, 0, 2).reshape(128, 4 * 512))

    in_maps = []
    for core in range(8):
        b, s = core // 2, core % 2
        perm = _core_perm(s)
        m = dict(shared)
        m["xkv"] = np.ascontiguousarray(x[b][perm])
        m["coskT"] = np.ascontiguousarray(cos_tab[perm].T)
        m["sinkT"] = np.ascontiguousarray(sin_tab[perm].T)
        bv = np.full((128, 32), MASKV, dtype=np.float32)
        for ch in range(2):
            nkt = 8 if ch == 0 else 16
            diag0 = 4 if ch == 0 else 8
            qpos = perm[512 + ch * 512: 512 + (ch + 1) * 512]
            for kt in range(nkt):
                col = ch * 16 + kt
                if diag0 <= kt < diag0 + 4:
                    bv[:, col] = NEGC
                    continue
                kpos = perm[kt * 128:(kt + 1) * 128]
                vis = bool(np.all(kpos[:, None] <= qpos[None, :]))
                novis = bool(np.all(kpos[:, None] > qpos[None, :]))
                assert vis or novis, (core, ch, kt)
                bv[:, col] = NEGC if vis else MASKV
        m["biasv_in"] = bv
        in_maps.append(m)

    if _NC_CACHE is None:
        _NC_CACHE = build_program()
    res = run_bass_kernel_spmd(_NC_CACHE, in_maps, core_ids=list(range(8)))

    outp = np.empty((B, T, D), dtype=np.float32)
    for core in range(8):
        b, s = core // 2, core % 2
        o = res.results[core]["out"]
        if s == 0:
            outp[b, 0:512] = o[0:512]
            outp[b, 1536:2048] = o[512:1024]
        else:
            outp[b, 512:1536] = o
    return outp
